# version 15
# speedup vs baseline: 1.6064x; 1.0328x over previous
import numpy as np
from contextlib import ExitStack

import concourse.bass as bass
import concourse.mybir as mybir
from concourse.bass_utils import run_bass_kernel_spmd

F32 = mybir.dt.float32
F32R = mybir.dt.float32r
BF16 = mybir.dt.bfloat16
AF = mybir.ActivationFunctionType
ALU = mybir.AluOpType
AX = mybir.AxisListType

NCORES = 8
TOK = 1024
D = 2048
DFF = 8192
NE = 8
H = 32
HL = 8
LSEQ = 4096
KD = 16
FT = 64
TS = 2
DT = 16
DS = 4
TSUB = 8
NCH = 16
LORA_S = 2.0

DEBUG = False


class Prog:
    ENGINES = ("sync", "tensor", "vector", "scalar", "gpsimd")

    def __init__(self):
        self.items = {e: [] for e in self.ENGINES}
        self.cnt = {}

    def sem(self, name):
        self.cnt[name] = 0

    def inc(self, name, n):
        self.cnt[name] += n
        return self.cnt[name]

    def emit(self, eng, fn, inc=None, inc_n=1):
        if inc is None:
            self.items[eng].append(fn)
            return None
        val = self.inc(inc, inc_n)

        def wrapped(e, s, fn=fn, inc=inc, inc_n=inc_n):
            fn(e, s).then_inc(s[inc], inc_n)
        self.items[eng].append(wrapped)
        return val

    def wait(self, eng, name, val):
        assert val <= self.cnt[name], f"wait {name}>={val} but count={self.cnt[name]}"
        if val <= 0:
            return
        self.items[eng].append(lambda e, s, name=name, val=val: e.wait_ge(s[name], val))


def _prep(inputs):
    import ml_dtypes
    C = np.ascontiguousarray
    x = C(np.asarray(inputs["L0_attn_x"], np.float32))
    Wx = C(np.asarray(inputs["L0_attn_Wx"], np.float32))
    AWg = C(np.asarray(inputs["L0_attn_Wg"], np.float32))
    bg = np.asarray(inputs["L0_attn_bg"], np.float32)
    Wo = C(np.asarray(inputs["L0_attn_Wo"], np.float32))
    A = np.asarray(inputs["L0_attn_A"], np.float32)
    Dp = np.asarray(inputs["L0_attn_Dp"], np.float32)
    rw = np.asarray(inputs["L0_ffn_router_w"], np.float32)
    fWg = np.asarray(inputs["L0_ffn_Wg"], np.float32)
    fWu = np.asarray(inputs["L0_ffn_Wu"], np.float32)
    fWd = np.asarray(inputs["L0_ffn_Wd"], np.float32)
    Ag = np.asarray(inputs["L0_ffn_Ag"], np.float32)
    Bg = np.asarray(inputs["L0_ffn_Bg"], np.float32)
    Au = np.asarray(inputs["L0_ffn_Au"], np.float32)
    Bu = np.asarray(inputs["L0_ffn_Bu"], np.float32)
    Ad = np.asarray(inputs["L0_ffn_Ad"], np.float32)
    Bd = np.asarray(inputs["L0_ffn_Bd"], np.float32)

    def _ktile(W, ncols):
        kt = W.reshape(KD, 128, ncols // 128, 128)
        return C(np.transpose(kt, (2, 1, 0, 3)).reshape(ncols // 128, 128, KD * 128))

    def _p5tile(WdT_):
        t = WdT_.reshape(NCH, 4, 128, DS, 512)
        return C(np.transpose(t, (0, 3, 2, 1, 4)).reshape(NCH, DS, 128, 4 * 512))

    def _wotile(Wo_):
        t = Wo_.reshape(4, 4, 128, DS, 512)
        return C(np.transpose(t, (0, 3, 2, 1, 4)).reshape(4, DS, 128, 4 * 512))

    def _adtpad(Admat):
        out = np.zeros((NE, DFF, 128), np.float32)
        for e in range(NE):
            out[e, :, 16 * e:16 * e + 16] = Admat[e].T
        return C(out)

    def _adtile(A3):
        t = A3.reshape(NE, FT, 128, 128)
        return C(np.transpose(t, (1, 2, 0, 3)).reshape(FT, 128, NE * 128))

    def pairs(Bmat):
        BT = np.transpose(Bmat, (0, 2, 1))
        return C(np.concatenate(
            [np.concatenate([BT[e], BT[(e - 1) % NE]], 0) for e in range(NE)], 0))

    bdq = np.zeros((128, D), np.float32)
    for e in range(NE):
        bdq[16 * e:16 * e + 16, :] = Bd[e].T

    shared = dict(
        Wx=Wx,
        rwT=C(rw.T),
        WgT=_ktile(fWg.T, DFF), WuT=_ktile(fWu.T, DFF),
        WdT=_p5tile(fWd.T), AWg=_ktile(AWg, D), Wo=_wotile(Wo),
        AgT=C(np.transpose(Ag, (2, 0, 1)).reshape(D, NE * 16)),
        AuT=C(np.transpose(Au, (2, 0, 1)).reshape(D, NE * 16)),
        AdT3=_adtile(_adtpad(Ad)),
        BgTp=pairs(Bg), BuTp=pairs(Bu),
        bdTp=C(bdq),
        E_mat=C(np.repeat(np.eye(H, dtype=np.float32), D // H, axis=1)),
        Dp_cols=C(np.repeat(Dp, D // H).reshape(DT, 128).T),
        bg_cols=C(bg.reshape(DT, 128).T),
        id128=np.eye(128, dtype=np.float32),
        id128b=np.eye(128, dtype=ml_dtypes.bfloat16),
        ones_col=np.ones((128, 1), np.float32),
        ones_b=np.ones((128, 1), ml_dtypes.bfloat16),
        misc_c=np.broadcast_to(np.array([1e-5, -0.5, LORA_S, -LORA_S, -1.0, 1.0 / D], np.float32), (128, 6)).copy(),
        zeros128=np.zeros((128, TOK), np.float32),
    )
    AT = C(A.T)
    per_core = []
    for c in range(NCORES):
        b, tp = divmod(c, 4)
        m = dict(shared)
        m["x_sh"] = C(x[b, tp * TOK:(tp + 1) * TOK, :])
        m["AT_loc"] = C(AT[:, HL * tp:HL * (tp + 1)])
        msk = np.zeros((128, 4), np.float32)
        msk[:, tp] = 1.0
        m["msk"] = msk
        mct = np.zeros((H, 128), np.float32)
        for k in range(H):
            r, il = divmod(k, HL)
            mct[k, 32 * r + HL * tp + il] = 1.0
        m["mctT"] = C(mct.T).astype(ml_dtypes.bfloat16)
        per_core.append(m)
    return per_core


def build_program():
    nc = bass.Bass("TRN2", target_bir_lowering=False)
    P = Prog()
    for s in ("sq", "gq", "cc", "pe", "vv", "ss", "st",
              "sx0", "sx1", "g1", "g2", "swx", "srw", "sag", "s2", "sB",
              "sd0", "sd1", "s3", "sa0", "sa1", "so", "s4", "sp0", "sp1",
              "sh0", "sh1", "su0", "su1", "sv0", "sv1", "sm0", "sm1",
              "sy0", "sy1", "sz0", "sz1", "sbd", "st0", "st1",
              "up0", "up1", "ad0", "ad1"):
        P.sem(s)

    din = {}
    for name, shape in [
        ("x_sh", [TOK, D]), ("AT_loc", [128, HL]), ("Wx", [D, 288]),
        ("rwT", [D, NE]), ("WgT", [DFF // 128, 128, KD * 128]), ("WuT", [DFF // 128, 128, KD * 128]), ("WdT", [NCH, DS, 128, 4 * 512]),
        ("AWg", [DT, 128, KD * 128]), ("Wo", [4, DS, 128, 4 * 512]), ("AgT", [D, 128]), ("AuT", [D, 128]),
        ("AdT3", [FT, 128, NE * 128]), ("E_mat", [H, D]), ("Dp_cols", [128, DT]),
        ("bg_cols", [128, DT]), ("id128", [128, 128]), ("ones_col", [128, 1]),
        ("misc_c", [128, 6]), ("msk", [128, 4]),
        ("zeros128", [128, TOK]), ("BgTp", [256, DFF]), ("BuTp", [256, DFF]),
        ("bdTp", [128, D]),
    ]:
        din[name] = nc.dram_tensor(name, shape, F32, kind="ExternalInput").ap()
    for name, shape in [("id128b", [128, 128]), ("ones_b", [128, 1]), ("mctT", [128, H])]:
        din[name] = nc.dram_tensor(name, shape, BF16, kind="ExternalInput").ap()
    dout = nc.dram_tensor("out_sh", [TOK, D], F32, kind="ExternalOutput").ap()

    dbg = {}
    if DEBUG:
        for name, shape in [
            ("dbg_xn", [D, TOK]), ("dbg_w", [TOK, NE]), ("dbg_y", [H, TOK]),
            ("dbg_yfull", [D, TOK]), ("dbg_z", [128, TOK]),
        ]:
            dbg[name] = nc.dram_tensor(name, shape, F32, kind="ExternalOutput").ap()
        dbg["dbg_hbar"] = nc.dram_tensor("dbg_hbar", [DFF, TOK], BF16, kind="ExternalOutput").ap()
        dbg["dbg_cols"] = nc.dram_tensor("dbg_cols", [128, 64], F32, kind="ExternalOutput").ap()
        for nm in ("dbg_B", "dbg_db", "dbg_av", "dbg_b2", "dbg_hs"):
            dbg[nm] = nc.dram_tensor(nm, [128, TOK], F32, kind="ExternalOutput").ap()
        dbg["dbg_gdb"] = nc.dram_tensor("dbg_gdb", [H, TOK], F32, kind="ExternalOutput").ap()
        dbg["dbg_gds"] = nc.dram_tensor("dbg_gds", [128, TOK], F32, kind="ExternalOutput").ap()
        dbg["dbg_arg"] = nc.dram_tensor("dbg_arg", [128, TOK], F32, kind="ExternalOutput").ap()
        dbg["dbg_arg0"] = nc.dram_tensor("dbg_arg0", [128, TOK], F32, kind="ExternalOutput").ap()
        dbg["dbg_dbloc"] = nc.dram_tensor("dbg_dbloc", [H, TOK], BF16, kind="ExternalOutput").ap()

    cc_bc_in = nc.dram_tensor("cc_bc_in", [256, TOK], BF16)
    g_bc = nc.dram_tensor("g_bc", [4 * 256, TOK], BF16)
    cc_db_in = nc.dram_tensor("cc_db_in", [H, TOK], BF16)
    g_dbg = nc.dram_tensor("g_dbg", [4 * H, TOK], BF16)
    d_dbloc = nc.dram_tensor("d_dbloc", [H, TOK], BF16)
    cc_y_in = nc.dram_tensor("cc_y_in", [HL, LSEQ], F32)
    g_yg = nc.dram_tensor("g_yg", [4 * HL, LSEQ], F32)
    d_yfull = nc.dram_tensor("d_yfull", [D, TOK], F32)
    d_hbar = nc.dram_tensor("d_hbar", [DFF, TOK], BF16)
    d_w8 = nc.dram_tensor("d_w8", [NE, TOK], BF16)

    sbuf = nc.alloc_sbuf_tensor
    xn = sbuf("xn", [128, KD * TOK], F32R)
    wbc = sbuf("wbc", [128, NE * TOK], BF16)
    xapg = [sbuf(f"xapg{q}", [128, TOK], F32R) for q in range(2)]
    xapu = [sbuf(f"xapu{q}", [128, TOK], F32R) for q in range(2)]
    z_sb = sbuf("z_sb", [128, TOK], F32R)
    AT_s = sbuf("AT_s", [128, HL], F32)
    Dp_s = sbuf("Dp_s", [128, DT], F32)
    bgc_s = sbuf("bgc_s", [128, DT], F32)
    id_s = sbuf("id_s", [128, 128], F32)
    idr_s = sbuf("idr_s", [128, 128], F32R)
    idb_s = sbuf("idb_s", [128, 128], BF16)
    ones_s = sbuf("ones_s", [128, 1], F32R)
    onesb_s = sbuf("onesb_s", [128, 1], BF16)
    misc_s = sbuf("misc_s", [128, 6], F32)
    msk_s = sbuf("msk_s", [128, 4], F32)
    mct_s = sbuf("mct_s", [128, H], BF16)
    cols = sbuf("cols", [128, 64], F32)
    st8 = sbuf("st8", [128, HL], F32)
    AR = sbuf("AR", [128, 12288], F32)
    ARF = sbuf("ARF", [128, 9216], F32R)

    def ar(off, n, dt_=F32):
        v = AR[:, off:off + n]
        return v if dt_ is F32 else v.bitcast(dt_)

    def arf(off, n):
        return ARF[:, off:off + n]

    stack = ExitStack()
    psA0 = stack.enter_context(nc.psum_tensor("psA0", [128, 512], F32))
    psA1 = stack.enter_context(nc.psum_tensor("psA1", [128, 512], F32))
    psB0 = stack.enter_context(nc.psum_tensor("psB0", [128, 512], F32))
    psB1 = stack.enter_context(nc.psum_tensor("psB1", [128, 512], F32))
    psZ0 = stack.enter_context(nc.psum_tensor("psZ0", [128, 1024], F32))
    psZ1 = stack.enter_context(nc.psum_tensor("psZ1", [128, 1024], F32))

    for name, dst, cast in [
        ("AT_loc", AT_s, None), ("Dp_cols", Dp_s, None), ("bg_cols", bgc_s, None),
        ("id128", id_s, None), ("id128", idr_s, F32R), ("id128b", idb_s, None),
        ("ones_col", ones_s, F32R), ("ones_b", onesb_s, None),
        ("misc_c", misc_s, None), ("msk", msk_s, None), ("mctT", mct_s, None),
    ]:
        src = din[name] if cast is None else din[name].bitcast(cast)
        P.emit("gpsimd", lambda g, s, src=src, dst=dst: g.dma_start(out=dst[:], in_=src),
               inc="g1", inc_n=16)
    consts_gq = P.cnt["g1"]

    wx_s = arf(4096, KD * 288)
    v_wx = P.emit("scalar", lambda e, s: e.dma_start(
        out=wx_s.rearrange("p (k c) -> p k c", k=KD),
        in_=din["Wx"].bitcast(F32R).rearrange("(k p) c -> p k c", p=128)), inc="swx", inc_n=16)
    rw_s = arf(8704, KD * NE)
    v_rw = P.emit("scalar", lambda e, s: e.dma_start(
        out=rw_s.rearrange("p (k c) -> p k c", k=KD),
        in_=din["rwT"].bitcast(F32R).rearrange("(k p) c -> p k c", p=128)), inc="srw", inc_n=16)
    ag_s = arf(0, KD * 128)
    au_s = arf(2048, KD * 128)
    v_ag = P.emit("scalar", lambda e, s: e.dma_start(
        out=ag_s.rearrange("p (k c) -> p k c", k=KD),
        in_=din["AgT"].bitcast(F32R).rearrange("(k p) c -> p k c", p=128)), inc="sag", inc_n=16)
    v_au = P.emit("scalar", lambda e, s: e.dma_start(
        out=au_s.rearrange("p (k c) -> p k c", k=KD),
        in_=din["AuT"].bitcast(F32R).rearrange("(k p) c -> p k c", p=128)), inc="sag", inc_n=16)

    x_t = din["x_sh"].rearrange("(n p) d -> n p d", p=128)
    red_vv = [0, 0]
    sqr_ss = [0, 0]
    for tt in range(TSUB):
        xt = ar(2048 * (tt % 2), 2048)
        sq_ = ar(6144 + 2048 * (tt % 2), 2048)
        sxp = "sx0" if tt % 2 == 0 else "sx1"
        qe = "sync" if tt % 2 == 0 else "gpsimd"
        if tt >= 2:
            P.wait(qe, "vv", red_vv[tt % 2])
        v_x = P.emit(qe, lambda e, s, tt=tt, xt=xt: e.dma_start(out=xt, in_=x_t[tt]),
                     inc=sxp, inc_n=16)
        P.wait("scalar", sxp, v_x)
        if tt >= 2:
            P.wait("scalar", "vv", red_vv[tt % 2])
        sqr_ss[tt % 2] = P.emit("scalar", lambda e, s, xt=xt, sq_=sq_: e.activation(sq_, xt, AF.Square), inc="ss")
        P.wait("vector", sxp, v_x)
        P.emit("vector", lambda e, s, xt=xt, tt=tt: e.reduce_sum(cols[:, tt:tt + 1], xt, AX.X))
        P.wait("vector", "ss", sqr_ss[tt % 2])
        red_vv[tt % 2] = P.emit("vector", lambda e, s, sq_=sq_, tt=tt: e.reduce_sum(
            cols[:, 8 + tt:9 + tt], sq_, AX.X), inc="vv")
    stats_vv = P.cnt["vv"]
    P.wait("scalar", "vv", stats_vv)
    P.wait("scalar", "g1", consts_gq)
    for tt in range(TSUB):
        P.emit("scalar", lambda e, s, tt=tt: e.activation(
            cols[:, 16 + tt:17 + tt], cols[:, tt:tt + 1], AF.Copy, scale=misc_s[:, 5:6]))
    sm_ss = P.emit("scalar", lambda e, s: e.activation(
        cols[:, 24:25], cols[:, 8:9], AF.Copy, scale=misc_s[:, 5:6]), inc="ss")
    for tt in range(1, TSUB):
        sm_ss = P.emit("scalar", lambda e, s, tt=tt: e.activation(
            cols[:, 24 + tt:25 + tt], cols[:, 8 + tt:9 + tt], AF.Copy, scale=misc_s[:, 5:6]), inc="ss")
    P.wait("vector", "ss", sm_ss)
    q_vv = 0
    for tt in range(TSUB):
        q_vv = P.emit("vector", lambda e, s, tt=tt: e.scalar_tensor_tensor(
            cols[:, 32 + tt:33 + tt], cols[:, 16 + tt:17 + tt], cols[:, 16 + tt:17 + tt],
            cols[:, 24 + tt:25 + tt], ALU.mult, ALU.subtract), inc="vv")
    P.wait("scalar", "vv", q_vv)
    for tt in range(TSUB):
        P.emit("scalar", lambda e, s, tt=tt: e.activation(
            cols[:, 40 + tt:41 + tt], cols[:, 32 + tt:33 + tt], AF.Ln,
            scale=misc_s[:, 4:5], bias=misc_s[:, 0:1]))
    rstd_ss = 0
    for tt in range(TSUB):
        rstd_ss = P.emit("scalar", lambda e, s, tt=tt: e.activation(
            cols[:, 48 + tt:49 + tt], cols[:, 40 + tt:41 + tt], AF.Exp, scale=misc_s[:, 1:2]), inc="ss")
    xt_last_vv = [0, 0]
    xnt_pe = [0, 0]
    tp_banks = [psA0, psA1, psB0, psB1]
    tp_ev_bank = [0, 0, 0, 0]
    for tt in range(TSUB):
        xt = ar(2048 * (tt % 2), 2048)
        xnt = ar(4096 + 2048 * (tt % 2), 2048)
        sxp = "sx0" if tt % 2 == 0 else "sx1"
        qe = "sync" if tt % 2 == 0 else "gpsimd"
        if tt < 2:
            P.wait(qe, "vv", stats_vv)
        else:
            P.wait(qe, "vv", xt_last_vv[tt % 2])
        v_x = P.emit(qe, lambda e, s, tt=tt, xt=xt: e.dma_start(out=xt, in_=x_t[tt]),
                     inc=sxp, inc_n=16)
        P.wait("vector", sxp, v_x)
        if tt == 0:
            P.wait("vector", "ss", rstd_ss)
        P.wait("vector", "pe", xnt_pe[tt % 2])
        vv2 = P.emit("vector", lambda e, s, xt=xt, xnt=xnt, tt=tt: e.tensor_scalar(
            xnt, xt, cols[:, 16 + tt:17 + tt], cols[:, 48 + tt:49 + tt],
            ALU.subtract, ALU.mult), inc="vv")
        xt_last_vv[tt % 2] = vv2
        if tt == 0:
            P.wait("tensor", "g1", consts_gq)
        P.wait("tensor", "vv", vv2)
        for j in range(DT):
            pst = tp_banks[j % 4][:, 0:128]
            P.wait("tensor", "ss", tp_ev_bank[j % 4])
            pv = P.emit("tensor", lambda e, s, xnt=xnt, j=j, pst=pst: e.transpose(
                pst, xnt[:, 128 * j:128 * (j + 1)], id_s[:]), inc="pe")
            P.wait("scalar", "pe", pv)
            dst = xn[:, j * TOK + tt * 128: j * TOK + tt * 128 + 128]
            tp_ev_bank[j % 4] = P.emit("scalar", lambda e, s, dst=dst, pst=pst: e.activation(dst, pst, AF.Copy), inc="ss")
        xnt_pe[tt % 2] = P.cnt["pe"]
    xn_done_ss = P.cnt["ss"]
    xn_done_vv = P.cnt["vv"]

    if DEBUG:
        P.wait("sync", "ss", xn_done_ss)
        P.emit("sync", lambda e, s: e.dma_start(out=dbg["dbg_cols"], in_=cols[:]), inc="sq", inc_n=16)
        for j in range(DT):
            P.emit("sync", lambda e, s, j=j: e.dma_start(
                out=dbg["dbg_xn"][128 * j:128 * (j + 1), :],
                in_=xn[:, j * TOK:(j + 1) * TOK].bitcast(F32)), inc="sq", inc_n=16)

    P.wait("tensor", "swx", v_wx)
    P.wait("tensor", "ss", xn_done_ss)
    st2_gq = {}
    for tt in range(TSUB):
        psm = psA0[:, 0:288]
        for k in range(KD):
            lh = xn[:, k * TOK + tt * 128: k * TOK + tt * 128 + 128]
            rh = wx_s[:, k * 288:(k + 1) * 288]
            P.emit("tensor", lambda e, s, psm=psm, lh=lh, rh=rh, k=k: e.matmul(
                psm, lh, rh, start=(k == 0), stop=(k == KD - 1)),
                inc="pe" if k == KD - 1 else None)
        pv = P.cnt["pe"]
        stage = ar(4608 + (tt % 2) * 160, 144, BF16)
        P.wait("scalar", "pe", pv)
        if tt == 0:
            P.wait("scalar", "vv", xn_done_vv)
        sv = P.emit("scalar", lambda e, s, stage=stage, psm=psm: e.activation(stage, psm, AF.Copy), inc="ss")
        P.wait("tensor", "ss", sv)
        for ci in range(3):
            w = 128 if ci < 2 else 32
            pst = (psB0 if ci % 2 == 0 else psB1)[:, 0:128].bitcast(BF16)[:, 0:128]
            src = stage[:, 128 * ci:128 * ci + w]
            pv2 = P.emit("tensor", lambda e, s, pst=pst, src=src, w=w: e.transpose(
                pst[0:w, :] if w < 128 else pst, src, idb_s[:]), inc="pe")
            st2 = ar(5000 + tt * 192 + ci * 64, 64, BF16)
            P.wait("vector", "pe", pv2)
            ev = P.emit("vector", lambda e, s, st2=st2, pst=pst, w=w: e.tensor_copy(
                st2[0:w, :], pst[0:w, :]), inc="vv")
            P.wait("tensor", "vv", ev)
            P.wait("gpsimd", "vv", ev)
            if ci < 2:
                P.emit("gpsimd", lambda e, s, st2=st2, ci=ci, tt=tt: e.dma_start(
                    out=cc_bc_in.ap()[128 * ci:128 * (ci + 1), 128 * tt:128 * (tt + 1)],
                    in_=st2[0:128, :]), inc="gq", inc_n=16)
            else:
                P.emit("gpsimd", lambda e, s, st2=st2, tt=tt: e.dma_start(
                    out=cc_db_in.ap()[:, 128 * tt:128 * (tt + 1)],
                    in_=st2[0:32, :]), inc="gq", inc_n=16)
    ssm_gq = P.cnt["gq"]

    groups = [[0, 1, 2, 3], [4, 5, 6, 7]]
    P.wait("gpsimd", "gq", ssm_gq)
    P.emit("gpsimd", lambda e, s: e.collective_compute(
        "AllGather", ALU.bypass, replica_groups=groups,
        ins=[cc_bc_in.ap().opt()], outs=[g_bc.ap().opt()]), inc="cc")
    P.emit("gpsimd", lambda e, s: e.collective_compute(
        "AllGather", ALU.bypass, replica_groups=groups,
        ins=[cc_db_in.ap().opt()], outs=[g_dbg.ap().opt()]), inc="cc")
    cc_ssm_done = P.cnt["cc"]

    P.wait("tensor", "srw", v_rw)
    rt = ar(6600, 64)
    re_ = ar(6664, 64)
    rm = ar(6728, 64)
    wt_all = ar(6792, 64)
    stg = ar(6856, 64, BF16)
    lg_ss = 0
    for tt in range(TSUB):
        psm = psA1[:, 0:NE]
        P.wait("tensor", "ss", lg_ss)
        for k in range(KD):
            lh = xn[:, k * TOK + tt * 128: k * TOK + tt * 128 + 128]
            rh = rw_s[:, k * NE:(k + 1) * NE]
            P.emit("tensor", lambda e, s, psm=psm, lh=lh, rh=rh, k=k: e.matmul(
                psm, lh, rh, start=(k == 0), stop=(k == KD - 1)),
                inc="pe" if k == KD - 1 else None)
        pv = P.cnt["pe"]
        P.wait("scalar", "pe", pv)
        lg_ss = P.emit("scalar", lambda e, s, psm=psm, tt=tt: e.activation(
            rt[:, 8 * tt:8 * tt + 8], psm, AF.Copy), inc="ss")
    P.wait("vector", "ss", lg_ss)
    for tt in range(TSUB):
        P.emit("vector", lambda e, s, tt=tt: e.reduce_max(cols[:, tt:tt + 1], rt[:, 8 * tt:8 * tt + 8], AX.X))
    for tt in range(TSUB):
        P.emit("vector", lambda e, s, tt=tt: e.reduce_max(cols[:, 8 + tt:9 + tt], rt[:, 8 * tt:8 * tt + 8], AX.X, negate=True))
    for tt in range(TSUB):
        P.emit("vector", lambda e, s, tt=tt: e.tensor_scalar(
            re_[:, 8 * tt:8 * tt + 8], rt[:, 8 * tt:8 * tt + 8], cols[:, tt:tt + 1], None, ALU.is_equal))
    for tt in range(TSUB):
        P.emit("vector", lambda e, s, tt=tt: e.scalar_tensor_tensor(
            rm[:, 8 * tt:8 * tt + 8], re_[:, 8 * tt:8 * tt + 8], -1e30, rt[:, 8 * tt:8 * tt + 8],
            ALU.mult, ALU.add))
    m2_vv = 0
    for tt in range(TSUB):
        m2_vv = P.emit("vector", lambda e, s, tt=tt: e.reduce_max(
            cols[:, 16 + tt:17 + tt], rm[:, 8 * tt:8 * tt + 8], AX.X), inc="vv")
    P.wait("scalar", "vv", m2_vv)
    ew_ss = 0
    for tt in range(TSUB):
        ew_ss = P.emit("scalar", lambda e, s, tt=tt: e.activation(
            re_[:, 8 * tt:8 * tt + 8], rt[:, 8 * tt:8 * tt + 8], AF.Exp, bias=cols[:, 8 + tt:9 + tt]), inc="ss")
    P.wait("vector", "ss", ew_ss)
    for tt in range(TSUB):
        P.emit("vector", lambda e, s, tt=tt: e.tensor_scalar(
            rm[:, 8 * tt:8 * tt + 8], rt[:, 8 * tt:8 * tt + 8], cols[:, 16 + tt:17 + tt], None, ALU.is_ge))
    for tt in range(TSUB):
        P.emit("vector", lambda e, s, tt=tt: e.tensor_tensor(
            wt_all[:, 8 * tt:8 * tt + 8], rm[:, 8 * tt:8 * tt + 8], re_[:, 8 * tt:8 * tt + 8], ALU.mult))
    for tt in range(TSUB):
        P.emit("vector", lambda e, s, tt=tt: e.reduce_sum(
            cols[:, 24 + tt:25 + tt], wt_all[:, 8 * tt:8 * tt + 8], AX.X))
    for tt in range(TSUB):
        P.emit("vector", lambda e, s, tt=tt: e.reciprocal(cols[:, 32 + tt:33 + tt], cols[:, 24 + tt:25 + tt]))
    w_vv = 0
    for tt in range(TSUB):
        w_vv = P.emit("vector", lambda e, s, tt=tt: e.tensor_scalar_mul(
            wt_all[:, 8 * tt:8 * tt + 8], wt_all[:, 8 * tt:8 * tt + 8], cols[:, 32 + tt:33 + tt]), inc="vv")
    if DEBUG:
        P.wait("sync", "vv", w_vv)
        for tt in range(TSUB):
            P.emit("sync", lambda e, s, tt=tt: e.dma_start(
                out=dbg["dbg_w"][128 * tt:128 * (tt + 1), :], in_=wt_all[:, 8 * tt:8 * tt + 8]),
                inc="sq", inc_n=16)
    P.wait("tensor", "vv", w_vv)
    wtr_ev = 0
    w8dma = 0
    for tt in range(TSUB):
        pst = psB0[:, 0:128]
        P.wait("tensor", "vv", wtr_ev)
        pv2 = P.emit("tensor", lambda e, s, pst=pst, tt=tt: e.transpose(
            pst[0:8, :], wt_all[:, 8 * tt:8 * tt + 8], id_s[:]), inc="pe")
        P.wait("vector", "pe", pv2)
        P.wait("vector", "gq", P.cnt["gq"])
        wtr_ev = P.emit("vector", lambda e, s, pst=pst: e.tensor_copy(stg[0:8, 0:128], pst[0:8, :]), inc="vv")
        P.wait("gpsimd", "vv", wtr_ev)
        w8dma = P.emit("gpsimd", lambda e, s, tt=tt: e.dma_start(
            out=d_w8.ap()[:, 128 * tt:128 * (tt + 1)], in_=stg[0:8, 0:128]), inc="gq", inc_n=16)
    w8_gq = P.cnt["gq"]
    P.wait("sync", "gq", w8_gq)
    for e_ in range(NE):
        P.emit("sync", lambda e, s, e_=e_: e.dma_start(
            out=wbc[:, e_ * TOK:(e_ + 1) * TOK],
            in_=d_w8.ap()[e_:e_ + 1, :].partition_broadcast(128)), inc="s4", inc_n=16)
    wbc_sq = P.cnt["s4"]

    P.wait("tensor", "sag", v_au)
    for src, psm in ((ag_s, psZ0), (au_s, psZ1)):
        for ts in range(TS):
            for k in range(KD):
                lh = src[:, k * 128:(k + 1) * 128]
                rh = xn[:, k * TOK + ts * 512: k * TOK + ts * 512 + 512]
                P.emit("tensor", lambda e, s, psm=psm, lh=lh, rh=rh, k=k, ts=ts: e.matmul(
                    psm[:, ts * 512:(ts + 1) * 512], lh, rh,
                    start=(k == 0), stop=(k == KD - 1)),
                    inc="pe" if k == KD - 1 else None)
    xa_pe = P.cnt["pe"]
    xq_pg = ar(0, TOK, F32R)
    xq_ng = ar(1024, TOK, F32R)
    xq_pu = ar(2048, TOK, F32R)
    xq_nu = ar(3072, TOK, F32R)
    P.wait("scalar", "pe", xa_pe)
    P.emit("scalar", lambda e, s: e.activation(xq_pg, psZ0[:], AF.Copy, scale=misc_s[:, 2:3]), inc="ss")
    P.emit("scalar", lambda e, s: e.activation(xq_ng, psZ0[:], AF.Copy, scale=misc_s[:, 3:4]), inc="ss")
    P.emit("scalar", lambda e, s: e.activation(xq_pu, psZ1[:], AF.Copy, scale=misc_s[:, 2:3]), inc="ss")
    s4 = P.emit("scalar", lambda e, s: e.activation(xq_nu, psZ1[:], AF.Copy, scale=misc_s[:, 3:4]), inc="ss")
    P.wait("gpsimd", "ss", s4)
    for e_ in range(NE):
        q, r = divmod(e_, 4)
        for pos, neg, dsts in ((xq_pg, xq_ng, xapg), (xq_pu, xq_nu, xapu)):
            P.emit("gpsimd", lambda e, s, dsts=dsts, q=q, r=r, pos=pos, e_=e_: e.dma_start(
                out=dsts[q][32 * r:32 * r + 16, :], in_=pos[16 * e_:16 * e_ + 16, :]), inc="g2", inc_n=16)
            if e_ == 0:
                P.emit("gpsimd", lambda e, s, dsts=dsts, q=q, r=r: e.dma_start(
                    out=dsts[q][32 * r + 16:32 * r + 32, :],
                    in_=din["zeros128"].bitcast(F32R)[0:16, :]), inc="g2", inc_n=16)
            else:
                P.emit("gpsimd", lambda e, s, dsts=dsts, q=q, r=r, neg=neg, ep=e_ - 1: e.dma_start(
                    out=dsts[q][32 * r + 16:32 * r + 32, :], in_=neg[16 * ep:16 * ep + 16, :]), inc="g2", inc_n=16)
    xap_gq = P.cnt["g2"]

    bc_v = g_bc.ap().rearrange("(r c) t -> r c t", r=4)
    db_v = d_dbloc.ap().rearrange("(r h) t -> r h t", r=4)
    P.wait("sync", "g2", xap_gq)
    P.wait("sync", "cc", cc_ssm_done)
    gd_s = ar(8704, 512, BF16)
    v_gd = P.emit("sync", lambda e, s: e.dma_start(out=gd_s, in_=g_dbg.ap()), inc="s2", inc_n=16)
    P.wait("tensor", "s2", v_gd)
    P.wait("tensor", "ss", P.cnt["ss"])
    P.wait("tensor", "vv", P.cnt["vv"])
    if DEBUG:
        P.wait("vector", "sq", v_gd)
        gdump = P.emit("vector", lambda e, s: e.tensor_copy(ar(2048, 1024), gd_s), inc="vv")
        P.wait("sync", "vv", gdump)
        gd_dsq = P.emit("sync", lambda e, s: e.dma_start(out=dbg["dbg_gds"], in_=ar(2048, 1024)), inc="sq", inc_n=16)
        P.wait("vector", "sq", gd_dsq)
    dbsel_ss = 0
    for q4 in range(2):
        psd = psZ1[0:32, 512 * q4:512 * (q4 + 1)]
        pv = P.emit("tensor", lambda e, s, psd=psd, q4=q4: e.matmul(
            psd, mct_s[:], gd_s[:, 512 * q4:512 * (q4 + 1)], start=True, stop=True), inc="pe")
        P.wait("scalar", "pe", pv)
        dbsel_ss = P.emit("scalar", lambda e, s, psd=psd, q4=q4: e.activation(
            ar(9216, 512, BF16)[0:32, 512 * q4:512 * (q4 + 1)], psd, AF.Copy), inc="ss")
    P.wait("sync", "ss", dbsel_ss)
    v_dbl = P.emit("sync", lambda e, s: e.dma_start(
        out=d_dbloc.ap(), in_=ar(9216, 512, BF16)[0:32, :]), inc="s2", inc_n=16)
    P.wait("sync", "s2", v_dbl)
    if DEBUG:
        P.emit("sync", lambda e, s: e.dma_start(out=dbg["dbg_dbloc"], in_=d_dbloc.ap()), inc="sq", inc_n=16)
    y_gq = [0]
    db_last_vv = [0, 0]
    sv_prev_scan = [0]
    scanU_prev = [0]
    yred_prev = [0]
    ydma_prev = [0, 0]
    for qt in range(4):
        Bb = ar(0, 512, BF16)
        Cb = ar(512, 512, BF16)
        if qt >= 1:
            P.wait("sync", "vv", P.cnt["vv"])
        v_b = P.emit("sync", lambda e, s, Bb=Bb, qt=qt: e.dma_start(
            out=Bb, in_=bc_v[qt, 0:128, :]), inc="sB", inc_n=16)
        v_c = P.emit("sync", lambda e, s, Cb=Cb, qt=qt: e.dma_start(
            out=Cb, in_=bc_v[qt, 128:256, :]), inc="sB", inc_n=16)
        for il in range(HL):
            dbb = ar(1024 + (il % 2) * 512, 512, BF16)
            if il >= 2:
                P.wait("sync", "vv", db_last_vv[il % 2])
            sdp = "sd0" if il % 2 == 0 else "sd1"
            v_d = P.emit("sync", lambda e, s, dbb=dbb, qt=qt, il=il: e.dma_start(
                out=dbb.rearrange("p (a t) -> p a t", a=1),
                in_=db_v[qt, il:il + 1, :].partition_broadcast(128)),
                inc=sdp, inc_n=16)
            arg = ar(2048, 1024)
            tmp = ar(3072, 1024)
            av = ar(4096, 1024)
            b2 = ar(5120, 1024)
            hs = ar(6144, 1024)
            Ub = ar(7168, 512, BF16)
            P.wait("vector", sdp, v_d)
            P.wait("vector", "sB", v_c)
            P.wait("vector", "ss", sv_prev_scan[0])
            vv1 = P.emit("vector", lambda e, s, arg=arg, Bb=Bb, dbb=dbb: e.tensor_tensor(
                arg, Bb, dbb, ALU.add), inc="vv")
            db_last_vv[il % 2] = vv1
            P.wait("scalar", "vv", vv1)
            P.emit("scalar", lambda e, s, tmp=tmp, arg=arg: e.activation(tmp, arg, AF.Exp))
            P.emit("scalar", lambda e, s, arg=arg, tmp=tmp: e.activation(arg, tmp, AF.Ln, bias=1.0))
            P.wait("scalar", "vv", scanU_prev[0])
            P.emit("scalar", lambda e, s, av=av, arg=arg, il=il: e.activation(
                av, arg, AF.Exp, scale=AT_s[:, il:il + 1]))
            sv = P.emit("scalar", lambda e, s, b2=b2, arg=arg: e.activation(b2, arg, AF.Square), inc="ss")
            sv_prev_scan[0] = sv
            P.wait("vector", "ss", sv)
            if qt == 0:
                vv2 = P.emit("vector", lambda e, s, hs=hs, av=av, b2=b2: e.tensor_tensor_scan(
                    hs, av, b2, 0.0, ALU.mult, ALU.add), inc="vv")
            else:
                vv2 = P.emit("vector", lambda e, s, hs=hs, av=av, b2=b2, il=il: e.tensor_tensor_scan(
                    hs, av, b2, st8[:, il:il + 1], ALU.mult, ALU.add), inc="vv")
            P.wait("vector", "pe", yred_prev[0])
            vv3 = P.emit("vector", lambda e, s, Ub=Ub, Cb=Cb, hs=hs: e.tensor_tensor(
                Ub, Cb, hs, ALU.mult), inc="vv")
            scanU_prev[0] = vv3
            if qt < 3:
                P.emit("vector", lambda e, s, hs=hs, il=il: e.tensor_copy(
                    st8[:, il:il + 1], hs[:, TOK - 1:TOK]))
            if DEBUG and qt == 0 and il == 0:
                dvv0 = P.emit("vector", lambda e, s, arg=arg: e.tensor_copy(ar(11264, 1024), arg), inc="vv")
                P.wait("sync", "vv", dvv0)
                d0sq = P.emit("sync", lambda e, s: e.dma_start(out=dbg["dbg_arg0"], in_=ar(11264, 1024)), inc="sq", inc_n=16)
                P.wait("vector", "sq", d0sq)
            if DEBUG and qt == 0 and il == 1:
                P.wait("sync", "vv", vv3)
                for nm, buf in (("dbg_av", av), ("dbg_b2", b2), ("dbg_hs", hs), ("dbg_arg", arg)):
                    P.emit("sync", lambda e, s, nm=nm, buf=buf: e.dma_start(
                        out=dbg[nm], in_=buf), inc="sq", inc_n=16)
                dstg = ar(10240, 1024)
                dvv = P.emit("vector", lambda e, s, Bb=Bb, dstg=dstg: e.tensor_copy(dstg, Bb), inc="vv")
                P.wait("sync", "vv", dvv)
                dsq = P.emit("sync", lambda e, s, dstg=dstg: e.dma_start(out=dbg["dbg_B"], in_=dstg), inc="sq", inc_n=16)
                P.wait("vector", "sq", dsq)
                dvv = P.emit("vector", lambda e, s, dbb=dbb, dstg=dstg: e.tensor_copy(dstg, dbb), inc="vv")
                P.wait("sync", "vv", dvv)
                dsq = P.emit("sync", lambda e, s, dstg=dstg: e.dma_start(out=dbg["dbg_db"], in_=dstg), inc="sq", inc_n=16)
                P.wait("vector", "sq", dsq)
                dvv = P.emit("vector", lambda e, s, dstg=dstg: e.tensor_copy(dstg[0:32, :], ar(9216, 512, BF16)[0:32, :]), inc="vv")
                P.wait("sync", "vv", dvv)
                dsq = P.emit("sync", lambda e, s, dstg=dstg: e.dma_start(out=dbg["dbg_gdb"], in_=dstg[0:32, :]), inc="sq", inc_n=16)
                P.wait("vector", "sq", dsq)
            P.wait("tensor", "vv", vv3)
            sv2 = 0
            ypar = (qt * HL + il) % 2
            for q2 in range(2):
                psm = psZ1[0:1, 512 * q2:512 * (q2 + 1)]
                pv = P.emit("tensor", lambda e, s, psm=psm, Ub=Ub, q2=q2: e.matmul(
                    psm, onesb_s[:], Ub[:, 512 * q2:512 * (q2 + 1)], start=True, stop=True), inc="pe")
                yred_prev[0] = pv
                P.wait("scalar", "pe", pv)
                P.wait("scalar", "gq", P.cnt["gq"])
                sv2 = P.emit("scalar", lambda e, s, psm=psm, q2=q2, ypar=ypar: e.activation(
                    ar(7680 + 1024 * ypar + 512 * q2, 512)[0:1, :], psm, AF.Copy), inc="ss")
                P.wait("tensor", "ss", sv2)
            P.wait("gpsimd", "ss", sv2)
            for q2 in range(2):
                col0 = qt * 1024 + 512 * q2
                y_gq[0] = P.emit("gpsimd", lambda e, s, il=il, col0=col0, q2=q2, ypar=ypar: e.dma_start(
                    out=cc_y_in.ap()[il:il + 1, col0:col0 + 512],
                    in_=ar(7680 + 1024 * ypar + 512 * q2, 512)[0:1, :]), inc="gq", inc_n=16)
            ydma_prev[ypar] = y_gq[0]
    P.wait("gpsimd", "gq", y_gq[0])
    P.emit("gpsimd", lambda e, s: e.collective_compute(
        "AllGather", ALU.bypass, replica_groups=groups,
        ins=[cc_y_in.ap().opt()], outs=[g_yg.ap().opt()]), inc="cc")
    cc_y_done = P.cnt["cc"]

    yT = arf(0, 1024)[0:32, :]
    E_s = arf(5120, 2048)[0:H, :]
    awg_v = din["AWg"].bitcast(F32R)
    ld_awg = [0] * DT
    P.wait("sync", "pe", xa_pe)
    v_E = P.emit("sync", lambda e, s: e.dma_start(out=E_s, in_=din["E_mat"].bitcast(F32R)), inc="s3", inc_n=16)
    ld_awg[0] = P.emit("sync", lambda e, s: e.dma_start(
        out=arf(1024, 2048), in_=awg_v[0]), inc="sa0", inc_n=16)
    ld_awg[1] = P.emit("sync", lambda e, s: e.dma_start(
        out=arf(3072, 2048), in_=awg_v[1]), inc="sa1", inc_n=16)
    P.wait("sync", "cc", cc_y_done)
    P.wait("sync", "vv", P.cnt["vv"])
    P.wait("sync", "ss", P.cnt["ss"])
    yg_s = ar(0, 4096)[0:32, :]
    yt0 = ar(4096, 1024)[0:32, :]
    yt1 = ar(5120, 1024)[0:32, :]
    v_yg = P.emit("sync", lambda e, s: e.dma_start(out=yg_s, in_=g_yg.ap()), inc="s3", inc_n=16)
    P.wait("vector", "s3", v_yg)
    P.emit("vector", lambda e, s: e.tensor_scalar_mul(yt0, yg_s[:, 0:1024], msk_s[0:32, 0:1]))
    P.emit("vector", lambda e, s: e.scalar_tensor_tensor(
        yt1, yg_s[:, 1024:2048], msk_s[0:32, 1:2], yt0, ALU.mult, ALU.add))
    P.emit("vector", lambda e, s: e.scalar_tensor_tensor(
        yt0, yg_s[:, 2048:3072], msk_s[0:32, 2:3], yt1, ALU.mult, ALU.add))
    ysel_vv = P.emit("vector", lambda e, s: e.scalar_tensor_tensor(
        yt1, yg_s[:, 3072:4096], msk_s[0:32, 3:4], yt0, ALU.mult, ALU.add), inc="vv")
    P.wait("scalar", "vv", ysel_vv)
    ysel_ss = P.emit("scalar", lambda e, s: e.activation(yT, yt1, AF.Copy), inc="ss")
    if DEBUG:
        P.wait("sync", "vv", ysel_vv)
        P.emit("sync", lambda e, s: e.dma_start(out=dbg["dbg_y"], in_=g_yg.ap()[0:32, 0:TOK]), inc="sq", inc_n=16)
    p3_vv = {0: P.cnt["vv"], 1: P.cnt["vv"]}
    p3_ss = {0: P.cnt["ss"], 1: P.cnt["ss"]}
    aw_pe = [0, 0]
    yf_sq = [0]
    yf_sq2 = {0: 0, 1: 0}
    for j in range(DT):
        aw = arf(1024 + (j % 2) * 2048, 2048)
        if 2 <= j + 1 < DT:
            if j >= 1:
                P.wait("sync", "pe", aw_pe[(j + 1) % 2])
            ld_awg[j + 1] = P.emit("sync", lambda e, s, j=j: e.dma_start(
                out=arf(1024 + ((j + 1) % 2) * 2048, 2048), in_=awg_v[j + 1]),
                inc=("sa0" if (j + 1) % 2 == 0 else "sa1"), inc_n=16)
        for ts in range(TS):
            it = (j * TS + ts) % 2
            psb = psZ1 if it == 0 else psZ0
            psy = psb[:, 0:512]
            psg_ = psb[:, 512:1024]
            P.wait("tensor", "ss", ysel_ss)
            P.wait("tensor", "s3", P.cnt["s3"])
            P.wait("tensor", "sa0" if j % 2 == 0 else "sa1", ld_awg[j])
            P.wait("tensor", "vv", p3_vv[it])
            P.wait("tensor", "ss", p3_ss[it])
            pv_y = P.emit("tensor", lambda e, s, psy=psy, j=j, ts=ts: e.matmul(
                psy, E_s[:, 128 * j:128 * (j + 1)], yT[:, 512 * ts:512 * (ts + 1)],
                start=True, stop=True), inc="pe")
            for k in range(KD):
                lh = aw[:, k * 128:(k + 1) * 128]
                rh = xn[:, k * TOK + ts * 512: k * TOK + ts * 512 + 512]
                P.emit("tensor", lambda e, s, psg_=psg_, lh=lh, rh=rh, k=k: e.matmul(
                    psg_, lh, rh, start=(k == 0), stop=(k == KD - 1)),
                    inc="pe" if k == KD - 1 else None)
            pv_g = P.cnt["pe"]
            if ts == 1:
                aw_pe[j % 2] = pv_g
            yh = ar(6144, 512)
            gt = ar(6656, 512)
            yf = ar(7168 + it * 512, 512)
            P.wait("vector", "pe", pv_y)
            P.wait("vector", "so", yf_sq2[it])
            p3_vv[it] = P.emit("vector", lambda e, s, yh=yh, psy=psy, j=j, ts=ts: e.scalar_tensor_tensor(
                yh, xn.bitcast(F32)[:, j * TOK + ts * 512: j * TOK + ts * 512 + 512],
                Dp_s[:, j:j + 1], psy, ALU.mult, ALU.add), inc="vv")
            P.wait("scalar", "pe", pv_g)
            p3_ss[it] = P.emit("scalar", lambda e, s, gt=gt, psg_=psg_, j=j: e.activation(
                gt, psg_, AF.Sigmoid, bias=bgc_s[:, j:j + 1]), inc="ss")
            P.wait("vector", "ss", p3_ss[it])
            vv2 = P.emit("vector", lambda e, s, yf=yf, yh=yh, gt=gt: e.tensor_tensor(
                yf, yh, gt, ALU.mult), inc="vv")
            P.wait("scalar", "vv", vv2)
            yf_sq[0] = P.emit("scalar", lambda e, s, yf=yf, j=j, ts=ts: e.dma_start(
                out=d_yfull.ap()[128 * j:128 * (j + 1), 512 * ts:512 * (ts + 1)],
                in_=yf), inc="so", inc_n=16)
            yf_sq2[it] = yf_sq[0]
    if DEBUG:
        P.wait("sync", "so", yf_sq[0])
        P.emit("sync", lambda e, s: e.dma_start(out=dbg["dbg_yfull"], in_=d_yfull.ap()), inc="sq", inc_n=16)

    def wg_blk(par): return arf(0 + par * 2048, 2048)
    def wu_blk(par): return arf(4096 + par * 2048, 2048)
    def bgp_t(par): return arf(8192 + par * 256, 256)
    def bup_t(par): return arf(8704 + par * 256, 256)
    def sg_b(X, epar): return ar(0 + (2 * X + epar) * 256, 256, BF16)
    def wu_b(X, epar): return ar(1024 + (2 * X + epar) * 256, 256, BF16)
    def hp_b(X, epar): return ar(2048 + (2 * X + epar) * 256, 256, BF16)
    def hb_b(par): return ar(3072 + par * 512, 512, BF16)
    def adt_t(par): return ar(4096 + par * 1024, 1024)
    def adt_bf(par): return ar(6144 + par * 512, 512, BF16)

    wgt_v = din["WgT"].bitcast(F32R)
    wut_v = din["WuT"].bitcast(F32R)
    bgp_v = din["BgTp"].bitcast(F32R).rearrange("(q p) f -> p q f", p=128)
    bup_v = din["BuTp"].bitcast(F32R).rearrange("(q p) f -> p q f", p=128)
    adt_v = din["AdT3"]

    P.wait("sync", "pe", P.cnt["pe"])
    P.wait("sync", "vv", P.cnt["vv"])
    P.wait("sync", "ss", P.cnt["ss"])
    P.wait("gpsimd", "pe", P.cnt["pe"])

    p4_ld = [0] * (FT + 2)
    p4_ldu = [0] * (FT + 2)
    p4_half_pe = [0, 0]
    p4_blk_pe = [0, 0]
    adt_vv = [0, 0]
    z_pe_all = [0, 0]

    def p4_wg_load(ft):
        par = ft % 2
        spp = "sp0" if par == 0 else "sp1"
        P.wait("sync", "pe", p4_half_pe[par])
        P.emit("sync", lambda e, s, dst=wg_blk(par), ft=ft: e.dma_start(
            out=dst, in_=wgt_v[ft]), inc=spp, inc_n=16)

    def p4_wu_load(ft):
        par = ft % 2
        upp = "up0" if par == 0 else "up1"
        P.wait("gpsimd", "pe", p4_half_pe[par])
        P.emit("gpsimd", lambda e, s, dst=wu_blk(par), ft=ft: e.dma_start(
            out=dst, in_=wut_v[ft]), inc=upp, inc_n=16)

    def p4_bp_loads(ft):
        par = ft % 2
        spp = "sp0" if par == 0 else "sp1"
        upp = "up0" if par == 0 else "up1"
        P.wait("sync", "pe", p4_blk_pe[par])
        P.emit("sync", lambda e, s, dst=bgp_t(par), ft=ft: e.dma_start(
            out=dst.rearrange("p (q c) -> p q c", q=2),
            in_=bgp_v[:, :, 128 * ft:128 * (ft + 1)]), inc=spp, inc_n=16)
        P.wait("gpsimd", "pe", p4_blk_pe[par])
        P.emit("gpsimd", lambda e, s, dst=bup_t(par), ft=ft: e.dma_start(
            out=dst.rearrange("p (q c) -> p q c", q=2),
            in_=bup_v[:, :, 128 * ft:128 * (ft + 1)]), inc=upp, inc_n=16)
        return P.cnt[spp], P.cnt[upp]

    def p4_adt(ft):
        par = ft % 2
        adp = "ad0" if par == 0 else "ad1"
        P.wait("scalar", "vv", adt_vv[par])
        v_ad = P.emit("scalar", lambda e, s, ft=ft, par=par: e.dma_start(
            out=adt_t(par), in_=adt_v[ft]), inc=adp, inc_n=16)
        P.wait("vector", adp, v_ad)
        P.wait("vector", "pe", z_pe_all[par])
        adt_vv[par] = P.emit("vector", lambda e, s, par=par: e.tensor_copy(
            adt_bf(par), adt_t(par)), inc="vv")

    sg_done = [P.cnt["ss"], P.cnt["ss"]]
    wu_done = [P.cnt["vv"], P.cnt["vv"]]
    last_sca_read = [P.cnt["ss"], P.cnt["ss"]]
    last_vv_read = [P.cnt["vv"], P.cnt["vv"]]
    hprime_vv = [[0, 0], [0, 0]]
    z_pe = [[0, 0], [0, 0]]
    hb_dma = [0, 0]
    first_wu = [True]
    first_diff = [True]

    def b_chunk(X, ft, chunk):
        par = ft % 2
        psg2 = psA0 if X == 0 else psB0
        psu2 = psA1 if X == 0 else psB1
        if chunk == 0:
            P.wait("tensor", "sp0" if par == 0 else "sp1", p4_ld[ft])
            P.wait("tensor", "up0" if par == 0 else "up1", p4_ldu[ft])
            P.wait("tensor", "ss", last_sca_read[X])
            P.wait("tensor", "vv", last_vv_read[X])
        for k in (2 * chunk, 2 * chunk + 1):
            rh = xn[:, k * TOK + X * 512: k * TOK + X * 512 + 512]
            lg = wg_blk(par)[:, k * 128:(k + 1) * 128]
            lu = wu_blk(par)[:, k * 128:(k + 1) * 128]
            P.emit("tensor", lambda e, s, psg2=psg2, lg=lg, rh=rh, k=k: e.matmul(
                psg2[:], lg, rh, start=(k == 0), stop=False))
            P.emit("tensor", lambda e, s, psu2=psu2, lu=lu, rh=rh, k=k: e.matmul(
                psu2[:], lu, rh, start=(k == 0), stop=False))

    def emit_z(X, ft, ez):
        par = ft % 2
        epz = ez % 2
        P.wait("tensor", "vv", hprime_vv[X][epz])
        P.wait("tensor", "vv", adt_vv[par])
        z_pe[X][epz] = P.emit("tensor", lambda e_, s, X=X, ez=ez, par=par, epz=epz, ft=ft: e_.matmul(
            psZ0[:, X * 512:(X + 1) * 512],
            adt_bf(par)[:, 128 * ez:128 * ez + 128], hp_b(X, epz)[:, 0:512],
            start=(ft == 0 and ez == 0), stop=(ft == FT - 1 and ez == NE - 1)), inc="pe")

    def e_step(X, ft, e):
        par = ft % 2
        epar = e % 2
        q, r = divmod(e, 4)
        ps_g = psA0 if X == 0 else psB0
        ps_u = psA1 if X == 0 else psB1
        lhg = bgp_t(par)[:, 128 * q:128 * (q + 1)]
        lhu = bup_t(par)[:, 128 * q:128 * (q + 1)]
        rg = xapg[q][32 * r:32 * r + 32, X * 512:X * 512 + 512]
        ru = xapu[q][32 * r:32 * r + 32, X * 512:X * 512 + 512]
        tpos = (32 * r, 0)
        if first_diff[0]:
            P.wait("tensor", "g2", xap_gq)
            first_diff[0] = False
        P.wait("tensor", "ss", sg_done[X])
        pvg = P.emit("tensor", lambda e_, s, ps_g=ps_g, lhg=lhg, rg=rg, r=r, e=e, tpos=tpos: e_.matmul(
            ps_g[:], lhg[32 * r:32 * r + 32, :], rg, start=False, stop=(e == NE - 1),
            tile_position=tpos), inc="pe")
        P.wait("tensor", "vv", wu_done[X])
        pvu = P.emit("tensor", lambda e_, s, ps_u=ps_u, lhu=lhu, ru=ru, r=r, e=e, tpos=tpos: e_.matmul(
            ps_u[:], lhu[32 * r:32 * r + 32, :], ru, start=False, stop=(e == NE - 1),
            tile_position=tpos), inc="pe")
        if e > 0:
            emit_z(X, ft, e - 1)
        P.wait("scalar", "pe", pvg)
        P.wait("scalar", "vv", hprime_vv[X][epar])
        sg_done[X] = P.emit("scalar", lambda e_, s, ps_g=ps_g, X=X, epar=epar: e_.activation(
            sg_b(X, epar), ps_g[:], AF.Silu), inc="ss")
        P.wait("vector", "pe", pvu)
        if first_wu[0]:
            P.wait("vector", "s4", wbc_sq)
            first_wu[0] = False
        P.wait("vector", "pe", z_pe[X][epar])
        wu_done[X] = P.emit("vector", lambda e_, s, ps_u=ps_u, e=e, X=X, epar=epar: e_.scalar_tensor_tensor(
            wu_b(X, epar), ps_u[:], 1.0, wbc[:, e * TOK + X * 512: e * TOK + X * 512 + 512],
            ALU.mult, ALU.mult), inc="vv")
        P.wait("vector", "ss", sg_done[X])
        P.emit("vector", lambda e_, s, X=X, epar=epar: e_.tensor_tensor(
            hp_b(X, epar)[:, 0:512], sg_b(X, epar), wu_b(X, epar), ALU.mult), inc="vv")
        if e == 0:
            P.wait("vector", "sh0" if par == 0 else "sh1", hb_dma[par])
            P.emit("vector", lambda e_, s, par=par, X=X, epar=epar: e_.tensor_copy(
                hb_b(par)[:, X * 512:(X + 1) * 512], hp_b(X, epar)[:, 0:512]))
        else:
            P.emit("vector", lambda e_, s, par=par, X=X, epar=epar: e_.tensor_tensor(
                hb_b(par)[:, X * 512:(X + 1) * 512],
                hb_b(par)[:, X * 512:(X + 1) * 512], hp_b(X, epar)[:, 0:512], ALU.add))
        hprime_vv[X][epar] = P.cnt["vv"]
        if e == NE - 1:
            emit_z(X, ft, e)
            last_sca_read[X] = sg_done[X]
            last_vv_read[X] = wu_done[X]
            if X == 1:
                z_pe_all[par] = z_pe[X][e % 2]
                P.wait("gpsimd", "vv", P.cnt["vv"])
                hb_dma[par] = P.emit("gpsimd", lambda e_, s, par=par, ft=ft: e_.dma_start(
                    out=d_hbar.ap()[128 * ft:128 * (ft + 1), :], in_=hb_b(par)[:, 0:1024]),
                    inc=("sh0" if par == 0 else "sh1"), inc_n=16)

    p4_wg_load(0)
    p4_wu_load(0)
    p4_ld[0], p4_ldu[0] = p4_bp_loads(0)
    p4_adt(0)
    p4_wg_load(1)
    p4_wu_load(1)
    p4_ld[1], p4_ldu[1] = p4_bp_loads(1)
    p4_adt(1)
    for chunk in range(8):
        b_chunk(0, 0, chunk)
    for ft in range(FT):
        if 2 <= ft + 1 < FT:
            p4_adt(ft + 1)
        for s in range(NE):
            e_step(0, ft, s)
            b_chunk(1, ft, s)
        if ft + 2 < FT:
            p4_half_pe[ft % 2] = P.cnt["pe"]
            p4_wg_load(ft + 2)
            p4_wu_load(ft + 2)
        for s in range(NE):
            if ft + 1 < FT:
                b_chunk(0, ft + 1, s)
            e_step(1, ft, s)
        if ft + 2 < FT:
            p4_blk_pe[ft % 2] = P.cnt["pe"]
            p4_ld[ft + 2], p4_ldu[ft + 2] = p4_bp_loads(ft + 2)
    z_last_pe = P.cnt["pe"]

    P.wait("scalar", "pe", z_last_pe)
    z_ss = P.emit("scalar", lambda e, s: e.activation(
        z_sb[:], psZ0[:], AF.Copy, scale=misc_s[:, 2:3]), inc="ss")
    if DEBUG:
        P.wait("sync", "ss", z_ss)
        P.emit("sync", lambda e, s: e.dma_start(out=dbg["dbg_z"], in_=z_sb[:].bitcast(F32)), inc="sq", inc_n=16)
        P.wait("sync", "sh0", hb_dma[0])
        P.wait("sync", "sh1", hb_dma[1])
        P.emit("sync", lambda e, s: e.dma_start(out=dbg["dbg_hbar"], in_=d_hbar.ap()), inc="sq", inc_n=16)

    def wd_arr(par): return ar(0 + par * 2048, 2048)
    def wd_bf(par): return ar(4096 + par * 1024, 1024, BF16)
    def hb_ch(par): return ar(6144 + par * 2048, 2048, BF16)
    def x_b(par): return ar(10240 + par * 512, 512)
    def ot_b(par): return ar(11264 + par * 512, 512)
    def wo_ch(par): return arf(0 + par * 2048, 2048)
    def yfT_b(par): return arf(4096 + par * 512, 512)
    bdT_s = arf(5120, D)

    wdt_v = din["WdT"]
    hb_vv = d_hbar.ap().rearrange("(c k p) t -> c p k t", p=128, k=4)
    wo_v = din["Wo"].bitcast(F32R)
    yf_v = d_yfull.ap().bitcast(F32R).rearrange("(k p) t -> p k t", p=128)
    x_v = din["x_sh"].bitcast(F32R).rearrange("(n p) d -> n p d", p=128)
    out_v = dout.rearrange("(n p) d -> n p d", p=128)

    ps_out = [psA0[:, :], psA1[:, :], psB0[:, :], psB1[:, :],
              psZ0[:, 0:512], psZ0[:, 512:1024], psZ1[:, 0:512], psZ1[:, 512:1024]]

    P.wait("sync", "pe", z_last_pe)
    P.wait("sync", "ss", P.cnt["ss"])
    P.wait("sync", "vv", P.cnt["vv"])
    P.wait("gpsimd", "pe", z_last_pe)
    v_bdt = P.emit("sync", lambda e, s: e.dma_start(
        out=bdT_s, in_=din["bdTp"].bitcast(F32R)), inc="sbd", inc_n=16)
    evac_vv = [0] * TSUB
    ch_pe = {0: 0, 1: 0}
    conv_ss = [0, 0]
    wo_pe = [0, 0]
    yfT_pe = [0, 0]
    def x4_b(i):
        return arf(7168 + i * 512, 512)

    def ot4_b(i):
        return ar(10240 + i * 512, 512)

    id_r = idr_s[:]
    evac_ss = [0] * TSUB
    v_x = [0] * TSUB
    v_wo = [0] * 4
    xid4 = [0, 0, 0, 0]
    ot_st4 = [0, 0, 0, 0]
    for ds in range(DS):
        xprev = list(xid4)
        for ch in range(NCH):
            par = ch % 2
            P.wait("sync", "ss", conv_ss[par])
            sup = "su0" if par == 0 else "su1"
            svp = "sv0" if par == 0 else "sv1"
            v_wd = P.emit("sync", lambda e, s, ds=ds, ch=ch, par=par: e.dma_start(
                out=wd_arr(par), in_=wdt_v[ch, ds]), inc=sup, inc_n=16)
            if ds == 0 and ch == 0:
                P.wait("gpsimd", "sh0", hb_dma[0])
                P.wait("gpsimd", "sh1", hb_dma[1])
            P.wait("gpsimd", "pe", ch_pe[par])
            v_hb = P.emit("gpsimd", lambda e, s, ch=ch, par=par: e.dma_start(
                out=hb_ch(par).rearrange("p (k c) -> p k c", k=4),
                in_=hb_vv[ch]), inc=svp, inc_n=16)
            P.wait("scalar", sup, v_wd)
            P.wait("scalar", "pe", ch_pe[par])
            conv_ss[par] = P.emit("scalar", lambda e, s, par=par: e.activation(
                wd_bf(par), wd_arr(par), AF.Copy), inc="ss")
            if ch <= 3:
                for tsub in (2 * ch, 2 * ch + 1):
                    sl = tsub % 4
                    gate = xprev[sl] if ch <= 1 else xid4[sl]
                    if gate:
                        P.wait("scalar", "pe", gate)
                    v_x[tsub] = P.emit("scalar", lambda e, s, tsub=tsub, ds=ds, sl=sl: e.dma_start(
                        out=x4_b(sl), in_=x_v[tsub][:, 512 * ds:512 * (ds + 1)]),
                        inc="sz0", inc_n=16)
            if ch in (1, 3):
                woc = (ch - 1) // 2
                parw = woc % 2
                if ds > 0:
                    P.wait("scalar", "pe", wo_pe[parw])
                v_wo[woc] = P.emit("scalar", lambda e, s, ds=ds, woc=woc, parw=parw: e.dma_start(
                    out=wo_ch(parw), in_=wo_v[woc, ds]),
                    inc=("sm0" if parw == 0 else "sm1"), inc_n=16)
            P.wait("tensor", "ss", conv_ss[par])
            P.wait("tensor", svp, v_hb)
            for tsub in range(TSUB):
                if ds > 0 and ch == 0:
                    P.wait("tensor", "ss", evac_ss[tsub])
                for k in range(4):
                    lh = hb_ch(par).rearrange("p (k t) -> p k t", k=4)[:, k, 128 * tsub:128 * (tsub + 1)]
                    rh = wd_bf(par).rearrange("p (k c) -> p k c", k=4)[:, k, :]
                    first = (ch == 0 and k == 0)
                    is_last = (tsub == TSUB - 1 and k == 3)
                    P.emit("tensor", lambda e, s, tsub=tsub, lh=lh, rh=rh, first=first: e.matmul(
                        ps_out[tsub], lh, rh, start=first, stop=False),
                        inc="pe" if is_last else None)
            if 1 <= ch <= 4:
                for tsub in (2 * (ch - 1), 2 * (ch - 1) + 1):
                    sl = tsub % 4
                    P.wait("tensor", "sz0", v_x[tsub])
                    P.emit("tensor", lambda e, s, tsub=tsub, sl=sl: e.matmul(
                        ps_out[tsub], id_r, x4_b(sl), start=False, stop=False), inc="pe")
                    xid4[sl] = P.cnt["pe"]
            ch_pe[par] = P.cnt["pe"]
        for woc in range(4):
            par = woc % 2
            smp = "sm0" if par == 0 else "sm1"
            if woc >= 2:
                P.wait("scalar", "pe", wo_pe[par])
                v_wo[woc] = P.emit("scalar", lambda e, s, ds=ds, woc=woc, par=par: e.dma_start(
                    out=wo_ch(par), in_=wo_v[woc, ds]), inc=smp, inc_n=16)
            if ds == 0 and woc == 0:
                P.wait("gpsimd", "so", yf_sq[0])
            for tsub in range(TSUB):
                parT = tsub % 2
                if ds > 0 or woc > 0 or tsub >= 2:
                    P.wait("gpsimd", "pe", yfT_pe[parT])
                syp = "sy0" if parT == 0 else "sy1"
                v_yf = P.emit("gpsimd", lambda e, s, tsub=tsub, parT=parT, woc=woc: e.dma_start(
                    out=yfT_b(parT).rearrange("p (k c) -> p k c", k=4),
                    in_=yf_v[:, 4 * woc:4 * (woc + 1), 128 * tsub:128 * (tsub + 1)]), inc=syp, inc_n=16)
                P.wait("tensor", smp, v_wo[woc])
                P.wait("tensor", syp, v_yf)
                for k in range(4):
                    lh = yfT_b(parT).rearrange("p (k c) -> p k c", k=4)[:, k, :]
                    rh = wo_ch(par).rearrange("p (k c) -> p k c", k=4)[:, k, :]
                    P.emit("tensor", lambda e, s, tsub=tsub, lh=lh, rh=rh: e.matmul(
                        ps_out[tsub], lh, rh, start=False, stop=False),
                        inc="pe" if k == 3 else None)
                yfT_pe[parT] = P.cnt["pe"]
            wo_pe[par] = P.cnt["pe"]
        for tsub in range(TSUB):
            P.wait("tensor", "ss", z_ss)
            if ds == 0 and tsub == 0:
                P.wait("tensor", "sbd", v_bdt)
            P.emit("tensor", lambda e, s, tsub=tsub, ds=ds: e.matmul(
                ps_out[tsub], z_sb[:, tsub * 128: tsub * 128 + 128],
                bdT_s[:, ds * 512: ds * 512 + 512],
                start=False, stop=True), inc="pe")
            pv = P.cnt["pe"]
            sl = tsub % 4
            P.wait("scalar", "pe", pv)
            if ot_st4[sl]:
                P.wait("scalar", "st0", ot_st4[sl])
            evac_ss[tsub] = P.emit("scalar", lambda e, s, tsub=tsub, sl=sl: e.activation(
                ot4_b(sl), ps_out[tsub], AF.Copy), inc="ss")
            ot_st4[sl] = P.emit("scalar", lambda e, s, tsub=tsub, ds=ds, sl=sl: e.dma_start(
                out=out_v[tsub][:, 512 * ds:512 * (ds + 1)], in_=ot4_b(sl)),
                inc="st0", inc_n=16)

    P.wait("sync", "st0", P.cnt["st0"])
    P.wait("gpsimd", "gq", P.cnt["gq"])

    with ExitStack() as st_ctx:
        sems = {}
        for name in P.cnt:
            sems[name] = st_ctx.enter_context(nc.semaphore(name))
        block = st_ctx.enter_context(nc.Block())

        def runner(eng_name):
            def run(engine):
                for fn in P.items[eng_name]:
                    fn(engine, sems)
            return run
        block.sync(runner("sync"))
        block.tensor(runner("tensor"))
        block.vector(runner("vector"))
        block.scalar(runner("scalar"))
        block.gpsimd(runner("gpsimd"))

    stack.close()
    return nc


def kernel(**inputs):
    per_core = _prep(inputs)
    nc = build_program()
    res = run_bass_kernel_spmd(nc, per_core, core_ids=list(range(NCORES)))
    out = np.empty((2, LSEQ, D), np.float32)
    for c in range(NCORES):
        b, tp = divmod(c, 4)
        out[b, tp * TOK:(tp + 1) * TOK, :] = res.results[c]["out_sh"]
    return out


# version 16
# speedup vs baseline: 1.6215x; 1.0094x over previous
import numpy as np
from contextlib import ExitStack

import concourse.bass as bass
import concourse.mybir as mybir
from concourse.bass_utils import run_bass_kernel_spmd

F32 = mybir.dt.float32
F32R = mybir.dt.float32r
BF16 = mybir.dt.bfloat16
AF = mybir.ActivationFunctionType
ALU = mybir.AluOpType
AX = mybir.AxisListType

NCORES = 8
TOK = 1024
D = 2048
DFF = 8192
NE = 8
H = 32
HL = 8
LSEQ = 4096
KD = 16
FT = 64
TS = 2
DT = 16
DS = 4
TSUB = 8
NCH = 16
LORA_S = 2.0

DEBUG = False


class Prog:
    ENGINES = ("sync", "tensor", "vector", "scalar", "gpsimd")

    def __init__(self):
        self.items = {e: [] for e in self.ENGINES}
        self.cnt = {}

    def sem(self, name):
        self.cnt[name] = 0

    def inc(self, name, n):
        self.cnt[name] += n
        return self.cnt[name]

    def emit(self, eng, fn, inc=None, inc_n=1):
        if inc is None:
            self.items[eng].append(fn)
            return None
        val = self.inc(inc, inc_n)

        def wrapped(e, s, fn=fn, inc=inc, inc_n=inc_n):
            fn(e, s).then_inc(s[inc], inc_n)
        self.items[eng].append(wrapped)
        return val

    def wait(self, eng, name, val):
        assert val <= self.cnt[name], f"wait {name}>={val} but count={self.cnt[name]}"
        if val <= 0:
            return
        self.items[eng].append(lambda e, s, name=name, val=val: e.wait_ge(s[name], val))


def _prep(inputs):
    import ml_dtypes
    C = np.ascontiguousarray
    x = C(np.asarray(inputs["L0_attn_x"], np.float32))
    Wx = C(np.asarray(inputs["L0_attn_Wx"], np.float32))
    AWg = C(np.asarray(inputs["L0_attn_Wg"], np.float32))
    bg = np.asarray(inputs["L0_attn_bg"], np.float32)
    Wo = C(np.asarray(inputs["L0_attn_Wo"], np.float32))
    A = np.asarray(inputs["L0_attn_A"], np.float32)
    Dp = np.asarray(inputs["L0_attn_Dp"], np.float32)
    rw = np.asarray(inputs["L0_ffn_router_w"], np.float32)
    fWg = np.asarray(inputs["L0_ffn_Wg"], np.float32)
    fWu = np.asarray(inputs["L0_ffn_Wu"], np.float32)
    fWd = np.asarray(inputs["L0_ffn_Wd"], np.float32)
    Ag = np.asarray(inputs["L0_ffn_Ag"], np.float32)
    Bg = np.asarray(inputs["L0_ffn_Bg"], np.float32)
    Au = np.asarray(inputs["L0_ffn_Au"], np.float32)
    Bu = np.asarray(inputs["L0_ffn_Bu"], np.float32)
    Ad = np.asarray(inputs["L0_ffn_Ad"], np.float32)
    Bd = np.asarray(inputs["L0_ffn_Bd"], np.float32)

    def _ktile(W, ncols):
        kt = W.reshape(KD, 128, ncols // 128, 128)
        return C(np.transpose(kt, (2, 1, 0, 3)).reshape(ncols // 128, 128, KD * 128))

    def _p5tile(WdT_):
        t = WdT_.reshape(NCH, 4, 128, DS, 512)
        return C(np.transpose(t, (0, 3, 2, 1, 4)).reshape(NCH, DS, 128, 4 * 512))

    def _wotile(Wo_):
        t = Wo_.reshape(4, 4, 128, DS, 512)
        return C(np.transpose(t, (0, 3, 2, 1, 4)).reshape(4, DS, 128, 4 * 512))

    def _adtpad(Admat):
        out = np.zeros((NE, DFF, 128), np.float32)
        for e in range(NE):
            out[e, :, 16 * e:16 * e + 16] = Admat[e].T
        return C(out)

    def _adtile(A3):
        t = A3.reshape(NE, FT, 128, 128)
        return C(np.transpose(t, (1, 2, 0, 3)).reshape(FT, 128, NE * 128))

    def pairs(Bmat):
        BT = np.transpose(Bmat, (0, 2, 1))
        return C(np.concatenate(
            [np.concatenate([BT[e], BT[(e - 1) % NE]], 0) for e in range(NE)], 0))

    bdq = np.zeros((128, D), np.float32)
    for e in range(NE):
        bdq[16 * e:16 * e + 16, :] = Bd[e].T

    shared = dict(
        Wx=Wx,
        rwT=C(rw.T),
        WgT=_ktile(fWg.T, DFF), WuT=_ktile(fWu.T, DFF),
        WdT=_p5tile(fWd.T), AWg=_ktile(AWg, D), Wo=_wotile(Wo),
        AgT=C(np.transpose(Ag, (2, 0, 1)).reshape(D, NE * 16)),
        AuT=C(np.transpose(Au, (2, 0, 1)).reshape(D, NE * 16)),
        AdT3=_adtile(_adtpad(Ad)),
        BgTp=pairs(Bg), BuTp=pairs(Bu),
        bdTp=C(bdq),
        E_mat=C(np.repeat(np.eye(H, dtype=np.float32), D // H, axis=1)),
        Dp_cols=C(np.repeat(Dp, D // H).reshape(DT, 128).T),
        bg_cols=C(bg.reshape(DT, 128).T),
        id128=np.eye(128, dtype=np.float32),
        id128b=np.eye(128, dtype=ml_dtypes.bfloat16),
        ones_col=np.ones((128, 1), np.float32),
        ones_b=np.ones((128, 1), ml_dtypes.bfloat16),
        misc_c=np.broadcast_to(np.array([1e-5, -0.5, LORA_S, -LORA_S, -1.0, 1.0 / D], np.float32), (128, 6)).copy(),
        zeros128=np.zeros((128, TOK), np.float32),
    )
    AT = C(A.T)
    per_core = []
    for c in range(NCORES):
        b, tp = divmod(c, 4)
        m = dict(shared)
        m["x_sh"] = C(x[b, tp * TOK:(tp + 1) * TOK, :])
        m["AT_loc"] = C(AT[:, HL * tp:HL * (tp + 1)])
        msk = np.zeros((128, 4), np.float32)
        msk[:, tp] = 1.0
        m["msk"] = msk
        mct = np.zeros((H, 128), np.float32)
        for k in range(H):
            r, il = divmod(k, HL)
            mct[k, 32 * r + HL * tp + il] = 1.0
        m["mctT"] = C(mct.T).astype(ml_dtypes.bfloat16)
        per_core.append(m)
    return per_core


def build_program():
    nc = bass.Bass("TRN2", target_bir_lowering=False)
    P = Prog()
    for s in ("sq", "gq", "cc", "pe", "vv", "ss", "st",
              "sx0", "sx1", "g1", "g2", "swx", "srw", "sag", "s2", "sB",
              "sd0", "sd1", "s3", "sa0", "sa1", "so", "s4", "sp0", "sp1",
              "sh0", "sh1", "su0", "su1", "sv0", "sv1", "sm0", "sm1",
              "sy0", "sy1", "sz0", "sz1", "sbd", "st0", "st1",
              "up0", "up1", "ad0", "ad1"):
        P.sem(s)

    din = {}
    for name, shape in [
        ("x_sh", [TOK, D]), ("AT_loc", [128, HL]), ("Wx", [D, 288]),
        ("rwT", [D, NE]), ("WgT", [DFF // 128, 128, KD * 128]), ("WuT", [DFF // 128, 128, KD * 128]), ("WdT", [NCH, DS, 128, 4 * 512]),
        ("AWg", [DT, 128, KD * 128]), ("Wo", [4, DS, 128, 4 * 512]), ("AgT", [D, 128]), ("AuT", [D, 128]),
        ("AdT3", [FT, 128, NE * 128]), ("E_mat", [H, D]), ("Dp_cols", [128, DT]),
        ("bg_cols", [128, DT]), ("id128", [128, 128]), ("ones_col", [128, 1]),
        ("misc_c", [128, 6]), ("msk", [128, 4]),
        ("zeros128", [128, TOK]), ("BgTp", [256, DFF]), ("BuTp", [256, DFF]),
        ("bdTp", [128, D]),
    ]:
        din[name] = nc.dram_tensor(name, shape, F32, kind="ExternalInput").ap()
    for name, shape in [("id128b", [128, 128]), ("ones_b", [128, 1]), ("mctT", [128, H])]:
        din[name] = nc.dram_tensor(name, shape, BF16, kind="ExternalInput").ap()
    dout = nc.dram_tensor("out_sh", [TOK, D], F32, kind="ExternalOutput").ap()

    dbg = {}
    if DEBUG:
        for name, shape in [
            ("dbg_xn", [D, TOK]), ("dbg_w", [TOK, NE]), ("dbg_y", [H, TOK]),
            ("dbg_yfull", [D, TOK]), ("dbg_z", [128, TOK]),
        ]:
            dbg[name] = nc.dram_tensor(name, shape, F32, kind="ExternalOutput").ap()
        dbg["dbg_hbar"] = nc.dram_tensor("dbg_hbar", [DFF, TOK], BF16, kind="ExternalOutput").ap()
        dbg["dbg_cols"] = nc.dram_tensor("dbg_cols", [128, 64], F32, kind="ExternalOutput").ap()
        for nm in ("dbg_B", "dbg_db", "dbg_av", "dbg_b2", "dbg_hs"):
            dbg[nm] = nc.dram_tensor(nm, [128, TOK], F32, kind="ExternalOutput").ap()
        dbg["dbg_gdb"] = nc.dram_tensor("dbg_gdb", [H, TOK], F32, kind="ExternalOutput").ap()
        dbg["dbg_gds"] = nc.dram_tensor("dbg_gds", [128, TOK], F32, kind="ExternalOutput").ap()
        dbg["dbg_arg"] = nc.dram_tensor("dbg_arg", [128, TOK], F32, kind="ExternalOutput").ap()
        dbg["dbg_arg0"] = nc.dram_tensor("dbg_arg0", [128, TOK], F32, kind="ExternalOutput").ap()
        dbg["dbg_dbloc"] = nc.dram_tensor("dbg_dbloc", [H, TOK], BF16, kind="ExternalOutput").ap()

    cc_bc_in = nc.dram_tensor("cc_bc_in", [256, TOK], BF16)
    g_bc = nc.dram_tensor("g_bc", [4 * 256, TOK], BF16)
    cc_db_in = nc.dram_tensor("cc_db_in", [H, TOK], BF16)
    g_dbg = nc.dram_tensor("g_dbg", [4 * H, TOK], BF16)
    d_dbloc = nc.dram_tensor("d_dbloc", [H, TOK], BF16)
    cc_y_in = nc.dram_tensor("cc_y_in", [HL, LSEQ], F32)
    g_yg = nc.dram_tensor("g_yg", [4 * HL, LSEQ], F32)
    d_yfull = nc.dram_tensor("d_yfull", [D, TOK], F32)
    d_hbar = nc.dram_tensor("d_hbar", [DFF, TOK], BF16)
    d_w8 = nc.dram_tensor("d_w8", [NE, TOK], BF16)

    sbuf = nc.alloc_sbuf_tensor
    xn = sbuf("xn", [128, KD * TOK], F32R)
    wbc = sbuf("wbc", [128, NE * TOK], BF16)
    xapg = [sbuf(f"xapg{q}", [128, TOK], F32R) for q in range(2)]
    xapu = [sbuf(f"xapu{q}", [128, TOK], F32R) for q in range(2)]
    z_sb = sbuf("z_sb", [128, TOK], F32R)
    AT_s = sbuf("AT_s", [128, HL], F32)
    Dp_s = sbuf("Dp_s", [128, DT], F32)
    bgc_s = sbuf("bgc_s", [128, DT], F32)
    id_s = sbuf("id_s", [128, 128], F32)
    idr_s = sbuf("idr_s", [128, 128], F32R)
    idb_s = sbuf("idb_s", [128, 128], BF16)
    ones_s = sbuf("ones_s", [128, 1], F32R)
    onesb_s = sbuf("onesb_s", [128, 1], BF16)
    misc_s = sbuf("misc_s", [128, 6], F32)
    msk_s = sbuf("msk_s", [128, 4], F32)
    mct_s = sbuf("mct_s", [128, H], BF16)
    cols = sbuf("cols", [128, 64], F32)
    st8 = sbuf("st8", [128, HL], F32)
    AR = sbuf("AR", [128, 12288], F32)
    ARF = sbuf("ARF", [128, 9216], F32R)

    def ar(off, n, dt_=F32):
        v = AR[:, off:off + n]
        return v if dt_ is F32 else v.bitcast(dt_)

    def arf(off, n):
        return ARF[:, off:off + n]

    stack = ExitStack()
    psA0 = stack.enter_context(nc.psum_tensor("psA0", [128, 512], F32))
    psA1 = stack.enter_context(nc.psum_tensor("psA1", [128, 512], F32))
    psB0 = stack.enter_context(nc.psum_tensor("psB0", [128, 512], F32))
    psB1 = stack.enter_context(nc.psum_tensor("psB1", [128, 512], F32))
    psZ0 = stack.enter_context(nc.psum_tensor("psZ0", [128, 1024], F32))
    psZ1 = stack.enter_context(nc.psum_tensor("psZ1", [128, 1024], F32))

    for name, dst, cast in [
        ("AT_loc", AT_s, None), ("Dp_cols", Dp_s, None), ("bg_cols", bgc_s, None),
        ("id128", id_s, None), ("id128", idr_s, F32R), ("id128b", idb_s, None),
        ("ones_col", ones_s, F32R), ("ones_b", onesb_s, None),
        ("misc_c", misc_s, None), ("msk", msk_s, None), ("mctT", mct_s, None),
    ]:
        src = din[name] if cast is None else din[name].bitcast(cast)
        P.emit("gpsimd", lambda g, s, src=src, dst=dst: g.dma_start(out=dst[:], in_=src),
               inc="g1", inc_n=16)
    consts_gq = P.cnt["g1"]

    wx_s = arf(4096, KD * 288)
    v_wx = P.emit("scalar", lambda e, s: e.dma_start(
        out=wx_s.rearrange("p (k c) -> p k c", k=KD),
        in_=din["Wx"].bitcast(F32R).rearrange("(k p) c -> p k c", p=128)), inc="swx", inc_n=16)
    rw_s = arf(8704, KD * NE)
    v_rw = P.emit("scalar", lambda e, s: e.dma_start(
        out=rw_s.rearrange("p (k c) -> p k c", k=KD),
        in_=din["rwT"].bitcast(F32R).rearrange("(k p) c -> p k c", p=128)), inc="srw", inc_n=16)
    ag_s = arf(0, KD * 128)
    au_s = arf(2048, KD * 128)
    v_ag = P.emit("scalar", lambda e, s: e.dma_start(
        out=ag_s.rearrange("p (k c) -> p k c", k=KD),
        in_=din["AgT"].bitcast(F32R).rearrange("(k p) c -> p k c", p=128)), inc="sag", inc_n=16)
    v_au = P.emit("scalar", lambda e, s: e.dma_start(
        out=au_s.rearrange("p (k c) -> p k c", k=KD),
        in_=din["AuT"].bitcast(F32R).rearrange("(k p) c -> p k c", p=128)), inc="sag", inc_n=16)

    x_t = din["x_sh"].rearrange("(n p) d -> n p d", p=128)
    red_vv = [0, 0]
    sqr_ss = [0, 0]
    for tt in range(TSUB):
        xt = ar(2048 * (tt % 2), 2048)
        sq_ = ar(6144 + 2048 * (tt % 2), 2048)
        sxp = "sx0" if tt % 2 == 0 else "sx1"
        qe = "sync" if tt % 2 == 0 else "gpsimd"
        if tt >= 2:
            P.wait(qe, "vv", red_vv[tt % 2])
        v_x = P.emit(qe, lambda e, s, tt=tt, xt=xt: e.dma_start(out=xt, in_=x_t[tt]),
                     inc=sxp, inc_n=16)
        P.wait("scalar", sxp, v_x)
        if tt >= 2:
            P.wait("scalar", "vv", red_vv[tt % 2])
        sqr_ss[tt % 2] = P.emit("scalar", lambda e, s, xt=xt, sq_=sq_: e.activation(sq_, xt, AF.Square), inc="ss")
        P.wait("vector", sxp, v_x)
        P.emit("vector", lambda e, s, xt=xt, tt=tt: e.reduce_sum(cols[:, tt:tt + 1], xt, AX.X))
        P.wait("vector", "ss", sqr_ss[tt % 2])
        red_vv[tt % 2] = P.emit("vector", lambda e, s, sq_=sq_, tt=tt: e.reduce_sum(
            cols[:, 8 + tt:9 + tt], sq_, AX.X), inc="vv")
    stats_vv = P.cnt["vv"]
    P.wait("scalar", "vv", stats_vv)
    P.wait("scalar", "g1", consts_gq)
    for tt in range(TSUB):
        P.emit("scalar", lambda e, s, tt=tt: e.activation(
            cols[:, 16 + tt:17 + tt], cols[:, tt:tt + 1], AF.Copy, scale=misc_s[:, 5:6]))
    sm_ss = P.emit("scalar", lambda e, s: e.activation(
        cols[:, 24:25], cols[:, 8:9], AF.Copy, scale=misc_s[:, 5:6]), inc="ss")
    for tt in range(1, TSUB):
        sm_ss = P.emit("scalar", lambda e, s, tt=tt: e.activation(
            cols[:, 24 + tt:25 + tt], cols[:, 8 + tt:9 + tt], AF.Copy, scale=misc_s[:, 5:6]), inc="ss")
    P.wait("vector", "ss", sm_ss)
    q_vv = 0
    for tt in range(TSUB):
        q_vv = P.emit("vector", lambda e, s, tt=tt: e.scalar_tensor_tensor(
            cols[:, 32 + tt:33 + tt], cols[:, 16 + tt:17 + tt], cols[:, 16 + tt:17 + tt],
            cols[:, 24 + tt:25 + tt], ALU.mult, ALU.subtract), inc="vv")
    P.wait("scalar", "vv", q_vv)
    for tt in range(TSUB):
        P.emit("scalar", lambda e, s, tt=tt: e.activation(
            cols[:, 40 + tt:41 + tt], cols[:, 32 + tt:33 + tt], AF.Ln,
            scale=misc_s[:, 4:5], bias=misc_s[:, 0:1]))
    rstd_ss = 0
    for tt in range(TSUB):
        rstd_ss = P.emit("scalar", lambda e, s, tt=tt: e.activation(
            cols[:, 48 + tt:49 + tt], cols[:, 40 + tt:41 + tt], AF.Exp, scale=misc_s[:, 1:2]), inc="ss")
    xt_last_vv = [0, 0]
    xnt_pe = [0, 0]
    tp_banks = [psA0, psA1, psB0, psB1]
    tp_ev_bank = [0, 0, 0, 0]
    for tt in range(TSUB):
        xt = ar(2048 * (tt % 2), 2048)
        xnt = ar(4096 + 2048 * (tt % 2), 2048)
        sxp = "sx0" if tt % 2 == 0 else "sx1"
        qe = "sync" if tt % 2 == 0 else "gpsimd"
        if tt < 2:
            P.wait(qe, "vv", stats_vv)
        else:
            P.wait(qe, "vv", xt_last_vv[tt % 2])
        v_x = P.emit(qe, lambda e, s, tt=tt, xt=xt: e.dma_start(out=xt, in_=x_t[tt]),
                     inc=sxp, inc_n=16)
        P.wait("vector", sxp, v_x)
        if tt == 0:
            P.wait("vector", "ss", rstd_ss)
        P.wait("vector", "pe", xnt_pe[tt % 2])
        vv2 = P.emit("vector", lambda e, s, xt=xt, xnt=xnt, tt=tt: e.tensor_scalar(
            xnt, xt, cols[:, 16 + tt:17 + tt], cols[:, 48 + tt:49 + tt],
            ALU.subtract, ALU.mult), inc="vv")
        xt_last_vv[tt % 2] = vv2
        if tt == 0:
            P.wait("tensor", "g1", consts_gq)
        P.wait("tensor", "vv", vv2)
        for j in range(DT):
            pst = tp_banks[j % 4][:, 0:128]
            P.wait("tensor", "ss", tp_ev_bank[j % 4])
            pv = P.emit("tensor", lambda e, s, xnt=xnt, j=j, pst=pst: e.transpose(
                pst, xnt[:, 128 * j:128 * (j + 1)], id_s[:]), inc="pe")
            P.wait("scalar", "pe", pv)
            dst = xn[:, j * TOK + tt * 128: j * TOK + tt * 128 + 128]
            tp_ev_bank[j % 4] = P.emit("scalar", lambda e, s, dst=dst, pst=pst: e.activation(dst, pst, AF.Copy), inc="ss")
        xnt_pe[tt % 2] = P.cnt["pe"]
    xn_done_ss = P.cnt["ss"]
    xn_done_vv = P.cnt["vv"]

    if DEBUG:
        P.wait("sync", "ss", xn_done_ss)
        P.emit("sync", lambda e, s: e.dma_start(out=dbg["dbg_cols"], in_=cols[:]), inc="sq", inc_n=16)
        for j in range(DT):
            P.emit("sync", lambda e, s, j=j: e.dma_start(
                out=dbg["dbg_xn"][128 * j:128 * (j + 1), :],
                in_=xn[:, j * TOK:(j + 1) * TOK].bitcast(F32)), inc="sq", inc_n=16)

    P.wait("tensor", "swx", v_wx)
    P.wait("tensor", "ss", xn_done_ss)
    st2_gq = {}
    for tt in range(TSUB):
        psm = psA0[:, 0:288]
        for k in range(KD):
            lh = xn[:, k * TOK + tt * 128: k * TOK + tt * 128 + 128]
            rh = wx_s[:, k * 288:(k + 1) * 288]
            P.emit("tensor", lambda e, s, psm=psm, lh=lh, rh=rh, k=k: e.matmul(
                psm, lh, rh, start=(k == 0), stop=(k == KD - 1)),
                inc="pe" if k == KD - 1 else None)
        pv = P.cnt["pe"]
        stage = ar(4608 + (tt % 2) * 160, 144, BF16)
        P.wait("scalar", "pe", pv)
        if tt == 0:
            P.wait("scalar", "vv", xn_done_vv)
        sv = P.emit("scalar", lambda e, s, stage=stage, psm=psm: e.activation(stage, psm, AF.Copy), inc="ss")
        P.wait("tensor", "ss", sv)
        for ci in range(3):
            w = 128 if ci < 2 else 32
            pst = (psB0 if ci % 2 == 0 else psB1)[:, 0:128].bitcast(BF16)[:, 0:128]
            src = stage[:, 128 * ci:128 * ci + w]
            pv2 = P.emit("tensor", lambda e, s, pst=pst, src=src, w=w: e.transpose(
                pst[0:w, :] if w < 128 else pst, src, idb_s[:]), inc="pe")
            st2 = ar(5000 + tt * 192 + ci * 64, 64, BF16)
            P.wait("vector", "pe", pv2)
            ev = P.emit("vector", lambda e, s, st2=st2, pst=pst, w=w: e.tensor_copy(
                st2[0:w, :], pst[0:w, :]), inc="vv")
            P.wait("tensor", "vv", ev)
            P.wait("gpsimd", "vv", ev)
            if ci < 2:
                P.emit("gpsimd", lambda e, s, st2=st2, ci=ci, tt=tt: e.dma_start(
                    out=cc_bc_in.ap()[128 * ci:128 * (ci + 1), 128 * tt:128 * (tt + 1)],
                    in_=st2[0:128, :]), inc="gq", inc_n=16)
            else:
                P.emit("gpsimd", lambda e, s, st2=st2, tt=tt: e.dma_start(
                    out=cc_db_in.ap()[:, 128 * tt:128 * (tt + 1)],
                    in_=st2[0:32, :]), inc="gq", inc_n=16)
    ssm_gq = P.cnt["gq"]

    groups = [[0, 1, 2, 3], [4, 5, 6, 7]]
    P.wait("gpsimd", "gq", ssm_gq)
    P.emit("gpsimd", lambda e, s: e.collective_compute(
        "AllGather", ALU.bypass, replica_groups=groups,
        ins=[cc_bc_in.ap().opt()], outs=[g_bc.ap().opt()]), inc="cc")
    P.emit("gpsimd", lambda e, s: e.collective_compute(
        "AllGather", ALU.bypass, replica_groups=groups,
        ins=[cc_db_in.ap().opt()], outs=[g_dbg.ap().opt()]), inc="cc")
    cc_ssm_done = P.cnt["cc"]

    P.wait("tensor", "srw", v_rw)
    rt = ar(6600, 64)
    re_ = ar(6664, 64)
    rm = ar(6728, 64)
    wt_all = ar(6792, 64)
    stg = ar(6856, 64, BF16)
    lg_ss = 0
    for tt in range(TSUB):
        psm = psA1[:, 0:NE]
        P.wait("tensor", "ss", lg_ss)
        for k in range(KD):
            lh = xn[:, k * TOK + tt * 128: k * TOK + tt * 128 + 128]
            rh = rw_s[:, k * NE:(k + 1) * NE]
            P.emit("tensor", lambda e, s, psm=psm, lh=lh, rh=rh, k=k: e.matmul(
                psm, lh, rh, start=(k == 0), stop=(k == KD - 1)),
                inc="pe" if k == KD - 1 else None)
        pv = P.cnt["pe"]
        P.wait("scalar", "pe", pv)
        lg_ss = P.emit("scalar", lambda e, s, psm=psm, tt=tt: e.activation(
            rt[:, 8 * tt:8 * tt + 8], psm, AF.Copy), inc="ss")
    P.wait("vector", "ss", lg_ss)
    for tt in range(TSUB):
        P.emit("vector", lambda e, s, tt=tt: e.reduce_max(cols[:, tt:tt + 1], rt[:, 8 * tt:8 * tt + 8], AX.X))
    for tt in range(TSUB):
        P.emit("vector", lambda e, s, tt=tt: e.reduce_max(cols[:, 8 + tt:9 + tt], rt[:, 8 * tt:8 * tt + 8], AX.X, negate=True))
    for tt in range(TSUB):
        P.emit("vector", lambda e, s, tt=tt: e.tensor_scalar(
            re_[:, 8 * tt:8 * tt + 8], rt[:, 8 * tt:8 * tt + 8], cols[:, tt:tt + 1], None, ALU.is_equal))
    for tt in range(TSUB):
        P.emit("vector", lambda e, s, tt=tt: e.scalar_tensor_tensor(
            rm[:, 8 * tt:8 * tt + 8], re_[:, 8 * tt:8 * tt + 8], -1e30, rt[:, 8 * tt:8 * tt + 8],
            ALU.mult, ALU.add))
    m2_vv = 0
    for tt in range(TSUB):
        m2_vv = P.emit("vector", lambda e, s, tt=tt: e.reduce_max(
            cols[:, 16 + tt:17 + tt], rm[:, 8 * tt:8 * tt + 8], AX.X), inc="vv")
    P.wait("scalar", "vv", m2_vv)
    ew_ss = 0
    for tt in range(TSUB):
        ew_ss = P.emit("scalar", lambda e, s, tt=tt: e.activation(
            re_[:, 8 * tt:8 * tt + 8], rt[:, 8 * tt:8 * tt + 8], AF.Exp, bias=cols[:, 8 + tt:9 + tt]), inc="ss")
    P.wait("vector", "ss", ew_ss)
    for tt in range(TSUB):
        P.emit("vector", lambda e, s, tt=tt: e.tensor_scalar(
            rm[:, 8 * tt:8 * tt + 8], rt[:, 8 * tt:8 * tt + 8], cols[:, 16 + tt:17 + tt], None, ALU.is_ge))
    for tt in range(TSUB):
        P.emit("vector", lambda e, s, tt=tt: e.tensor_tensor(
            wt_all[:, 8 * tt:8 * tt + 8], rm[:, 8 * tt:8 * tt + 8], re_[:, 8 * tt:8 * tt + 8], ALU.mult))
    for tt in range(TSUB):
        P.emit("vector", lambda e, s, tt=tt: e.reduce_sum(
            cols[:, 24 + tt:25 + tt], wt_all[:, 8 * tt:8 * tt + 8], AX.X))
    for tt in range(TSUB):
        P.emit("vector", lambda e, s, tt=tt: e.reciprocal(cols[:, 32 + tt:33 + tt], cols[:, 24 + tt:25 + tt]))
    w_vv = 0
    for tt in range(TSUB):
        w_vv = P.emit("vector", lambda e, s, tt=tt: e.tensor_scalar_mul(
            wt_all[:, 8 * tt:8 * tt + 8], wt_all[:, 8 * tt:8 * tt + 8], cols[:, 32 + tt:33 + tt]), inc="vv")
    if DEBUG:
        P.wait("sync", "vv", w_vv)
        for tt in range(TSUB):
            P.emit("sync", lambda e, s, tt=tt: e.dma_start(
                out=dbg["dbg_w"][128 * tt:128 * (tt + 1), :], in_=wt_all[:, 8 * tt:8 * tt + 8]),
                inc="sq", inc_n=16)
    P.wait("tensor", "vv", w_vv)
    wtr_ev = 0
    w8dma = 0
    for tt in range(TSUB):
        pst = psB0[:, 0:128]
        P.wait("tensor", "vv", wtr_ev)
        pv2 = P.emit("tensor", lambda e, s, pst=pst, tt=tt: e.transpose(
            pst[0:8, :], wt_all[:, 8 * tt:8 * tt + 8], id_s[:]), inc="pe")
        P.wait("vector", "pe", pv2)
        P.wait("vector", "gq", P.cnt["gq"])
        wtr_ev = P.emit("vector", lambda e, s, pst=pst: e.tensor_copy(stg[0:8, 0:128], pst[0:8, :]), inc="vv")
        P.wait("gpsimd", "vv", wtr_ev)
        w8dma = P.emit("gpsimd", lambda e, s, tt=tt: e.dma_start(
            out=d_w8.ap()[:, 128 * tt:128 * (tt + 1)], in_=stg[0:8, 0:128]), inc="gq", inc_n=16)
    w8_gq = P.cnt["gq"]
    P.wait("sync", "gq", w8_gq)
    for e_ in range(NE):
        P.emit("sync", lambda e, s, e_=e_: e.dma_start(
            out=wbc[:, e_ * TOK:(e_ + 1) * TOK],
            in_=d_w8.ap()[e_:e_ + 1, :].partition_broadcast(128)), inc="s4", inc_n=16)
    wbc_sq = P.cnt["s4"]

    P.wait("tensor", "sag", v_au)
    for src, psm in ((ag_s, psZ0), (au_s, psZ1)):
        for ts in range(TS):
            for k in range(KD):
                lh = src[:, k * 128:(k + 1) * 128]
                rh = xn[:, k * TOK + ts * 512: k * TOK + ts * 512 + 512]
                P.emit("tensor", lambda e, s, psm=psm, lh=lh, rh=rh, k=k, ts=ts: e.matmul(
                    psm[:, ts * 512:(ts + 1) * 512], lh, rh,
                    start=(k == 0), stop=(k == KD - 1)),
                    inc="pe" if k == KD - 1 else None)
    xa_pe = P.cnt["pe"]
    xq_pg = ar(0, TOK, F32R)
    xq_ng = ar(1024, TOK, F32R)
    xq_pu = ar(2048, TOK, F32R)
    xq_nu = ar(3072, TOK, F32R)
    P.wait("scalar", "pe", xa_pe)
    P.emit("scalar", lambda e, s: e.activation(xq_pg, psZ0[:], AF.Copy, scale=misc_s[:, 2:3]), inc="ss")
    P.emit("scalar", lambda e, s: e.activation(xq_ng, psZ0[:], AF.Copy, scale=misc_s[:, 3:4]), inc="ss")
    P.emit("scalar", lambda e, s: e.activation(xq_pu, psZ1[:], AF.Copy, scale=misc_s[:, 2:3]), inc="ss")
    s4 = P.emit("scalar", lambda e, s: e.activation(xq_nu, psZ1[:], AF.Copy, scale=misc_s[:, 3:4]), inc="ss")
    P.wait("gpsimd", "ss", s4)
    for e_ in range(NE):
        q, r = divmod(e_, 4)
        for pos, neg, dsts in ((xq_pg, xq_ng, xapg), (xq_pu, xq_nu, xapu)):
            P.emit("gpsimd", lambda e, s, dsts=dsts, q=q, r=r, pos=pos, e_=e_: e.dma_start(
                out=dsts[q][32 * r:32 * r + 16, :], in_=pos[16 * e_:16 * e_ + 16, :]), inc="g2", inc_n=16)
            if e_ == 0:
                P.emit("gpsimd", lambda e, s, dsts=dsts, q=q, r=r: e.dma_start(
                    out=dsts[q][32 * r + 16:32 * r + 32, :],
                    in_=din["zeros128"].bitcast(F32R)[0:16, :]), inc="g2", inc_n=16)
            else:
                P.emit("gpsimd", lambda e, s, dsts=dsts, q=q, r=r, neg=neg, ep=e_ - 1: e.dma_start(
                    out=dsts[q][32 * r + 16:32 * r + 32, :], in_=neg[16 * ep:16 * ep + 16, :]), inc="g2", inc_n=16)
    xap_gq = P.cnt["g2"]

    bc_v = g_bc.ap().rearrange("(r c) t -> r c t", r=4)
    db_v = d_dbloc.ap().rearrange("(r h) t -> r h t", r=4)
    P.wait("sync", "g2", xap_gq)
    P.wait("sync", "cc", cc_ssm_done)
    gd_s = ar(8704, 512, BF16)
    v_gd = P.emit("sync", lambda e, s: e.dma_start(out=gd_s, in_=g_dbg.ap()), inc="s2", inc_n=16)
    P.wait("tensor", "s2", v_gd)
    P.wait("tensor", "ss", P.cnt["ss"])
    P.wait("tensor", "vv", P.cnt["vv"])
    if DEBUG:
        P.wait("vector", "sq", v_gd)
        gdump = P.emit("vector", lambda e, s: e.tensor_copy(ar(2048, 1024), gd_s), inc="vv")
        P.wait("sync", "vv", gdump)
        gd_dsq = P.emit("sync", lambda e, s: e.dma_start(out=dbg["dbg_gds"], in_=ar(2048, 1024)), inc="sq", inc_n=16)
        P.wait("vector", "sq", gd_dsq)
    dbsel_ss = 0
    for q4 in range(2):
        psd = psZ1[0:32, 512 * q4:512 * (q4 + 1)]
        pv = P.emit("tensor", lambda e, s, psd=psd, q4=q4: e.matmul(
            psd, mct_s[:], gd_s[:, 512 * q4:512 * (q4 + 1)], start=True, stop=True), inc="pe")
        P.wait("scalar", "pe", pv)
        dbsel_ss = P.emit("scalar", lambda e, s, psd=psd, q4=q4: e.activation(
            ar(9216, 512, BF16)[0:32, 512 * q4:512 * (q4 + 1)], psd, AF.Copy), inc="ss")
    P.wait("sync", "ss", dbsel_ss)
    v_dbl = P.emit("sync", lambda e, s: e.dma_start(
        out=d_dbloc.ap(), in_=ar(9216, 512, BF16)[0:32, :]), inc="s2", inc_n=16)
    P.wait("sync", "s2", v_dbl)
    if DEBUG:
        P.emit("sync", lambda e, s: e.dma_start(out=dbg["dbg_dbloc"], in_=d_dbloc.ap()), inc="sq", inc_n=16)
    y_gq = [0]
    db_last_vv = [0, 0]
    av_free_ss = [0, 0]
    scanU_prev = [0]
    yred_prev = [0, 0]
    ydma_prev = [0, 0]
    for qt in range(4):
        Bb = ar(0, 512, BF16)
        Cb = ar(512, 512, BF16)
        if qt >= 1:
            P.wait("sync", "vv", P.cnt["vv"])
        v_b = P.emit("sync", lambda e, s, Bb=Bb, qt=qt: e.dma_start(
            out=Bb, in_=bc_v[qt, 0:128, :]), inc="sB", inc_n=16)
        v_c = P.emit("sync", lambda e, s, Cb=Cb, qt=qt: e.dma_start(
            out=Cb, in_=bc_v[qt, 128:256, :]), inc="sB", inc_n=16)
        for il in range(HL):
            dbb = ar(1024 + (il % 2) * 512, 512, BF16)
            if il >= 2:
                P.wait("sync", "vv", db_last_vv[il % 2])
            sdp = "sd0" if il % 2 == 0 else "sd1"
            v_d = P.emit("sync", lambda e, s, dbb=dbb, qt=qt, il=il: e.dma_start(
                out=dbb.rearrange("p (a t) -> p a t", a=1),
                in_=db_v[qt, il:il + 1, :].partition_broadcast(128)),
                inc=sdp, inc_n=16)
            gp = (qt * HL + il) % 2
            arg = ar(2048, 1024) if gp == 0 else ar(9728, 1024)
            tmp = ar(3072, 1024)
            av = ar(4096, 1024)
            b2 = ar(5120, 1024)
            hs = ar(6144, 1024)
            Ub = ar(7168, 512, BF16) if gp == 0 else ar(10752, 512, BF16)
            P.wait("vector", sdp, v_d)
            P.wait("vector", "sB", v_c)
            P.wait("vector", "ss", av_free_ss[gp])
            vv1 = P.emit("vector", lambda e, s, arg=arg, Bb=Bb, dbb=dbb: e.tensor_tensor(
                arg, Bb, dbb, ALU.add), inc="vv")
            db_last_vv[il % 2] = vv1
            P.wait("scalar", "vv", vv1)
            P.emit("scalar", lambda e, s, tmp=tmp, arg=arg: e.activation(tmp, arg, AF.Exp))
            ln_ss = P.emit("scalar", lambda e, s, arg=arg, tmp=tmp: e.activation(
                arg, tmp, AF.Ln, bias=1.0), inc="ss")
            P.wait("scalar", "vv", scanU_prev[0])
            av_free_ss[gp] = P.emit("scalar", lambda e, s, av=av, arg=arg, il=il: e.activation(
                av, arg, AF.Exp, scale=AT_s[:, il:il + 1]), inc="ss")
            P.wait("vector", "ss", ln_ss)
            P.emit("vector", lambda e, s, b2=b2, arg=arg: e.tensor_tensor(
                b2, arg, arg, ALU.mult))
            P.wait("vector", "ss", av_free_ss[gp])
            if qt == 0:
                vv2 = P.emit("vector", lambda e, s, hs=hs, av=av, b2=b2: e.tensor_tensor_scan(
                    hs, av, b2, 0.0, ALU.mult, ALU.add), inc="vv")
            else:
                vv2 = P.emit("vector", lambda e, s, hs=hs, av=av, b2=b2, il=il: e.tensor_tensor_scan(
                    hs, av, b2, st8[:, il:il + 1], ALU.mult, ALU.add), inc="vv")
            scanU_prev[0] = vv2
            P.wait("vector", "pe", yred_prev[gp])
            vv3 = P.emit("vector", lambda e, s, Ub=Ub, Cb=Cb, hs=hs: e.tensor_tensor(
                Ub, Cb, hs, ALU.mult), inc="vv")
            if qt < 3:
                P.emit("vector", lambda e, s, hs=hs, il=il: e.tensor_copy(
                    st8[:, il:il + 1], hs[:, TOK - 1:TOK]))
            if DEBUG and qt == 0 and il == 0:
                dvv0 = P.emit("vector", lambda e, s, arg=arg: e.tensor_copy(ar(11264, 1024), arg), inc="vv")
                P.wait("sync", "vv", dvv0)
                d0sq = P.emit("sync", lambda e, s: e.dma_start(out=dbg["dbg_arg0"], in_=ar(11264, 1024)), inc="sq", inc_n=16)
                P.wait("vector", "sq", d0sq)
            if DEBUG and qt == 0 and il == 1:
                P.wait("sync", "vv", vv3)
                for nm, buf in (("dbg_av", av), ("dbg_b2", b2), ("dbg_hs", hs), ("dbg_arg", arg)):
                    P.emit("sync", lambda e, s, nm=nm, buf=buf: e.dma_start(
                        out=dbg[nm], in_=buf), inc="sq", inc_n=16)
                dstg = ar(10240, 1024)
                dvv = P.emit("vector", lambda e, s, Bb=Bb, dstg=dstg: e.tensor_copy(dstg, Bb), inc="vv")
                P.wait("sync", "vv", dvv)
                dsq = P.emit("sync", lambda e, s, dstg=dstg: e.dma_start(out=dbg["dbg_B"], in_=dstg), inc="sq", inc_n=16)
                P.wait("vector", "sq", dsq)
                dvv = P.emit("vector", lambda e, s, dbb=dbb, dstg=dstg: e.tensor_copy(dstg, dbb), inc="vv")
                P.wait("sync", "vv", dvv)
                dsq = P.emit("sync", lambda e, s, dstg=dstg: e.dma_start(out=dbg["dbg_db"], in_=dstg), inc="sq", inc_n=16)
                P.wait("vector", "sq", dsq)
                dvv = P.emit("vector", lambda e, s, dstg=dstg: e.tensor_copy(dstg[0:32, :], ar(9216, 512, BF16)[0:32, :]), inc="vv")
                P.wait("sync", "vv", dvv)
                dsq = P.emit("sync", lambda e, s, dstg=dstg: e.dma_start(out=dbg["dbg_gdb"], in_=dstg[0:32, :]), inc="sq", inc_n=16)
                P.wait("vector", "sq", dsq)
            P.wait("tensor", "vv", vv3)
            sv2 = 0
            ypar = (qt * HL + il) % 2
            for q2 in range(2):
                psm = psZ1[0:1, 512 * q2:512 * (q2 + 1)]
                pv = P.emit("tensor", lambda e, s, psm=psm, Ub=Ub, q2=q2: e.matmul(
                    psm, onesb_s[:], Ub[:, 512 * q2:512 * (q2 + 1)], start=True, stop=True), inc="pe")
                yred_prev[gp] = pv
                P.wait("scalar", "pe", pv)
                P.wait("scalar", "gq", P.cnt["gq"])
                sv2 = P.emit("scalar", lambda e, s, psm=psm, q2=q2, ypar=ypar: e.activation(
                    ar(7680 + 1024 * ypar + 512 * q2, 512)[0:1, :], psm, AF.Copy), inc="ss")
                P.wait("tensor", "ss", sv2)
            P.wait("gpsimd", "ss", sv2)
            for q2 in range(2):
                col0 = qt * 1024 + 512 * q2
                y_gq[0] = P.emit("gpsimd", lambda e, s, il=il, col0=col0, q2=q2, ypar=ypar: e.dma_start(
                    out=cc_y_in.ap()[il:il + 1, col0:col0 + 512],
                    in_=ar(7680 + 1024 * ypar + 512 * q2, 512)[0:1, :]), inc="gq", inc_n=16)
            ydma_prev[ypar] = y_gq[0]
    P.wait("gpsimd", "gq", y_gq[0])
    P.emit("gpsimd", lambda e, s: e.collective_compute(
        "AllGather", ALU.bypass, replica_groups=groups,
        ins=[cc_y_in.ap().opt()], outs=[g_yg.ap().opt()]), inc="cc")
    cc_y_done = P.cnt["cc"]

    yT = arf(0, 1024)[0:32, :]
    E_s = arf(5120, 2048)[0:H, :]
    awg_v = din["AWg"].bitcast(F32R)
    ld_awg = [0] * DT
    P.wait("sync", "pe", xa_pe)
    v_E = P.emit("sync", lambda e, s: e.dma_start(out=E_s, in_=din["E_mat"].bitcast(F32R)), inc="s3", inc_n=16)
    ld_awg[0] = P.emit("sync", lambda e, s: e.dma_start(
        out=arf(1024, 2048), in_=awg_v[0]), inc="sa0", inc_n=16)
    ld_awg[1] = P.emit("sync", lambda e, s: e.dma_start(
        out=arf(3072, 2048), in_=awg_v[1]), inc="sa1", inc_n=16)
    P.wait("sync", "cc", cc_y_done)
    P.wait("sync", "vv", P.cnt["vv"])
    P.wait("sync", "ss", P.cnt["ss"])
    yg_s = ar(0, 4096)[0:32, :]
    yt0 = ar(4096, 1024)[0:32, :]
    yt1 = ar(5120, 1024)[0:32, :]
    v_yg = P.emit("sync", lambda e, s: e.dma_start(out=yg_s, in_=g_yg.ap()), inc="s3", inc_n=16)
    P.wait("vector", "s3", v_yg)
    P.emit("vector", lambda e, s: e.tensor_scalar_mul(yt0, yg_s[:, 0:1024], msk_s[0:32, 0:1]))
    P.emit("vector", lambda e, s: e.scalar_tensor_tensor(
        yt1, yg_s[:, 1024:2048], msk_s[0:32, 1:2], yt0, ALU.mult, ALU.add))
    P.emit("vector", lambda e, s: e.scalar_tensor_tensor(
        yt0, yg_s[:, 2048:3072], msk_s[0:32, 2:3], yt1, ALU.mult, ALU.add))
    ysel_vv = P.emit("vector", lambda e, s: e.scalar_tensor_tensor(
        yt1, yg_s[:, 3072:4096], msk_s[0:32, 3:4], yt0, ALU.mult, ALU.add), inc="vv")
    P.wait("scalar", "vv", ysel_vv)
    ysel_ss = P.emit("scalar", lambda e, s: e.activation(yT, yt1, AF.Copy), inc="ss")
    if DEBUG:
        P.wait("sync", "vv", ysel_vv)
        P.emit("sync", lambda e, s: e.dma_start(out=dbg["dbg_y"], in_=g_yg.ap()[0:32, 0:TOK]), inc="sq", inc_n=16)
    p3_vv = {0: P.cnt["vv"], 1: P.cnt["vv"]}
    p3_ss = {0: P.cnt["ss"], 1: P.cnt["ss"]}
    aw_pe = [0, 0]
    yf_sq = [0]
    yf_sq2 = {0: 0, 1: 0}
    for j in range(DT):
        aw = arf(1024 + (j % 2) * 2048, 2048)
        if 2 <= j + 1 < DT:
            if j >= 1:
                P.wait("sync", "pe", aw_pe[(j + 1) % 2])
            ld_awg[j + 1] = P.emit("sync", lambda e, s, j=j: e.dma_start(
                out=arf(1024 + ((j + 1) % 2) * 2048, 2048), in_=awg_v[j + 1]),
                inc=("sa0" if (j + 1) % 2 == 0 else "sa1"), inc_n=16)
        for ts in range(TS):
            it = (j * TS + ts) % 2
            psb = psZ1 if it == 0 else psZ0
            psy = psb[:, 0:512]
            psg_ = psb[:, 512:1024]
            P.wait("tensor", "ss", ysel_ss)
            P.wait("tensor", "s3", P.cnt["s3"])
            P.wait("tensor", "sa0" if j % 2 == 0 else "sa1", ld_awg[j])
            P.wait("tensor", "vv", p3_vv[it])
            P.wait("tensor", "ss", p3_ss[it])
            pv_y = P.emit("tensor", lambda e, s, psy=psy, j=j, ts=ts: e.matmul(
                psy, E_s[:, 128 * j:128 * (j + 1)], yT[:, 512 * ts:512 * (ts + 1)],
                start=True, stop=True), inc="pe")
            for k in range(KD):
                lh = aw[:, k * 128:(k + 1) * 128]
                rh = xn[:, k * TOK + ts * 512: k * TOK + ts * 512 + 512]
                P.emit("tensor", lambda e, s, psg_=psg_, lh=lh, rh=rh, k=k: e.matmul(
                    psg_, lh, rh, start=(k == 0), stop=(k == KD - 1)),
                    inc="pe" if k == KD - 1 else None)
            pv_g = P.cnt["pe"]
            if ts == 1:
                aw_pe[j % 2] = pv_g
            yh = ar(6144, 512)
            gt = ar(6656, 512)
            yf = ar(7168 + it * 512, 512)
            P.wait("vector", "pe", pv_y)
            P.wait("vector", "so", yf_sq2[it])
            p3_vv[it] = P.emit("vector", lambda e, s, yh=yh, psy=psy, j=j, ts=ts: e.scalar_tensor_tensor(
                yh, xn.bitcast(F32)[:, j * TOK + ts * 512: j * TOK + ts * 512 + 512],
                Dp_s[:, j:j + 1], psy, ALU.mult, ALU.add), inc="vv")
            P.wait("scalar", "pe", pv_g)
            p3_ss[it] = P.emit("scalar", lambda e, s, gt=gt, psg_=psg_, j=j: e.activation(
                gt, psg_, AF.Sigmoid, bias=bgc_s[:, j:j + 1]), inc="ss")
            P.wait("vector", "ss", p3_ss[it])
            vv2 = P.emit("vector", lambda e, s, yf=yf, yh=yh, gt=gt: e.tensor_tensor(
                yf, yh, gt, ALU.mult), inc="vv")
            P.wait("scalar", "vv", vv2)
            yf_sq[0] = P.emit("scalar", lambda e, s, yf=yf, j=j, ts=ts: e.dma_start(
                out=d_yfull.ap()[128 * j:128 * (j + 1), 512 * ts:512 * (ts + 1)],
                in_=yf), inc="so", inc_n=16)
            yf_sq2[it] = yf_sq[0]
    if DEBUG:
        P.wait("sync", "so", yf_sq[0])
        P.emit("sync", lambda e, s: e.dma_start(out=dbg["dbg_yfull"], in_=d_yfull.ap()), inc="sq", inc_n=16)

    def wg_blk(par): return arf(0 + par * 2048, 2048)
    def wu_blk(par): return arf(4096 + par * 2048, 2048)
    def bgp_t(par): return arf(8192 + par * 256, 256)
    def bup_t(par): return arf(8704 + par * 256, 256)
    def sg_b(X, epar): return ar(0 + (2 * X + epar) * 256, 256, BF16)
    def wu_b(X, epar): return ar(1024 + (2 * X + epar) * 256, 256, BF16)
    def hp_b(X, epar): return ar(2048 + (2 * X + epar) * 256, 256, BF16)
    def hb_b(par): return ar(3072 + par * 512, 512, BF16)
    def adt_t(par): return ar(4096 + par * 1024, 1024)
    def adt_bf(par): return ar(6144 + par * 512, 512, BF16)

    wgt_v = din["WgT"].bitcast(F32R)
    wut_v = din["WuT"].bitcast(F32R)
    bgp_v = din["BgTp"].bitcast(F32R).rearrange("(q p) f -> p q f", p=128)
    bup_v = din["BuTp"].bitcast(F32R).rearrange("(q p) f -> p q f", p=128)
    adt_v = din["AdT3"]

    P.wait("sync", "pe", P.cnt["pe"])
    P.wait("sync", "vv", P.cnt["vv"])
    P.wait("sync", "ss", P.cnt["ss"])
    P.wait("gpsimd", "pe", P.cnt["pe"])

    p4_ld = [0] * (FT + 2)
    p4_ldu = [0] * (FT + 2)
    p4_half_pe = [0, 0]
    p4_blk_pe = [0, 0]
    adt_vv = [0, 0]
    z_pe_all = [0, 0]

    def p4_wg_load(ft):
        par = ft % 2
        spp = "sp0" if par == 0 else "sp1"
        P.wait("sync", "pe", p4_half_pe[par])
        P.emit("sync", lambda e, s, dst=wg_blk(par), ft=ft: e.dma_start(
            out=dst, in_=wgt_v[ft]), inc=spp, inc_n=16)

    def p4_wu_load(ft):
        par = ft % 2
        upp = "up0" if par == 0 else "up1"
        P.wait("gpsimd", "pe", p4_half_pe[par])
        P.emit("gpsimd", lambda e, s, dst=wu_blk(par), ft=ft: e.dma_start(
            out=dst, in_=wut_v[ft]), inc=upp, inc_n=16)

    def p4_bp_loads(ft):
        par = ft % 2
        spp = "sp0" if par == 0 else "sp1"
        upp = "up0" if par == 0 else "up1"
        P.wait("sync", "pe", p4_blk_pe[par])
        P.emit("sync", lambda e, s, dst=bgp_t(par), ft=ft: e.dma_start(
            out=dst.rearrange("p (q c) -> p q c", q=2),
            in_=bgp_v[:, :, 128 * ft:128 * (ft + 1)]), inc=spp, inc_n=16)
        P.wait("gpsimd", "pe", p4_blk_pe[par])
        P.emit("gpsimd", lambda e, s, dst=bup_t(par), ft=ft: e.dma_start(
            out=dst.rearrange("p (q c) -> p q c", q=2),
            in_=bup_v[:, :, 128 * ft:128 * (ft + 1)]), inc=upp, inc_n=16)
        return P.cnt[spp], P.cnt[upp]

    def p4_adt(ft):
        par = ft % 2
        adp = "ad0" if par == 0 else "ad1"
        P.wait("scalar", "vv", adt_vv[par])
        v_ad = P.emit("scalar", lambda e, s, ft=ft, par=par: e.dma_start(
            out=adt_t(par), in_=adt_v[ft]), inc=adp, inc_n=16)
        P.wait("vector", adp, v_ad)
        P.wait("vector", "pe", z_pe_all[par])
        adt_vv[par] = P.emit("vector", lambda e, s, par=par: e.tensor_copy(
            adt_bf(par), adt_t(par)), inc="vv")

    sg_done = [P.cnt["ss"], P.cnt["ss"]]
    wu_done = [P.cnt["vv"], P.cnt["vv"]]
    last_sca_read = [P.cnt["ss"], P.cnt["ss"]]
    last_vv_read = [P.cnt["vv"], P.cnt["vv"]]
    hprime_vv = [[0, 0], [0, 0]]
    z_pe = [[0, 0], [0, 0]]
    hb_dma = [0, 0]
    first_wu = [True]
    first_diff = [True]

    def b_chunk(X, ft, chunk):
        par = ft % 2
        psg2 = psA0 if X == 0 else psB0
        psu2 = psA1 if X == 0 else psB1
        if chunk == 0:
            P.wait("tensor", "sp0" if par == 0 else "sp1", p4_ld[ft])
            P.wait("tensor", "up0" if par == 0 else "up1", p4_ldu[ft])
            P.wait("tensor", "ss", last_sca_read[X])
            P.wait("tensor", "vv", last_vv_read[X])
        for k in (2 * chunk, 2 * chunk + 1):
            rh = xn[:, k * TOK + X * 512: k * TOK + X * 512 + 512]
            lg = wg_blk(par)[:, k * 128:(k + 1) * 128]
            lu = wu_blk(par)[:, k * 128:(k + 1) * 128]
            P.emit("tensor", lambda e, s, psg2=psg2, lg=lg, rh=rh, k=k: e.matmul(
                psg2[:], lg, rh, start=(k == 0), stop=False))
            P.emit("tensor", lambda e, s, psu2=psu2, lu=lu, rh=rh, k=k: e.matmul(
                psu2[:], lu, rh, start=(k == 0), stop=False))

    def emit_z(X, ft, ez):
        par = ft % 2
        epz = ez % 2
        P.wait("tensor", "vv", hprime_vv[X][epz])
        P.wait("tensor", "vv", adt_vv[par])
        z_pe[X][epz] = P.emit("tensor", lambda e_, s, X=X, ez=ez, par=par, epz=epz, ft=ft: e_.matmul(
            psZ0[:, X * 512:(X + 1) * 512],
            adt_bf(par)[:, 128 * ez:128 * ez + 128], hp_b(X, epz)[:, 0:512],
            start=(ft == 0 and ez == 0), stop=(ft == FT - 1 and ez == NE - 1)), inc="pe")

    def e_step(X, ft, e):
        par = ft % 2
        epar = e % 2
        q, r = divmod(e, 4)
        ps_g = psA0 if X == 0 else psB0
        ps_u = psA1 if X == 0 else psB1
        lhg = bgp_t(par)[:, 128 * q:128 * (q + 1)]
        lhu = bup_t(par)[:, 128 * q:128 * (q + 1)]
        rg = xapg[q][32 * r:32 * r + 32, X * 512:X * 512 + 512]
        ru = xapu[q][32 * r:32 * r + 32, X * 512:X * 512 + 512]
        tpos = (32 * r, 0)
        if first_diff[0]:
            P.wait("tensor", "g2", xap_gq)
            first_diff[0] = False
        P.wait("tensor", "ss", sg_done[X])
        pvg = P.emit("tensor", lambda e_, s, ps_g=ps_g, lhg=lhg, rg=rg, r=r, e=e, tpos=tpos: e_.matmul(
            ps_g[:], lhg[32 * r:32 * r + 32, :], rg, start=False, stop=(e == NE - 1),
            tile_position=tpos), inc="pe")
        P.wait("tensor", "vv", wu_done[X])
        pvu = P.emit("tensor", lambda e_, s, ps_u=ps_u, lhu=lhu, ru=ru, r=r, e=e, tpos=tpos: e_.matmul(
            ps_u[:], lhu[32 * r:32 * r + 32, :], ru, start=False, stop=(e == NE - 1),
            tile_position=tpos), inc="pe")
        if e > 0:
            emit_z(X, ft, e - 1)
        P.wait("scalar", "pe", pvg)
        P.wait("scalar", "vv", hprime_vv[X][epar])
        sg_done[X] = P.emit("scalar", lambda e_, s, ps_g=ps_g, X=X, epar=epar: e_.activation(
            sg_b(X, epar), ps_g[:], AF.Silu), inc="ss")
        P.wait("vector", "pe", pvu)
        if first_wu[0]:
            P.wait("vector", "s4", wbc_sq)
            first_wu[0] = False
        P.wait("vector", "pe", z_pe[X][epar])
        wu_done[X] = P.emit("vector", lambda e_, s, ps_u=ps_u, e=e, X=X, epar=epar: e_.scalar_tensor_tensor(
            wu_b(X, epar), ps_u[:], 1.0, wbc[:, e * TOK + X * 512: e * TOK + X * 512 + 512],
            ALU.mult, ALU.mult), inc="vv")
        P.wait("vector", "ss", sg_done[X])
        P.emit("vector", lambda e_, s, X=X, epar=epar: e_.tensor_tensor(
            hp_b(X, epar)[:, 0:512], sg_b(X, epar), wu_b(X, epar), ALU.mult), inc="vv")
        if e == 0:
            P.wait("vector", "sh0" if par == 0 else "sh1", hb_dma[par])
            P.emit("vector", lambda e_, s, par=par, X=X, epar=epar: e_.tensor_copy(
                hb_b(par)[:, X * 512:(X + 1) * 512], hp_b(X, epar)[:, 0:512]))
        else:
            P.emit("vector", lambda e_, s, par=par, X=X, epar=epar: e_.tensor_tensor(
                hb_b(par)[:, X * 512:(X + 1) * 512],
                hb_b(par)[:, X * 512:(X + 1) * 512], hp_b(X, epar)[:, 0:512], ALU.add))
        hprime_vv[X][epar] = P.cnt["vv"]
        if e == NE - 1:
            emit_z(X, ft, e)
            last_sca_read[X] = sg_done[X]
            last_vv_read[X] = wu_done[X]
            if X == 1:
                z_pe_all[par] = z_pe[X][e % 2]
                P.wait("gpsimd", "vv", P.cnt["vv"])
                hb_dma[par] = P.emit("gpsimd", lambda e_, s, par=par, ft=ft: e_.dma_start(
                    out=d_hbar.ap()[128 * ft:128 * (ft + 1), :], in_=hb_b(par)[:, 0:1024]),
                    inc=("sh0" if par == 0 else "sh1"), inc_n=16)

    p4_wg_load(0)
    p4_wu_load(0)
    p4_ld[0], p4_ldu[0] = p4_bp_loads(0)
    p4_adt(0)
    p4_wg_load(1)
    p4_wu_load(1)
    p4_ld[1], p4_ldu[1] = p4_bp_loads(1)
    p4_adt(1)
    for chunk in range(8):
        b_chunk(0, 0, chunk)
    for ft in range(FT):
        if 2 <= ft + 1 < FT:
            p4_adt(ft + 1)
        for s in range(NE):
            e_step(0, ft, s)
            b_chunk(1, ft, s)
        if ft + 2 < FT:
            p4_half_pe[ft % 2] = P.cnt["pe"]
            p4_wg_load(ft + 2)
            p4_wu_load(ft + 2)
        for s in range(NE):
            if ft + 1 < FT:
                b_chunk(0, ft + 1, s)
            e_step(1, ft, s)
        if ft + 2 < FT:
            p4_blk_pe[ft % 2] = P.cnt["pe"]
            p4_ld[ft + 2], p4_ldu[ft + 2] = p4_bp_loads(ft + 2)
    z_last_pe = P.cnt["pe"]

    P.wait("scalar", "pe", z_last_pe)
    z_ss = P.emit("scalar", lambda e, s: e.activation(
        z_sb[:], psZ0[:], AF.Copy, scale=misc_s[:, 2:3]), inc="ss")
    if DEBUG:
        P.wait("sync", "ss", z_ss)
        P.emit("sync", lambda e, s: e.dma_start(out=dbg["dbg_z"], in_=z_sb[:].bitcast(F32)), inc="sq", inc_n=16)
        P.wait("sync", "sh0", hb_dma[0])
        P.wait("sync", "sh1", hb_dma[1])
        P.emit("sync", lambda e, s: e.dma_start(out=dbg["dbg_hbar"], in_=d_hbar.ap()), inc="sq", inc_n=16)

    def wd_arr(par): return ar(0 + par * 2048, 2048)
    def wd_bf(par): return ar(4096 + par * 1024, 1024, BF16)
    def hb_ch(par): return ar(6144 + par * 2048, 2048, BF16)
    def x_b(par): return ar(10240 + par * 512, 512)
    def ot_b(par): return ar(11264 + par * 512, 512)
    def wo_ch(par): return arf(0 + par * 2048, 2048)
    def yfT_b(par): return arf(4096 + par * 512, 512)
    bdT_s = arf(5120, D)

    wdt_v = din["WdT"]
    hb_vv = d_hbar.ap().rearrange("(c k p) t -> c p k t", p=128, k=4)
    wo_v = din["Wo"].bitcast(F32R)
    yf_v = d_yfull.ap().bitcast(F32R).rearrange("(k p) t -> p k t", p=128)
    x_v = din["x_sh"].bitcast(F32R).rearrange("(n p) d -> n p d", p=128)
    out_v = dout.rearrange("(n p) d -> n p d", p=128)

    ps_out = [psA0[:, :], psA1[:, :], psB0[:, :], psB1[:, :],
              psZ0[:, 0:512], psZ0[:, 512:1024], psZ1[:, 0:512], psZ1[:, 512:1024]]

    P.wait("sync", "pe", z_last_pe)
    P.wait("sync", "ss", P.cnt["ss"])
    P.wait("sync", "vv", P.cnt["vv"])
    P.wait("gpsimd", "pe", z_last_pe)
    v_bdt = P.emit("sync", lambda e, s: e.dma_start(
        out=bdT_s, in_=din["bdTp"].bitcast(F32R)), inc="sbd", inc_n=16)
    evac_vv = [0] * TSUB
    ch_pe = {0: 0, 1: 0}
    conv_ss = [0, 0]
    wo_pe = [0, 0]
    yfT_pe = [0, 0]
    def x4_b(i):
        return arf(7168 + i * 512, 512)

    def ot4_b(i):
        return ar(10240 + i * 512, 512)

    id_r = idr_s[:]
    evac_ss = [0] * TSUB
    v_x = [0] * TSUB
    v_wo = [0] * 4
    xid4 = [0, 0, 0, 0]
    ot_st4 = [0, 0, 0, 0]
    for ds in range(DS):
        xprev = list(xid4)
        for ch in range(NCH):
            par = ch % 2
            P.wait("sync", "ss", conv_ss[par])
            sup = "su0" if par == 0 else "su1"
            svp = "sv0" if par == 0 else "sv1"
            v_wd = P.emit("sync", lambda e, s, ds=ds, ch=ch, par=par: e.dma_start(
                out=wd_arr(par), in_=wdt_v[ch, ds]), inc=sup, inc_n=16)
            if ds == 0 and ch == 0:
                P.wait("gpsimd", "sh0", hb_dma[0])
                P.wait("gpsimd", "sh1", hb_dma[1])
            P.wait("gpsimd", "pe", ch_pe[par])
            v_hb = P.emit("gpsimd", lambda e, s, ch=ch, par=par: e.dma_start(
                out=hb_ch(par).rearrange("p (k c) -> p k c", k=4),
                in_=hb_vv[ch]), inc=svp, inc_n=16)
            P.wait("scalar", sup, v_wd)
            P.wait("scalar", "pe", ch_pe[par])
            conv_ss[par] = P.emit("scalar", lambda e, s, par=par: e.activation(
                wd_bf(par), wd_arr(par), AF.Copy), inc="ss")
            if ch <= 3:
                for tsub in (2 * ch, 2 * ch + 1):
                    sl = tsub % 4
                    gate = xprev[sl] if ch <= 1 else xid4[sl]
                    if gate:
                        P.wait("scalar", "pe", gate)
                    v_x[tsub] = P.emit("scalar", lambda e, s, tsub=tsub, ds=ds, sl=sl: e.dma_start(
                        out=x4_b(sl), in_=x_v[tsub][:, 512 * ds:512 * (ds + 1)]),
                        inc="sz0", inc_n=16)
            if ch in (1, 3):
                woc = (ch - 1) // 2
                parw = woc % 2
                if ds > 0:
                    P.wait("scalar", "pe", wo_pe[parw])
                v_wo[woc] = P.emit("scalar", lambda e, s, ds=ds, woc=woc, parw=parw: e.dma_start(
                    out=wo_ch(parw), in_=wo_v[woc, ds]),
                    inc=("sm0" if parw == 0 else "sm1"), inc_n=16)
            P.wait("tensor", "ss", conv_ss[par])
            P.wait("tensor", svp, v_hb)
            for tsub in range(TSUB):
                if ds > 0 and ch == 0:
                    P.wait("tensor", "ss", evac_ss[tsub])
                for k in range(4):
                    lh = hb_ch(par).rearrange("p (k t) -> p k t", k=4)[:, k, 128 * tsub:128 * (tsub + 1)]
                    rh = wd_bf(par).rearrange("p (k c) -> p k c", k=4)[:, k, :]
                    first = (ch == 0 and k == 0)
                    is_last = (tsub == TSUB - 1 and k == 3)
                    P.emit("tensor", lambda e, s, tsub=tsub, lh=lh, rh=rh, first=first: e.matmul(
                        ps_out[tsub], lh, rh, start=first, stop=False),
                        inc="pe" if is_last else None)
            if 1 <= ch <= 4:
                for tsub in (2 * (ch - 1), 2 * (ch - 1) + 1):
                    sl = tsub % 4
                    P.wait("tensor", "sz0", v_x[tsub])
                    P.emit("tensor", lambda e, s, tsub=tsub, sl=sl: e.matmul(
                        ps_out[tsub], id_r, x4_b(sl), start=False, stop=False), inc="pe")
                    xid4[sl] = P.cnt["pe"]
            ch_pe[par] = P.cnt["pe"]
        for woc in range(4):
            par = woc % 2
            smp = "sm0" if par == 0 else "sm1"
            if woc >= 2:
                P.wait("scalar", "pe", wo_pe[par])
                v_wo[woc] = P.emit("scalar", lambda e, s, ds=ds, woc=woc, par=par: e.dma_start(
                    out=wo_ch(par), in_=wo_v[woc, ds]), inc=smp, inc_n=16)
            if ds == 0 and woc == 0:
                P.wait("gpsimd", "so", yf_sq[0])
            for tsub in range(TSUB):
                parT = tsub % 2
                if ds > 0 or woc > 0 or tsub >= 2:
                    P.wait("gpsimd", "pe", yfT_pe[parT])
                syp = "sy0" if parT == 0 else "sy1"
                v_yf = P.emit("gpsimd", lambda e, s, tsub=tsub, parT=parT, woc=woc: e.dma_start(
                    out=yfT_b(parT).rearrange("p (k c) -> p k c", k=4),
                    in_=yf_v[:, 4 * woc:4 * (woc + 1), 128 * tsub:128 * (tsub + 1)]), inc=syp, inc_n=16)
                P.wait("tensor", smp, v_wo[woc])
                P.wait("tensor", syp, v_yf)
                for k in range(4):
                    lh = yfT_b(parT).rearrange("p (k c) -> p k c", k=4)[:, k, :]
                    rh = wo_ch(par).rearrange("p (k c) -> p k c", k=4)[:, k, :]
                    P.emit("tensor", lambda e, s, tsub=tsub, lh=lh, rh=rh: e.matmul(
                        ps_out[tsub], lh, rh, start=False, stop=False),
                        inc="pe" if k == 3 else None)
                yfT_pe[parT] = P.cnt["pe"]
            wo_pe[par] = P.cnt["pe"]
        for tsub in range(TSUB):
            P.wait("tensor", "ss", z_ss)
            if ds == 0 and tsub == 0:
                P.wait("tensor", "sbd", v_bdt)
            P.emit("tensor", lambda e, s, tsub=tsub, ds=ds: e.matmul(
                ps_out[tsub], z_sb[:, tsub * 128: tsub * 128 + 128],
                bdT_s[:, ds * 512: ds * 512 + 512],
                start=False, stop=True), inc="pe")
            pv = P.cnt["pe"]
            sl = tsub % 4
            P.wait("scalar", "pe", pv)
            if ot_st4[sl]:
                P.wait("scalar", "st0", ot_st4[sl])
            evac_ss[tsub] = P.emit("scalar", lambda e, s, tsub=tsub, sl=sl: e.activation(
                ot4_b(sl), ps_out[tsub], AF.Copy), inc="ss")
            ot_st4[sl] = P.emit("scalar", lambda e, s, tsub=tsub, ds=ds, sl=sl: e.dma_start(
                out=out_v[tsub][:, 512 * ds:512 * (ds + 1)], in_=ot4_b(sl)),
                inc="st0", inc_n=16)

    P.wait("sync", "st0", P.cnt["st0"])
    P.wait("gpsimd", "gq", P.cnt["gq"])

    with ExitStack() as st_ctx:
        sems = {}
        for name in P.cnt:
            sems[name] = st_ctx.enter_context(nc.semaphore(name))
        block = st_ctx.enter_context(nc.Block())

        def runner(eng_name):
            def run(engine):
                for fn in P.items[eng_name]:
                    fn(engine, sems)
            return run
        block.sync(runner("sync"))
        block.tensor(runner("tensor"))
        block.vector(runner("vector"))
        block.scalar(runner("scalar"))
        block.gpsimd(runner("gpsimd"))

    stack.close()
    return nc


def kernel(**inputs):
    per_core = _prep(inputs)
    nc = build_program()
    res = run_bass_kernel_spmd(nc, per_core, core_ids=list(range(NCORES)))
    out = np.empty((2, LSEQ, D), np.float32)
    for c in range(NCORES):
        b, tp = divmod(c, 4)
        out[b, tp * TOK:(tp + 1) * TOK, :] = res.results[c]["out_sh"]
    return out


# version 17
# speedup vs baseline: 1.6782x; 1.0350x over previous
import numpy as np
from contextlib import ExitStack

import concourse.bass as bass
import concourse.mybir as mybir
from concourse.bass_utils import run_bass_kernel_spmd

F32 = mybir.dt.float32
F32R = mybir.dt.float32r
BF16 = mybir.dt.bfloat16
AF = mybir.ActivationFunctionType
ALU = mybir.AluOpType
AX = mybir.AxisListType

NCORES = 8
TOK = 1024
D = 2048
DFF = 8192
NE = 8
H = 32
HL = 8
LSEQ = 4096
KD = 16
FT = 64
TS = 2
DT = 16
DS = 4
TSUB = 8
NCH = 16
LORA_S = 2.0

DEBUG = False


class Prog:
    ENGINES = ("sync", "tensor", "vector", "scalar", "gpsimd")

    def __init__(self):
        self.items = {e: [] for e in self.ENGINES}
        self.cnt = {}

    def sem(self, name):
        self.cnt[name] = 0

    def inc(self, name, n):
        self.cnt[name] += n
        return self.cnt[name]

    def emit(self, eng, fn, inc=None, inc_n=1):
        if inc is None:
            self.items[eng].append(fn)
            return None
        val = self.inc(inc, inc_n)

        def wrapped(e, s, fn=fn, inc=inc, inc_n=inc_n):
            fn(e, s).then_inc(s[inc], inc_n)
        self.items[eng].append(wrapped)
        return val

    def wait(self, eng, name, val):
        assert val <= self.cnt[name], f"wait {name}>={val} but count={self.cnt[name]}"
        if val <= 0:
            return
        self.items[eng].append(lambda e, s, name=name, val=val: e.wait_ge(s[name], val))


def _prep(inputs):
    import ml_dtypes
    C = np.ascontiguousarray
    x = C(np.asarray(inputs["L0_attn_x"], np.float32))
    Wx = C(np.asarray(inputs["L0_attn_Wx"], np.float32))
    AWg = C(np.asarray(inputs["L0_attn_Wg"], np.float32))
    bg = np.asarray(inputs["L0_attn_bg"], np.float32)
    Wo = C(np.asarray(inputs["L0_attn_Wo"], np.float32))
    A = np.asarray(inputs["L0_attn_A"], np.float32)
    Dp = np.asarray(inputs["L0_attn_Dp"], np.float32)
    rw = np.asarray(inputs["L0_ffn_router_w"], np.float32)
    fWg = np.asarray(inputs["L0_ffn_Wg"], np.float32)
    fWu = np.asarray(inputs["L0_ffn_Wu"], np.float32)
    fWd = np.asarray(inputs["L0_ffn_Wd"], np.float32)
    Ag = np.asarray(inputs["L0_ffn_Ag"], np.float32)
    Bg = np.asarray(inputs["L0_ffn_Bg"], np.float32)
    Au = np.asarray(inputs["L0_ffn_Au"], np.float32)
    Bu = np.asarray(inputs["L0_ffn_Bu"], np.float32)
    Ad = np.asarray(inputs["L0_ffn_Ad"], np.float32)
    Bd = np.asarray(inputs["L0_ffn_Bd"], np.float32)

    def _ktile(W, ncols):
        kt = W.reshape(KD, 128, ncols // 128, 128)
        return C(np.transpose(kt, (2, 1, 0, 3)).reshape(ncols // 128, 128, KD * 128))

    def _p5tile(WdT_):
        t = WdT_.reshape(NCH, 4, 128, DS, 512)
        return C(np.transpose(t, (0, 3, 2, 1, 4)).reshape(NCH, DS, 128, 4 * 512))

    def _wotile(Wo_):
        t = Wo_.reshape(4, 4, 128, DS, 512)
        return C(np.transpose(t, (0, 3, 2, 1, 4)).reshape(4, DS, 128, 4 * 512))

    def _adtpad(Admat):
        out = np.zeros((NE, DFF, 128), np.float32)
        for e in range(NE):
            out[e, :, 16 * e:16 * e + 16] = Admat[e].T
        return C(out)

    def _adtile(A3):
        t = A3.reshape(NE, FT, 128, 128)
        return C(np.transpose(t, (1, 2, 0, 3)).reshape(FT, 128, NE * 128))

    def pairs(Bmat):
        BT = np.transpose(Bmat, (0, 2, 1))
        return C(np.concatenate(
            [np.concatenate([BT[e], BT[(e - 1) % NE]], 0) for e in range(NE)], 0))

    bdq = np.zeros((128, D), np.float32)
    for e in range(NE):
        bdq[16 * e:16 * e + 16, :] = Bd[e].T

    shared = dict(
        Wx=Wx,
        rwT=C(rw.T),
        WgT=_ktile(fWg.T, DFF), WuT=_ktile(fWu.T, DFF),
        WdT=_p5tile(fWd.T), AWg=_ktile(AWg, D), Wo=_wotile(Wo),
        AgT=C(np.transpose(Ag, (2, 0, 1)).reshape(D, NE * 16)),
        AuT=C(np.transpose(Au, (2, 0, 1)).reshape(D, NE * 16)),
        AdT3=_adtile(_adtpad(Ad)),
        BgTp=pairs(Bg), BuTp=pairs(Bu),
        bdTp=C(bdq),
        E_mat=C(np.repeat(np.eye(H, dtype=np.float32), D // H, axis=1)),
        Dp_cols=C(np.repeat(Dp, D // H).reshape(DT, 128).T),
        bg_cols=C(bg.reshape(DT, 128).T),
        id128=np.eye(128, dtype=np.float32),
        id128b=np.eye(128, dtype=ml_dtypes.bfloat16),
        ones_col=np.ones((128, 1), np.float32),
        ones_b=np.ones((128, 1), ml_dtypes.bfloat16),
        misc_c=np.broadcast_to(np.array([1e-5, -0.5, LORA_S, -LORA_S, -1.0, 1.0 / D], np.float32), (128, 6)).copy(),
        zeros128=np.zeros((128, TOK), np.float32),
    )
    AT = C(A.T)
    per_core = []
    for c in range(NCORES):
        b, tp = divmod(c, 4)
        m = dict(shared)
        m["x_sh"] = C(x[b, tp * TOK:(tp + 1) * TOK, :])
        m["AT_loc"] = C(AT[:, HL * tp:HL * (tp + 1)])
        msk = np.zeros((128, 4), np.float32)
        msk[:, tp] = 1.0
        m["msk"] = msk
        mct = np.zeros((H, 128), np.float32)
        for k in range(H):
            r, il = divmod(k, HL)
            mct[k, 32 * r + HL * tp + il] = 1.0
        m["mctT"] = C(mct.T).astype(ml_dtypes.bfloat16)
        per_core.append(m)
    return per_core


def build_program():
    nc = bass.Bass("TRN2", target_bir_lowering=False)
    P = Prog()
    for s in ("sq", "gq", "cc", "pe", "vv", "ss", "st",
              "sx0", "sx1", "g1", "g2", "swx", "srw", "sag", "s2", "sB",
              "sd0", "sd1", "s3", "sa0", "sa1", "so", "s4", "sp0", "sp1",
              "sh0", "sh1", "su0", "su1", "sv0", "sv1", "sm0", "sm1",
              "sy0", "sy1", "sz0", "sz1", "sbd", "st0", "st1",
              "up0", "up1", "ad0", "ad1"):
        P.sem(s)

    din = {}
    for name, shape in [
        ("x_sh", [TOK, D]), ("AT_loc", [128, HL]), ("Wx", [D, 288]),
        ("rwT", [D, NE]), ("WgT", [DFF // 128, 128, KD * 128]), ("WuT", [DFF // 128, 128, KD * 128]), ("WdT", [NCH, DS, 128, 4 * 512]),
        ("AWg", [DT, 128, KD * 128]), ("Wo", [4, DS, 128, 4 * 512]), ("AgT", [D, 128]), ("AuT", [D, 128]),
        ("AdT3", [FT, 128, NE * 128]), ("E_mat", [H, D]), ("Dp_cols", [128, DT]),
        ("bg_cols", [128, DT]), ("id128", [128, 128]), ("ones_col", [128, 1]),
        ("misc_c", [128, 6]), ("msk", [128, 4]),
        ("zeros128", [128, TOK]), ("BgTp", [256, DFF]), ("BuTp", [256, DFF]),
        ("bdTp", [128, D]),
    ]:
        din[name] = nc.dram_tensor(name, shape, F32, kind="ExternalInput").ap()
    for name, shape in [("id128b", [128, 128]), ("ones_b", [128, 1]), ("mctT", [128, H])]:
        din[name] = nc.dram_tensor(name, shape, BF16, kind="ExternalInput").ap()
    dout = nc.dram_tensor("out_sh", [TOK, D], F32, kind="ExternalOutput").ap()

    dbg = {}
    if DEBUG:
        for name, shape in [
            ("dbg_xn", [D, TOK]), ("dbg_w", [TOK, NE]), ("dbg_y", [H, TOK]),
            ("dbg_yfull", [D, TOK]), ("dbg_z", [128, TOK]),
        ]:
            dbg[name] = nc.dram_tensor(name, shape, F32, kind="ExternalOutput").ap()
        dbg["dbg_hbar"] = nc.dram_tensor("dbg_hbar", [DFF, TOK], BF16, kind="ExternalOutput").ap()
        dbg["dbg_cols"] = nc.dram_tensor("dbg_cols", [128, 64], F32, kind="ExternalOutput").ap()
        for nm in ("dbg_B", "dbg_db", "dbg_av", "dbg_b2", "dbg_hs"):
            dbg[nm] = nc.dram_tensor(nm, [128, TOK], F32, kind="ExternalOutput").ap()
        dbg["dbg_gdb"] = nc.dram_tensor("dbg_gdb", [H, TOK], F32, kind="ExternalOutput").ap()
        dbg["dbg_gds"] = nc.dram_tensor("dbg_gds", [128, TOK], F32, kind="ExternalOutput").ap()
        dbg["dbg_arg"] = nc.dram_tensor("dbg_arg", [128, TOK], F32, kind="ExternalOutput").ap()
        dbg["dbg_arg0"] = nc.dram_tensor("dbg_arg0", [128, TOK], F32, kind="ExternalOutput").ap()
        dbg["dbg_dbloc"] = nc.dram_tensor("dbg_dbloc", [H, TOK], BF16, kind="ExternalOutput").ap()

    cc_bc_in = nc.dram_tensor("cc_bc_in", [256, TOK], BF16)
    g_bc = nc.dram_tensor("g_bc", [4 * 256, TOK], BF16)
    cc_db_in = nc.dram_tensor("cc_db_in", [H, TOK], BF16)
    g_dbg = nc.dram_tensor("g_dbg", [4 * H, TOK], BF16)
    d_dbloc = nc.dram_tensor("d_dbloc", [H, TOK], BF16)
    cc_y_in = nc.dram_tensor("cc_y_in", [HL, LSEQ], F32)
    g_yg = nc.dram_tensor("g_yg", [4 * HL, LSEQ], F32)
    d_yfull = nc.dram_tensor("d_yfull", [D, TOK], F32)
    d_hbar = nc.dram_tensor("d_hbar", [DFF, TOK], BF16)
    d_w8 = nc.dram_tensor("d_w8", [NE, TOK], BF16)

    sbuf = nc.alloc_sbuf_tensor
    xn = sbuf("xn", [128, KD * TOK], F32R)
    wbc = sbuf("wbc", [128, NE * TOK], BF16)
    xapg = [sbuf(f"xapg{q}", [128, TOK], F32R) for q in range(2)]
    xapu = [sbuf(f"xapu{q}", [128, TOK], F32R) for q in range(2)]
    z_sb = sbuf("z_sb", [128, TOK], F32R)
    AT_s = sbuf("AT_s", [128, HL], F32)
    Dp_s = sbuf("Dp_s", [128, DT], F32)
    bgc_s = sbuf("bgc_s", [128, DT], F32)
    id_s = sbuf("id_s", [128, 128], F32)
    idr_s = sbuf("idr_s", [128, 128], F32R)
    idb_s = sbuf("idb_s", [128, 128], BF16)
    ones_s = sbuf("ones_s", [128, 1], F32R)
    onesb_s = sbuf("onesb_s", [128, 1], BF16)
    misc_s = sbuf("misc_s", [128, 6], F32)
    msk_s = sbuf("msk_s", [128, 4], F32)
    mct_s = sbuf("mct_s", [128, H], BF16)
    cols = sbuf("cols", [128, 64], F32)
    st8 = sbuf("st8", [128, HL], F32)
    AR = sbuf("AR", [128, 12288], F32)
    ARF = sbuf("ARF", [128, 9216], F32R)

    def ar(off, n, dt_=F32):
        v = AR[:, off:off + n]
        return v if dt_ is F32 else v.bitcast(dt_)

    def arf(off, n):
        return ARF[:, off:off + n]

    stack = ExitStack()
    psA0 = stack.enter_context(nc.psum_tensor("psA0", [128, 512], F32))
    psA1 = stack.enter_context(nc.psum_tensor("psA1", [128, 512], F32))
    psB0 = stack.enter_context(nc.psum_tensor("psB0", [128, 512], F32))
    psB1 = stack.enter_context(nc.psum_tensor("psB1", [128, 512], F32))
    psZ0 = stack.enter_context(nc.psum_tensor("psZ0", [128, 1024], F32))
    psZ1 = stack.enter_context(nc.psum_tensor("psZ1", [128, 1024], F32))

    for name, dst, cast in [
        ("AT_loc", AT_s, None), ("Dp_cols", Dp_s, None), ("bg_cols", bgc_s, None),
        ("id128", id_s, None), ("id128", idr_s, F32R), ("id128b", idb_s, None),
        ("ones_col", ones_s, F32R), ("ones_b", onesb_s, None),
        ("misc_c", misc_s, None), ("msk", msk_s, None), ("mctT", mct_s, None),
    ]:
        src = din[name] if cast is None else din[name].bitcast(cast)
        P.emit("gpsimd", lambda g, s, src=src, dst=dst: g.dma_start(out=dst[:], in_=src),
               inc="g1", inc_n=16)
    consts_gq = P.cnt["g1"]

    wx_s = arf(4096, KD * 288)
    v_wx = P.emit("scalar", lambda e, s: e.dma_start(
        out=wx_s.rearrange("p (k c) -> p k c", k=KD),
        in_=din["Wx"].bitcast(F32R).rearrange("(k p) c -> p k c", p=128)), inc="swx", inc_n=16)
    rw_s = arf(8704, KD * NE)
    v_rw = P.emit("scalar", lambda e, s: e.dma_start(
        out=rw_s.rearrange("p (k c) -> p k c", k=KD),
        in_=din["rwT"].bitcast(F32R).rearrange("(k p) c -> p k c", p=128)), inc="srw", inc_n=16)
    ag_s = arf(0, KD * 128)
    au_s = arf(2048, KD * 128)
    v_ag = P.emit("scalar", lambda e, s: e.dma_start(
        out=ag_s.rearrange("p (k c) -> p k c", k=KD),
        in_=din["AgT"].bitcast(F32R).rearrange("(k p) c -> p k c", p=128)), inc="sag", inc_n=16)
    v_au = P.emit("scalar", lambda e, s: e.dma_start(
        out=au_s.rearrange("p (k c) -> p k c", k=KD),
        in_=din["AuT"].bitcast(F32R).rearrange("(k p) c -> p k c", p=128)), inc="sag", inc_n=16)

    x_t = din["x_sh"].rearrange("(n p) d -> n p d", p=128)
    red_vv = [0, 0]
    sqr_ss = [0, 0]
    for tt in range(TSUB):
        xt = ar(2048 * (tt % 2), 2048)
        sq_ = ar(6144 + 2048 * (tt % 2), 2048)
        sxp = "sx0" if tt % 2 == 0 else "sx1"
        qe = "sync" if tt % 2 == 0 else "gpsimd"
        if tt >= 2:
            P.wait(qe, "vv", red_vv[tt % 2])
        v_x = P.emit(qe, lambda e, s, tt=tt, xt=xt: e.dma_start(out=xt, in_=x_t[tt]),
                     inc=sxp, inc_n=16)
        P.wait("scalar", sxp, v_x)
        if tt >= 2:
            P.wait("scalar", "vv", red_vv[tt % 2])
        sqr_ss[tt % 2] = P.emit("scalar", lambda e, s, xt=xt, sq_=sq_: e.activation(sq_, xt, AF.Square), inc="ss")
        P.wait("vector", sxp, v_x)
        P.emit("vector", lambda e, s, xt=xt, tt=tt: e.reduce_sum(cols[:, tt:tt + 1], xt, AX.X))
        P.wait("vector", "ss", sqr_ss[tt % 2])
        red_vv[tt % 2] = P.emit("vector", lambda e, s, sq_=sq_, tt=tt: e.reduce_sum(
            cols[:, 8 + tt:9 + tt], sq_, AX.X), inc="vv")
    stats_vv = P.cnt["vv"]
    P.wait("scalar", "vv", stats_vv)
    P.wait("scalar", "g1", consts_gq)
    for tt in range(TSUB):
        P.emit("scalar", lambda e, s, tt=tt: e.activation(
            cols[:, 16 + tt:17 + tt], cols[:, tt:tt + 1], AF.Copy, scale=misc_s[:, 5:6]))
    sm_ss = P.emit("scalar", lambda e, s: e.activation(
        cols[:, 24:25], cols[:, 8:9], AF.Copy, scale=misc_s[:, 5:6]), inc="ss")
    for tt in range(1, TSUB):
        sm_ss = P.emit("scalar", lambda e, s, tt=tt: e.activation(
            cols[:, 24 + tt:25 + tt], cols[:, 8 + tt:9 + tt], AF.Copy, scale=misc_s[:, 5:6]), inc="ss")
    P.wait("vector", "ss", sm_ss)
    q_vv = 0
    for tt in range(TSUB):
        q_vv = P.emit("vector", lambda e, s, tt=tt: e.scalar_tensor_tensor(
            cols[:, 32 + tt:33 + tt], cols[:, 16 + tt:17 + tt], cols[:, 16 + tt:17 + tt],
            cols[:, 24 + tt:25 + tt], ALU.mult, ALU.subtract), inc="vv")
    P.wait("scalar", "vv", q_vv)
    for tt in range(TSUB):
        P.emit("scalar", lambda e, s, tt=tt: e.activation(
            cols[:, 40 + tt:41 + tt], cols[:, 32 + tt:33 + tt], AF.Ln,
            scale=misc_s[:, 4:5], bias=misc_s[:, 0:1]))
    rstd_ss = 0
    for tt in range(TSUB):
        rstd_ss = P.emit("scalar", lambda e, s, tt=tt: e.activation(
            cols[:, 48 + tt:49 + tt], cols[:, 40 + tt:41 + tt], AF.Exp, scale=misc_s[:, 1:2]), inc="ss")
    xt_last_vv = [0, 0]
    xnt_pe = [0, 0]
    tp_banks = [psA0, psA1, psB0, psB1]
    tp_ev_bank = [0, 0, 0, 0]
    for tt in range(TSUB):
        xt = ar(2048 * (tt % 2), 2048)
        xnt = ar(4096 + 2048 * (tt % 2), 2048)
        sxp = "sx0" if tt % 2 == 0 else "sx1"
        qe = "sync" if tt % 2 == 0 else "gpsimd"
        if tt < 2:
            P.wait(qe, "vv", stats_vv)
        else:
            P.wait(qe, "vv", xt_last_vv[tt % 2])
        v_x = P.emit(qe, lambda e, s, tt=tt, xt=xt: e.dma_start(out=xt, in_=x_t[tt]),
                     inc=sxp, inc_n=16)
        P.wait("vector", sxp, v_x)
        if tt == 0:
            P.wait("vector", "ss", rstd_ss)
        P.wait("vector", "pe", xnt_pe[tt % 2])
        vv2 = P.emit("vector", lambda e, s, xt=xt, xnt=xnt, tt=tt: e.tensor_scalar(
            xnt, xt, cols[:, 16 + tt:17 + tt], cols[:, 48 + tt:49 + tt],
            ALU.subtract, ALU.mult), inc="vv")
        xt_last_vv[tt % 2] = vv2
        if tt == 0:
            P.wait("tensor", "g1", consts_gq)
        P.wait("tensor", "vv", vv2)
        for j in range(DT):
            pst = tp_banks[j % 4][:, 0:128]
            P.wait("tensor", "ss", tp_ev_bank[j % 4])
            pv = P.emit("tensor", lambda e, s, xnt=xnt, j=j, pst=pst: e.transpose(
                pst, xnt[:, 128 * j:128 * (j + 1)], id_s[:]), inc="pe")
            P.wait("scalar", "pe", pv)
            dst = xn[:, j * TOK + tt * 128: j * TOK + tt * 128 + 128]
            tp_ev_bank[j % 4] = P.emit("scalar", lambda e, s, dst=dst, pst=pst: e.activation(dst, pst, AF.Copy), inc="ss")
        xnt_pe[tt % 2] = P.cnt["pe"]
    xn_done_ss = P.cnt["ss"]
    xn_done_vv = P.cnt["vv"]

    if DEBUG:
        P.wait("sync", "ss", xn_done_ss)
        P.emit("sync", lambda e, s: e.dma_start(out=dbg["dbg_cols"], in_=cols[:]), inc="sq", inc_n=16)
        for j in range(DT):
            P.emit("sync", lambda e, s, j=j: e.dma_start(
                out=dbg["dbg_xn"][128 * j:128 * (j + 1), :],
                in_=xn[:, j * TOK:(j + 1) * TOK].bitcast(F32)), inc="sq", inc_n=16)

    P.wait("tensor", "swx", v_wx)
    P.wait("tensor", "ss", xn_done_ss)
    st2_gq = {}
    for tt in range(TSUB):
        psm = psA0[:, 0:288]
        for k in range(KD):
            lh = xn[:, k * TOK + tt * 128: k * TOK + tt * 128 + 128]
            rh = wx_s[:, k * 288:(k + 1) * 288]
            P.emit("tensor", lambda e, s, psm=psm, lh=lh, rh=rh, k=k: e.matmul(
                psm, lh, rh, start=(k == 0), stop=(k == KD - 1)),
                inc="pe" if k == KD - 1 else None)
        pv = P.cnt["pe"]
        stage = ar(4608 + (tt % 2) * 160, 144, BF16)
        P.wait("scalar", "pe", pv)
        if tt == 0:
            P.wait("scalar", "vv", xn_done_vv)
        sv = P.emit("scalar", lambda e, s, stage=stage, psm=psm: e.activation(stage, psm, AF.Copy), inc="ss")
        P.wait("tensor", "ss", sv)
        for ci in range(3):
            w = 128 if ci < 2 else 32
            pst = (psB0 if ci % 2 == 0 else psB1)[:, 0:128].bitcast(BF16)[:, 0:128]
            src = stage[:, 128 * ci:128 * ci + w]
            pv2 = P.emit("tensor", lambda e, s, pst=pst, src=src, w=w: e.transpose(
                pst[0:w, :] if w < 128 else pst, src, idb_s[:]), inc="pe")
            st2 = ar(5000 + tt * 192 + ci * 64, 64, BF16)
            P.wait("vector", "pe", pv2)
            ev = P.emit("vector", lambda e, s, st2=st2, pst=pst, w=w: e.tensor_copy(
                st2[0:w, :], pst[0:w, :]), inc="vv")
            P.wait("tensor", "vv", ev)
            P.wait("gpsimd", "vv", ev)
            if ci < 2:
                P.emit("gpsimd", lambda e, s, st2=st2, ci=ci, tt=tt: e.dma_start(
                    out=cc_bc_in.ap()[128 * ci:128 * (ci + 1), 128 * tt:128 * (tt + 1)],
                    in_=st2[0:128, :]), inc="gq", inc_n=16)
            else:
                P.emit("gpsimd", lambda e, s, st2=st2, tt=tt: e.dma_start(
                    out=cc_db_in.ap()[:, 128 * tt:128 * (tt + 1)],
                    in_=st2[0:32, :]), inc="gq", inc_n=16)
    ssm_gq = P.cnt["gq"]

    groups = [[0, 1, 2, 3], [4, 5, 6, 7]]
    P.wait("gpsimd", "gq", ssm_gq)
    P.emit("gpsimd", lambda e, s: e.collective_compute(
        "AllGather", ALU.bypass, replica_groups=groups,
        ins=[cc_bc_in.ap().opt()], outs=[g_bc.ap().opt()]), inc="cc")
    P.emit("gpsimd", lambda e, s: e.collective_compute(
        "AllGather", ALU.bypass, replica_groups=groups,
        ins=[cc_db_in.ap().opt()], outs=[g_dbg.ap().opt()]), inc="cc")
    cc_ssm_done = P.cnt["cc"]

    P.wait("tensor", "srw", v_rw)
    rt = ar(6600, 64)
    re_ = ar(6664, 64)
    rm = ar(6728, 64)
    wt_all = ar(6792, 64)
    stg = ar(6856, 64, BF16)
    lg_ss = 0
    for tt in range(TSUB):
        psm = psA1[:, 0:NE]
        P.wait("tensor", "ss", lg_ss)
        for k in range(KD):
            lh = xn[:, k * TOK + tt * 128: k * TOK + tt * 128 + 128]
            rh = rw_s[:, k * NE:(k + 1) * NE]
            P.emit("tensor", lambda e, s, psm=psm, lh=lh, rh=rh, k=k: e.matmul(
                psm, lh, rh, start=(k == 0), stop=(k == KD - 1)),
                inc="pe" if k == KD - 1 else None)
        pv = P.cnt["pe"]
        P.wait("scalar", "pe", pv)
        lg_ss = P.emit("scalar", lambda e, s, psm=psm, tt=tt: e.activation(
            rt[:, 8 * tt:8 * tt + 8], psm, AF.Copy), inc="ss")
    P.wait("vector", "ss", lg_ss)
    for tt in range(TSUB):
        P.emit("vector", lambda e, s, tt=tt: e.reduce_max(cols[:, tt:tt + 1], rt[:, 8 * tt:8 * tt + 8], AX.X))
    for tt in range(TSUB):
        P.emit("vector", lambda e, s, tt=tt: e.reduce_max(cols[:, 8 + tt:9 + tt], rt[:, 8 * tt:8 * tt + 8], AX.X, negate=True))
    for tt in range(TSUB):
        P.emit("vector", lambda e, s, tt=tt: e.tensor_scalar(
            re_[:, 8 * tt:8 * tt + 8], rt[:, 8 * tt:8 * tt + 8], cols[:, tt:tt + 1], None, ALU.is_equal))
    for tt in range(TSUB):
        P.emit("vector", lambda e, s, tt=tt: e.scalar_tensor_tensor(
            rm[:, 8 * tt:8 * tt + 8], re_[:, 8 * tt:8 * tt + 8], -1e30, rt[:, 8 * tt:8 * tt + 8],
            ALU.mult, ALU.add))
    m2_vv = 0
    for tt in range(TSUB):
        m2_vv = P.emit("vector", lambda e, s, tt=tt: e.reduce_max(
            cols[:, 16 + tt:17 + tt], rm[:, 8 * tt:8 * tt + 8], AX.X), inc="vv")
    P.wait("scalar", "vv", m2_vv)
    ew_ss = 0
    for tt in range(TSUB):
        ew_ss = P.emit("scalar", lambda e, s, tt=tt: e.activation(
            re_[:, 8 * tt:8 * tt + 8], rt[:, 8 * tt:8 * tt + 8], AF.Exp, bias=cols[:, 8 + tt:9 + tt]), inc="ss")
    P.wait("vector", "ss", ew_ss)
    for tt in range(TSUB):
        P.emit("vector", lambda e, s, tt=tt: e.tensor_scalar(
            rm[:, 8 * tt:8 * tt + 8], rt[:, 8 * tt:8 * tt + 8], cols[:, 16 + tt:17 + tt], None, ALU.is_ge))
    for tt in range(TSUB):
        P.emit("vector", lambda e, s, tt=tt: e.tensor_tensor(
            wt_all[:, 8 * tt:8 * tt + 8], rm[:, 8 * tt:8 * tt + 8], re_[:, 8 * tt:8 * tt + 8], ALU.mult))
    for tt in range(TSUB):
        P.emit("vector", lambda e, s, tt=tt: e.reduce_sum(
            cols[:, 24 + tt:25 + tt], wt_all[:, 8 * tt:8 * tt + 8], AX.X))
    for tt in range(TSUB):
        P.emit("vector", lambda e, s, tt=tt: e.reciprocal(cols[:, 32 + tt:33 + tt], cols[:, 24 + tt:25 + tt]))
    w_vv = 0
    for tt in range(TSUB):
        w_vv = P.emit("vector", lambda e, s, tt=tt: e.tensor_scalar_mul(
            wt_all[:, 8 * tt:8 * tt + 8], wt_all[:, 8 * tt:8 * tt + 8], cols[:, 32 + tt:33 + tt]), inc="vv")
    if DEBUG:
        P.wait("sync", "vv", w_vv)
        for tt in range(TSUB):
            P.emit("sync", lambda e, s, tt=tt: e.dma_start(
                out=dbg["dbg_w"][128 * tt:128 * (tt + 1), :], in_=wt_all[:, 8 * tt:8 * tt + 8]),
                inc="sq", inc_n=16)
    P.wait("tensor", "vv", w_vv)
    wtr_ev = 0
    w8dma = 0
    for tt in range(TSUB):
        pst = psB0[:, 0:128]
        P.wait("tensor", "vv", wtr_ev)
        pv2 = P.emit("tensor", lambda e, s, pst=pst, tt=tt: e.transpose(
            pst[0:8, :], wt_all[:, 8 * tt:8 * tt + 8], id_s[:]), inc="pe")
        P.wait("vector", "pe", pv2)
        P.wait("vector", "gq", P.cnt["gq"])
        wtr_ev = P.emit("vector", lambda e, s, pst=pst: e.tensor_copy(stg[0:8, 0:128], pst[0:8, :]), inc="vv")
        P.wait("gpsimd", "vv", wtr_ev)
        w8dma = P.emit("gpsimd", lambda e, s, tt=tt: e.dma_start(
            out=d_w8.ap()[:, 128 * tt:128 * (tt + 1)], in_=stg[0:8, 0:128]), inc="gq", inc_n=16)
    w8_gq = P.cnt["gq"]
    P.wait("sync", "gq", w8_gq)
    for e_ in range(NE):
        P.emit("sync", lambda e, s, e_=e_: e.dma_start(
            out=wbc[:, e_ * TOK:(e_ + 1) * TOK],
            in_=d_w8.ap()[e_:e_ + 1, :].partition_broadcast(128)), inc="s4", inc_n=16)
    wbc_sq = P.cnt["s4"]

    P.wait("tensor", "sag", v_au)
    for src, psm in ((ag_s, psZ0), (au_s, psZ1)):
        for ts in range(TS):
            for k in range(KD):
                lh = src[:, k * 128:(k + 1) * 128]
                rh = xn[:, k * TOK + ts * 512: k * TOK + ts * 512 + 512]
                P.emit("tensor", lambda e, s, psm=psm, lh=lh, rh=rh, k=k, ts=ts: e.matmul(
                    psm[:, ts * 512:(ts + 1) * 512], lh, rh,
                    start=(k == 0), stop=(k == KD - 1)),
                    inc="pe" if k == KD - 1 else None)
    xa_pe = P.cnt["pe"]
    xq_pg = ar(0, TOK, F32R)
    xq_ng = ar(1024, TOK, F32R)
    xq_pu = ar(2048, TOK, F32R)
    xq_nu = ar(3072, TOK, F32R)
    P.wait("scalar", "pe", xa_pe)
    P.emit("scalar", lambda e, s: e.activation(xq_pg, psZ0[:], AF.Copy, scale=misc_s[:, 2:3]), inc="ss")
    P.emit("scalar", lambda e, s: e.activation(xq_ng, psZ0[:], AF.Copy, scale=misc_s[:, 3:4]), inc="ss")
    P.emit("scalar", lambda e, s: e.activation(xq_pu, psZ1[:], AF.Copy, scale=misc_s[:, 2:3]), inc="ss")
    s4 = P.emit("scalar", lambda e, s: e.activation(xq_nu, psZ1[:], AF.Copy, scale=misc_s[:, 3:4]), inc="ss")
    P.wait("gpsimd", "ss", s4)
    for e_ in range(NE):
        q, r = divmod(e_, 4)
        for pos, neg, dsts in ((xq_pg, xq_ng, xapg), (xq_pu, xq_nu, xapu)):
            P.emit("gpsimd", lambda e, s, dsts=dsts, q=q, r=r, pos=pos, e_=e_: e.dma_start(
                out=dsts[q][32 * r:32 * r + 16, :], in_=pos[16 * e_:16 * e_ + 16, :]), inc="g2", inc_n=16)
            if e_ == 0:
                P.emit("gpsimd", lambda e, s, dsts=dsts, q=q, r=r: e.dma_start(
                    out=dsts[q][32 * r + 16:32 * r + 32, :],
                    in_=din["zeros128"].bitcast(F32R)[0:16, :]), inc="g2", inc_n=16)
            else:
                P.emit("gpsimd", lambda e, s, dsts=dsts, q=q, r=r, neg=neg, ep=e_ - 1: e.dma_start(
                    out=dsts[q][32 * r + 16:32 * r + 32, :], in_=neg[16 * ep:16 * ep + 16, :]), inc="g2", inc_n=16)
    xap_gq = P.cnt["g2"]

    bc_v = g_bc.ap().rearrange("(r c) t -> r c t", r=4)
    db_v = d_dbloc.ap().rearrange("(r h) t -> r h t", r=4)
    P.wait("sync", "g2", xap_gq)
    P.wait("sync", "cc", cc_ssm_done)
    gd_s = ar(8704, 512, BF16)
    v_gd = P.emit("sync", lambda e, s: e.dma_start(out=gd_s, in_=g_dbg.ap()), inc="s2", inc_n=16)
    P.wait("tensor", "s2", v_gd)
    P.wait("tensor", "ss", P.cnt["ss"])
    P.wait("tensor", "vv", P.cnt["vv"])
    if DEBUG:
        P.wait("vector", "sq", v_gd)
        gdump = P.emit("vector", lambda e, s: e.tensor_copy(ar(2048, 1024), gd_s), inc="vv")
        P.wait("sync", "vv", gdump)
        gd_dsq = P.emit("sync", lambda e, s: e.dma_start(out=dbg["dbg_gds"], in_=ar(2048, 1024)), inc="sq", inc_n=16)
        P.wait("vector", "sq", gd_dsq)
    dbsel_ss = 0
    for q4 in range(2):
        psd = psZ1[0:32, 512 * q4:512 * (q4 + 1)]
        pv = P.emit("tensor", lambda e, s, psd=psd, q4=q4: e.matmul(
            psd, mct_s[:], gd_s[:, 512 * q4:512 * (q4 + 1)], start=True, stop=True), inc="pe")
        P.wait("scalar", "pe", pv)
        dbsel_ss = P.emit("scalar", lambda e, s, psd=psd, q4=q4: e.activation(
            ar(9216, 512, BF16)[0:32, 512 * q4:512 * (q4 + 1)], psd, AF.Copy), inc="ss")
    P.wait("sync", "ss", dbsel_ss)
    v_dbl = P.emit("sync", lambda e, s: e.dma_start(
        out=d_dbloc.ap(), in_=ar(9216, 512, BF16)[0:32, :]), inc="s2", inc_n=16)
    P.wait("sync", "s2", v_dbl)
    if DEBUG:
        P.emit("sync", lambda e, s: e.dma_start(out=dbg["dbg_dbloc"], in_=d_dbloc.ap()), inc="sq", inc_n=16)
    y_gq = [0]
    db_last_vv = [0, 0]
    av_free_ss = [0, 0]
    b2_gp = [0, 0]
    scanU_gp = [0, 0]
    scanU_prev = [0]
    yred_prev = [0, 0]
    sv2_gp = [0, 0]
    ydma_prev = [0, 0]
    for qt in range(4):
        Bb = ar(0, 512, BF16)
        Cb = ar(512, 512, BF16)
        if qt >= 1:
            P.wait("sync", "vv", P.cnt["vv"])
        v_b = P.emit("sync", lambda e, s, Bb=Bb, qt=qt: e.dma_start(
            out=Bb, in_=bc_v[qt, 0:128, :]), inc="sB", inc_n=16)
        v_c = P.emit("sync", lambda e, s, Cb=Cb, qt=qt: e.dma_start(
            out=Cb, in_=bc_v[qt, 128:256, :]), inc="sB", inc_n=16)
        for il in range(HL):
            dbb = ar(1024 + (il % 2) * 512, 512, BF16)
            if il >= 2:
                P.wait("sync", "vv", db_last_vv[il % 2])
            sdp = "sd0" if il % 2 == 0 else "sd1"
            v_d = P.emit("sync", lambda e, s, dbb=dbb, qt=qt, il=il: e.dma_start(
                out=dbb.rearrange("p (a t) -> p a t", a=1),
                in_=db_v[qt, il:il + 1, :].partition_broadcast(128)),
                inc=sdp, inc_n=16)
            gp = (qt * HL + il) % 2
            arg = ar(2048, 1024) if gp == 0 else ar(9728, 1024)
            tmp = ar(3072, 1024)
            av = ar(4096, 1024)
            b2 = ar(5120, 1024) if gp == 0 else ar(11008, 1024)
            hs = ar(6144, 1024)
            Ub = ar(7168, 512, BF16) if gp == 0 else ar(10752, 512, BF16)
            P.wait("vector", sdp, v_d)
            P.wait("vector", "sB", v_c)
            P.wait("vector", "ss", av_free_ss[gp])
            P.wait("vector", "vv", b2_gp[gp])
            vv1 = P.emit("vector", lambda e, s, arg=arg, Bb=Bb, dbb=dbb: e.tensor_tensor(
                arg, Bb, dbb, ALU.add), inc="vv")
            db_last_vv[il % 2] = vv1
            P.wait("scalar", "vv", vv1)
            P.emit("scalar", lambda e, s, tmp=tmp, arg=arg: e.activation(tmp, arg, AF.Exp))
            ln_ss = P.emit("scalar", lambda e, s, arg=arg, tmp=tmp: e.activation(
                arg, tmp, AF.Ln, bias=1.0), inc="ss")
            P.wait("scalar", "vv", scanU_prev[0])
            av_free_ss[gp] = P.emit("scalar", lambda e, s, av=av, arg=arg, il=il: e.activation(
                av, arg, AF.Exp, scale=AT_s[:, il:il + 1]), inc="ss")
            P.wait("gpsimd", "ss", ln_ss)
            P.wait("gpsimd", "vv", scanU_gp[gp])
            b2_gp[gp] = P.emit("gpsimd", lambda e, s, b2=b2, arg=arg: e.tensor_tensor(
                b2, arg, arg, ALU.mult), inc="vv")
            P.wait("vector", "vv", b2_gp[gp])
            P.wait("vector", "ss", av_free_ss[gp])
            if qt == 0:
                vv2 = P.emit("vector", lambda e, s, hs=hs, av=av, b2=b2: e.tensor_tensor_scan(
                    hs, av, b2, 0.0, ALU.mult, ALU.add), inc="vv")
            else:
                vv2 = P.emit("vector", lambda e, s, hs=hs, av=av, b2=b2, il=il: e.tensor_tensor_scan(
                    hs, av, b2, st8[:, il:il + 1], ALU.mult, ALU.add), inc="vv")
            scanU_prev[0] = vv2
            scanU_gp[gp] = vv2
            P.wait("vector", "pe", yred_prev[gp])
            vv3 = P.emit("vector", lambda e, s, Ub=Ub, Cb=Cb, hs=hs: e.tensor_tensor(
                Ub, Cb, hs, ALU.mult), inc="vv")
            if qt < 3:
                P.emit("vector", lambda e, s, hs=hs, il=il: e.tensor_copy(
                    st8[:, il:il + 1], hs[:, TOK - 1:TOK]))
            if DEBUG and qt == 0 and il == 0:
                dvv0 = P.emit("vector", lambda e, s, arg=arg: e.tensor_copy(ar(11264, 1024), arg), inc="vv")
                P.wait("sync", "vv", dvv0)
                d0sq = P.emit("sync", lambda e, s: e.dma_start(out=dbg["dbg_arg0"], in_=ar(11264, 1024)), inc="sq", inc_n=16)
                P.wait("vector", "sq", d0sq)
            if DEBUG and qt == 0 and il == 1:
                P.wait("sync", "vv", vv3)
                for nm, buf in (("dbg_av", av), ("dbg_b2", b2), ("dbg_hs", hs), ("dbg_arg", arg)):
                    P.emit("sync", lambda e, s, nm=nm, buf=buf: e.dma_start(
                        out=dbg[nm], in_=buf), inc="sq", inc_n=16)
                dstg = ar(10240, 1024)
                dvv = P.emit("vector", lambda e, s, Bb=Bb, dstg=dstg: e.tensor_copy(dstg, Bb), inc="vv")
                P.wait("sync", "vv", dvv)
                dsq = P.emit("sync", lambda e, s, dstg=dstg: e.dma_start(out=dbg["dbg_B"], in_=dstg), inc="sq", inc_n=16)
                P.wait("vector", "sq", dsq)
                dvv = P.emit("vector", lambda e, s, dbb=dbb, dstg=dstg: e.tensor_copy(dstg, dbb), inc="vv")
                P.wait("sync", "vv", dvv)
                dsq = P.emit("sync", lambda e, s, dstg=dstg: e.dma_start(out=dbg["dbg_db"], in_=dstg), inc="sq", inc_n=16)
                P.wait("vector", "sq", dsq)
                dvv = P.emit("vector", lambda e, s, dstg=dstg: e.tensor_copy(dstg[0:32, :], ar(9216, 512, BF16)[0:32, :]), inc="vv")
                P.wait("sync", "vv", dvv)
                dsq = P.emit("sync", lambda e, s, dstg=dstg: e.dma_start(out=dbg["dbg_gdb"], in_=dstg[0:32, :]), inc="sq", inc_n=16)
                P.wait("vector", "sq", dsq)
            P.wait("tensor", "vv", vv3)
            sv2 = 0
            ypar = gp
            P.wait("tensor", "ss", sv2_gp[gp])
            pvs = []
            for q2 in range(2):
                psm = (psZ1 if gp == 0 else psZ0)[0:1, 512 * q2:512 * (q2 + 1)]
                pv = P.emit("tensor", lambda e, s, psm=psm, Ub=Ub, q2=q2: e.matmul(
                    psm, onesb_s[:], Ub[:, 512 * q2:512 * (q2 + 1)], start=True, stop=True), inc="pe")
                pvs.append((pv, psm))
            yred_prev[gp] = pvs[1][0]
            for q2, (pv, psm) in enumerate(pvs):
                P.wait("scalar", "pe", pv)
                P.wait("scalar", "gq", ydma_prev[ypar])
                sv2 = P.emit("scalar", lambda e, s, psm=psm, q2=q2, ypar=ypar: e.activation(
                    ar(7680 + 1024 * ypar + 512 * q2, 512)[0:1, :], psm, AF.Copy), inc="ss")
            sv2_gp[gp] = sv2
            P.wait("gpsimd", "ss", sv2)
            for q2 in range(2):
                col0 = qt * 1024 + 512 * q2
                y_gq[0] = P.emit("gpsimd", lambda e, s, il=il, col0=col0, q2=q2, ypar=ypar: e.dma_start(
                    out=cc_y_in.ap()[il:il + 1, col0:col0 + 512],
                    in_=ar(7680 + 1024 * ypar + 512 * q2, 512)[0:1, :]), inc="gq", inc_n=16)
            ydma_prev[ypar] = y_gq[0]
    P.wait("gpsimd", "gq", y_gq[0])
    P.emit("gpsimd", lambda e, s: e.collective_compute(
        "AllGather", ALU.bypass, replica_groups=groups,
        ins=[cc_y_in.ap().opt()], outs=[g_yg.ap().opt()]), inc="cc")
    cc_y_done = P.cnt["cc"]

    yT = arf(0, 1024)[0:32, :]
    E_s = arf(5120, 2048)[0:H, :]
    awg_v = din["AWg"].bitcast(F32R)
    ld_awg = [0] * DT
    P.wait("sync", "pe", xa_pe)
    v_E = P.emit("sync", lambda e, s: e.dma_start(out=E_s, in_=din["E_mat"].bitcast(F32R)), inc="s3", inc_n=16)
    ld_awg[0] = P.emit("sync", lambda e, s: e.dma_start(
        out=arf(1024, 2048), in_=awg_v[0]), inc="sa0", inc_n=16)
    ld_awg[1] = P.emit("sync", lambda e, s: e.dma_start(
        out=arf(3072, 2048), in_=awg_v[1]), inc="sa1", inc_n=16)
    P.wait("sync", "cc", cc_y_done)
    P.wait("sync", "vv", P.cnt["vv"])
    P.wait("sync", "ss", P.cnt["ss"])
    yg_s = ar(0, 4096)[0:32, :]
    yt0 = ar(4096, 1024)[0:32, :]
    yt1 = ar(5120, 1024)[0:32, :]
    v_yg = P.emit("sync", lambda e, s: e.dma_start(out=yg_s, in_=g_yg.ap()), inc="s3", inc_n=16)
    P.wait("vector", "s3", v_yg)
    P.emit("vector", lambda e, s: e.tensor_scalar_mul(yt0, yg_s[:, 0:1024], msk_s[0:32, 0:1]))
    P.emit("vector", lambda e, s: e.scalar_tensor_tensor(
        yt1, yg_s[:, 1024:2048], msk_s[0:32, 1:2], yt0, ALU.mult, ALU.add))
    P.emit("vector", lambda e, s: e.scalar_tensor_tensor(
        yt0, yg_s[:, 2048:3072], msk_s[0:32, 2:3], yt1, ALU.mult, ALU.add))
    ysel_vv = P.emit("vector", lambda e, s: e.scalar_tensor_tensor(
        yt1, yg_s[:, 3072:4096], msk_s[0:32, 3:4], yt0, ALU.mult, ALU.add), inc="vv")
    P.wait("scalar", "vv", ysel_vv)
    ysel_ss = P.emit("scalar", lambda e, s: e.activation(yT, yt1, AF.Copy), inc="ss")
    if DEBUG:
        P.wait("sync", "vv", ysel_vv)
        P.emit("sync", lambda e, s: e.dma_start(out=dbg["dbg_y"], in_=g_yg.ap()[0:32, 0:TOK]), inc="sq", inc_n=16)
    p3_vv = {0: P.cnt["vv"], 1: P.cnt["vv"]}
    p3_ss = {0: P.cnt["ss"], 1: P.cnt["ss"]}
    aw_pe = [0, 0]
    yf_sq = [0]
    yf_sq2 = {0: 0, 1: 0}
    for j in range(DT):
        aw = arf(1024 + (j % 2) * 2048, 2048)
        if 2 <= j + 1 < DT:
            if j >= 1:
                P.wait("sync", "pe", aw_pe[(j + 1) % 2])
            ld_awg[j + 1] = P.emit("sync", lambda e, s, j=j: e.dma_start(
                out=arf(1024 + ((j + 1) % 2) * 2048, 2048), in_=awg_v[j + 1]),
                inc=("sa0" if (j + 1) % 2 == 0 else "sa1"), inc_n=16)
        for ts in range(TS):
            it = (j * TS + ts) % 2
            psb = psZ1 if it == 0 else psZ0
            psy = psb[:, 0:512]
            psg_ = psb[:, 512:1024]
            P.wait("tensor", "ss", ysel_ss)
            P.wait("tensor", "s3", P.cnt["s3"])
            P.wait("tensor", "sa0" if j % 2 == 0 else "sa1", ld_awg[j])
            P.wait("tensor", "vv", p3_vv[it])
            P.wait("tensor", "ss", p3_ss[it])
            pv_y = P.emit("tensor", lambda e, s, psy=psy, j=j, ts=ts: e.matmul(
                psy, E_s[:, 128 * j:128 * (j + 1)], yT[:, 512 * ts:512 * (ts + 1)],
                start=True, stop=True), inc="pe")
            for k in range(KD):
                lh = aw[:, k * 128:(k + 1) * 128]
                rh = xn[:, k * TOK + ts * 512: k * TOK + ts * 512 + 512]
                P.emit("tensor", lambda e, s, psg_=psg_, lh=lh, rh=rh, k=k: e.matmul(
                    psg_, lh, rh, start=(k == 0), stop=(k == KD - 1)),
                    inc="pe" if k == KD - 1 else None)
            pv_g = P.cnt["pe"]
            if ts == 1:
                aw_pe[j % 2] = pv_g
            yh = ar(6144, 512)
            gt = ar(6656, 512)
            yf = ar(7168 + it * 512, 512)
            P.wait("vector", "pe", pv_y)
            P.wait("vector", "so", yf_sq2[it])
            p3_vv[it] = P.emit("vector", lambda e, s, yh=yh, psy=psy, j=j, ts=ts: e.scalar_tensor_tensor(
                yh, xn.bitcast(F32)[:, j * TOK + ts * 512: j * TOK + ts * 512 + 512],
                Dp_s[:, j:j + 1], psy, ALU.mult, ALU.add), inc="vv")
            P.wait("scalar", "pe", pv_g)
            p3_ss[it] = P.emit("scalar", lambda e, s, gt=gt, psg_=psg_, j=j: e.activation(
                gt, psg_, AF.Sigmoid, bias=bgc_s[:, j:j + 1]), inc="ss")
            P.wait("vector", "ss", p3_ss[it])
            vv2 = P.emit("vector", lambda e, s, yf=yf, yh=yh, gt=gt: e.tensor_tensor(
                yf, yh, gt, ALU.mult), inc="vv")
            P.wait("scalar", "vv", vv2)
            yf_sq[0] = P.emit("scalar", lambda e, s, yf=yf, j=j, ts=ts: e.dma_start(
                out=d_yfull.ap()[128 * j:128 * (j + 1), 512 * ts:512 * (ts + 1)],
                in_=yf), inc="so", inc_n=16)
            yf_sq2[it] = yf_sq[0]
    if DEBUG:
        P.wait("sync", "so", yf_sq[0])
        P.emit("sync", lambda e, s: e.dma_start(out=dbg["dbg_yfull"], in_=d_yfull.ap()), inc="sq", inc_n=16)

    def wg_blk(par): return arf(0 + par * 2048, 2048)
    def wu_blk(par): return arf(4096 + par * 2048, 2048)
    def bgp_t(par): return arf(8192 + par * 256, 256)
    def bup_t(par): return arf(8704 + par * 256, 256)
    def sg_b(X, epar): return ar(0 + (2 * X + epar) * 256, 256, BF16)
    def wu_b(X, epar): return ar(1024 + (2 * X + epar) * 256, 256, BF16)
    def hp_b(X, epar): return ar(2048 + (2 * X + epar) * 256, 256, BF16)
    def hb_b(par): return ar(3072 + par * 512, 512, BF16)
    def adt_t(par): return ar(4096 + par * 1024, 1024)
    def adt_bf(par): return ar(6144 + par * 512, 512, BF16)

    wgt_v = din["WgT"].bitcast(F32R)
    wut_v = din["WuT"].bitcast(F32R)
    bgp_v = din["BgTp"].bitcast(F32R).rearrange("(q p) f -> p q f", p=128)
    bup_v = din["BuTp"].bitcast(F32R).rearrange("(q p) f -> p q f", p=128)
    adt_v = din["AdT3"]

    P.wait("sync", "pe", P.cnt["pe"])
    P.wait("sync", "vv", P.cnt["vv"])
    P.wait("sync", "ss", P.cnt["ss"])
    P.wait("gpsimd", "pe", P.cnt["pe"])

    p4_ld = [0] * (FT + 2)
    p4_ldu = [0] * (FT + 2)
    p4_half_pe = [0, 0]
    p4_blk_pe = [0, 0]
    adt_vv = [0, 0]
    z_pe_all = [0, 0]

    def p4_wg_load(ft):
        par = ft % 2
        spp = "sp0" if par == 0 else "sp1"
        P.wait("sync", "pe", p4_half_pe[par])
        P.emit("sync", lambda e, s, dst=wg_blk(par), ft=ft: e.dma_start(
            out=dst, in_=wgt_v[ft]), inc=spp, inc_n=16)

    def p4_wu_load(ft):
        par = ft % 2
        upp = "up0" if par == 0 else "up1"
        P.wait("gpsimd", "pe", p4_half_pe[par])
        P.emit("gpsimd", lambda e, s, dst=wu_blk(par), ft=ft: e.dma_start(
            out=dst, in_=wut_v[ft]), inc=upp, inc_n=16)

    def p4_bp_loads(ft):
        par = ft % 2
        spp = "sp0" if par == 0 else "sp1"
        upp = "up0" if par == 0 else "up1"
        P.wait("sync", "pe", p4_blk_pe[par])
        P.emit("sync", lambda e, s, dst=bgp_t(par), ft=ft: e.dma_start(
            out=dst.rearrange("p (q c) -> p q c", q=2),
            in_=bgp_v[:, :, 128 * ft:128 * (ft + 1)]), inc=spp, inc_n=16)
        P.wait("gpsimd", "pe", p4_blk_pe[par])
        P.emit("gpsimd", lambda e, s, dst=bup_t(par), ft=ft: e.dma_start(
            out=dst.rearrange("p (q c) -> p q c", q=2),
            in_=bup_v[:, :, 128 * ft:128 * (ft + 1)]), inc=upp, inc_n=16)
        return P.cnt[spp], P.cnt[upp]

    def p4_adt(ft):
        par = ft % 2
        adp = "ad0" if par == 0 else "ad1"
        P.wait("scalar", "vv", adt_vv[par])
        v_ad = P.emit("scalar", lambda e, s, ft=ft, par=par: e.dma_start(
            out=adt_t(par), in_=adt_v[ft]), inc=adp, inc_n=16)
        P.wait("vector", adp, v_ad)
        P.wait("vector", "pe", z_pe_all[par])
        adt_vv[par] = P.emit("vector", lambda e, s, par=par: e.tensor_copy(
            adt_bf(par), adt_t(par)), inc="vv")

    sg_done = [P.cnt["ss"], P.cnt["ss"]]
    wu_done = [P.cnt["vv"], P.cnt["vv"]]
    last_sca_read = [P.cnt["ss"], P.cnt["ss"]]
    last_vv_read = [P.cnt["vv"], P.cnt["vv"]]
    hprime_vv = [[0, 0], [0, 0]]
    z_pe = [[0, 0], [0, 0]]
    hb_dma = [0, 0]
    first_wu = [True]
    first_diff = [True]

    def b_chunk(X, ft, chunk):
        par = ft % 2
        psg2 = psA0 if X == 0 else psB0
        psu2 = psA1 if X == 0 else psB1
        if chunk == 0:
            P.wait("tensor", "sp0" if par == 0 else "sp1", p4_ld[ft])
            P.wait("tensor", "up0" if par == 0 else "up1", p4_ldu[ft])
            P.wait("tensor", "ss", last_sca_read[X])
            P.wait("tensor", "vv", last_vv_read[X])
        for k in (2 * chunk, 2 * chunk + 1):
            rh = xn[:, k * TOK + X * 512: k * TOK + X * 512 + 512]
            lg = wg_blk(par)[:, k * 128:(k + 1) * 128]
            lu = wu_blk(par)[:, k * 128:(k + 1) * 128]
            P.emit("tensor", lambda e, s, psg2=psg2, lg=lg, rh=rh, k=k: e.matmul(
                psg2[:], lg, rh, start=(k == 0), stop=False))
            P.emit("tensor", lambda e, s, psu2=psu2, lu=lu, rh=rh, k=k: e.matmul(
                psu2[:], lu, rh, start=(k == 0), stop=False))

    def emit_z(X, ft, ez):
        par = ft % 2
        epz = ez % 2
        P.wait("tensor", "vv", hprime_vv[X][epz])
        P.wait("tensor", "vv", adt_vv[par])
        z_pe[X][epz] = P.emit("tensor", lambda e_, s, X=X, ez=ez, par=par, epz=epz, ft=ft: e_.matmul(
            psZ0[:, X * 512:(X + 1) * 512],
            adt_bf(par)[:, 128 * ez:128 * ez + 128], hp_b(X, epz)[:, 0:512],
            start=(ft == 0 and ez == 0), stop=(ft == FT - 1 and ez == NE - 1)), inc="pe")

    def e_step(X, ft, e):
        par = ft % 2
        epar = e % 2
        q, r = divmod(e, 4)
        ps_g = psA0 if X == 0 else psB0
        ps_u = psA1 if X == 0 else psB1
        lhg = bgp_t(par)[:, 128 * q:128 * (q + 1)]
        lhu = bup_t(par)[:, 128 * q:128 * (q + 1)]
        rg = xapg[q][32 * r:32 * r + 32, X * 512:X * 512 + 512]
        ru = xapu[q][32 * r:32 * r + 32, X * 512:X * 512 + 512]
        tpos = (32 * r, 0)
        if first_diff[0]:
            P.wait("tensor", "g2", xap_gq)
            first_diff[0] = False
        P.wait("tensor", "ss", sg_done[X])
        pvg = P.emit("tensor", lambda e_, s, ps_g=ps_g, lhg=lhg, rg=rg, r=r, e=e, tpos=tpos: e_.matmul(
            ps_g[:], lhg[32 * r:32 * r + 32, :], rg, start=False, stop=(e == NE - 1),
            tile_position=tpos), inc="pe")
        P.wait("tensor", "vv", wu_done[X])
        pvu = P.emit("tensor", lambda e_, s, ps_u=ps_u, lhu=lhu, ru=ru, r=r, e=e, tpos=tpos: e_.matmul(
            ps_u[:], lhu[32 * r:32 * r + 32, :], ru, start=False, stop=(e == NE - 1),
            tile_position=tpos), inc="pe")
        if e > 0:
            emit_z(X, ft, e - 1)
        P.wait("scalar", "pe", pvg)
        P.wait("scalar", "vv", hprime_vv[X][epar])
        sg_done[X] = P.emit("scalar", lambda e_, s, ps_g=ps_g, X=X, epar=epar: e_.activation(
            sg_b(X, epar), ps_g[:], AF.Silu), inc="ss")
        P.wait("vector", "pe", pvu)
        if first_wu[0]:
            P.wait("vector", "s4", wbc_sq)
            first_wu[0] = False
        P.wait("vector", "pe", z_pe[X][epar])
        wu_done[X] = P.emit("vector", lambda e_, s, ps_u=ps_u, e=e, X=X, epar=epar: e_.scalar_tensor_tensor(
            wu_b(X, epar), ps_u[:], 1.0, wbc[:, e * TOK + X * 512: e * TOK + X * 512 + 512],
            ALU.mult, ALU.mult), inc="vv")
        P.wait("vector", "ss", sg_done[X])
        P.emit("vector", lambda e_, s, X=X, epar=epar: e_.tensor_tensor(
            hp_b(X, epar)[:, 0:512], sg_b(X, epar), wu_b(X, epar), ALU.mult), inc="vv")
        if e == 0:
            P.wait("vector", "sh0" if par == 0 else "sh1", hb_dma[par])
            P.emit("vector", lambda e_, s, par=par, X=X, epar=epar: e_.tensor_copy(
                hb_b(par)[:, X * 512:(X + 1) * 512], hp_b(X, epar)[:, 0:512]))
        else:
            P.emit("vector", lambda e_, s, par=par, X=X, epar=epar: e_.tensor_tensor(
                hb_b(par)[:, X * 512:(X + 1) * 512],
                hb_b(par)[:, X * 512:(X + 1) * 512], hp_b(X, epar)[:, 0:512], ALU.add))
        hprime_vv[X][epar] = P.cnt["vv"]
        if e == NE - 1:
            emit_z(X, ft, e)
            last_sca_read[X] = sg_done[X]
            last_vv_read[X] = wu_done[X]
            if X == 1:
                z_pe_all[par] = z_pe[X][e % 2]
                P.wait("gpsimd", "vv", P.cnt["vv"])
                hb_dma[par] = P.emit("gpsimd", lambda e_, s, par=par, ft=ft: e_.dma_start(
                    out=d_hbar.ap()[128 * ft:128 * (ft + 1), :], in_=hb_b(par)[:, 0:1024]),
                    inc=("sh0" if par == 0 else "sh1"), inc_n=16)

    p4_wg_load(0)
    p4_wu_load(0)
    p4_ld[0], p4_ldu[0] = p4_bp_loads(0)
    p4_adt(0)
    p4_wg_load(1)
    p4_wu_load(1)
    p4_ld[1], p4_ldu[1] = p4_bp_loads(1)
    p4_adt(1)
    for chunk in range(8):
        b_chunk(0, 0, chunk)
    for ft in range(FT):
        if 2 <= ft + 1 < FT:
            p4_adt(ft + 1)
        for s in range(NE):
            e_step(0, ft, s)
            b_chunk(1, ft, s)
        if ft + 2 < FT:
            p4_half_pe[ft % 2] = P.cnt["pe"]
            p4_wg_load(ft + 2)
            p4_wu_load(ft + 2)
        for s in range(NE):
            if ft + 1 < FT:
                b_chunk(0, ft + 1, s)
            e_step(1, ft, s)
        if ft + 2 < FT:
            p4_blk_pe[ft % 2] = P.cnt["pe"]
            p4_ld[ft + 2], p4_ldu[ft + 2] = p4_bp_loads(ft + 2)
    z_last_pe = P.cnt["pe"]

    P.wait("scalar", "pe", z_last_pe)
    z_ss = P.emit("scalar", lambda e, s: e.activation(
        z_sb[:], psZ0[:], AF.Copy, scale=misc_s[:, 2:3]), inc="ss")
    if DEBUG:
        P.wait("sync", "ss", z_ss)
        P.emit("sync", lambda e, s: e.dma_start(out=dbg["dbg_z"], in_=z_sb[:].bitcast(F32)), inc="sq", inc_n=16)
        P.wait("sync", "sh0", hb_dma[0])
        P.wait("sync", "sh1", hb_dma[1])
        P.emit("sync", lambda e, s: e.dma_start(out=dbg["dbg_hbar"], in_=d_hbar.ap()), inc="sq", inc_n=16)

    def wd_arr(par): return ar(0 + par * 2048, 2048)
    def wd_bf(par): return ar(4096 + par * 1024, 1024, BF16)
    def hb_ch(par): return ar(6144 + par * 2048, 2048, BF16)
    def x_b(par): return ar(10240 + par * 512, 512)
    def ot_b(par): return ar(11264 + par * 512, 512)
    def wo_ch(par): return arf(0 + par * 2048, 2048)
    def yfT_b(par): return arf(4096 + par * 512, 512)
    bdT_s = arf(5120, D)

    wdt_v = din["WdT"]
    hb_vv = d_hbar.ap().rearrange("(c k p) t -> c p k t", p=128, k=4)
    wo_v = din["Wo"].bitcast(F32R)
    yf_v = d_yfull.ap().bitcast(F32R).rearrange("(k p) t -> p k t", p=128)
    x_v = din["x_sh"].bitcast(F32R).rearrange("(n p) d -> n p d", p=128)
    out_v = dout.rearrange("(n p) d -> n p d", p=128)

    ps_out = [psA0[:, :], psA1[:, :], psB0[:, :], psB1[:, :],
              psZ0[:, 0:512], psZ0[:, 512:1024], psZ1[:, 0:512], psZ1[:, 512:1024]]

    P.wait("sync", "pe", z_last_pe)
    P.wait("sync", "ss", P.cnt["ss"])
    P.wait("sync", "vv", P.cnt["vv"])
    P.wait("gpsimd", "pe", z_last_pe)
    v_bdt = P.emit("sync", lambda e, s: e.dma_start(
        out=bdT_s, in_=din["bdTp"].bitcast(F32R)), inc="sbd", inc_n=16)
    evac_vv = [0] * TSUB
    ch_pe = {0: 0, 1: 0}
    conv_ss = [0, 0]
    wo_pe = [0, 0]
    yfT_pe = [0, 0]
    def x4_b(i):
        return arf(7168 + i * 512, 512)

    def ot4_b(i):
        return ar(10240 + i * 512, 512)

    id_r = idr_s[:]
    evac_ss = [0] * TSUB
    v_x = [0] * TSUB
    v_wo = [0] * 4
    xid4 = [0, 0, 0, 0]
    ot_st4 = [0, 0, 0, 0]
    for ds in range(DS):
        xprev = list(xid4)
        for ch in range(NCH):
            par = ch % 2
            P.wait("sync", "ss", conv_ss[par])
            sup = "su0" if par == 0 else "su1"
            svp = "sv0" if par == 0 else "sv1"
            v_wd = P.emit("sync", lambda e, s, ds=ds, ch=ch, par=par: e.dma_start(
                out=wd_arr(par), in_=wdt_v[ch, ds]), inc=sup, inc_n=16)
            if ds == 0 and ch == 0:
                P.wait("gpsimd", "sh0", hb_dma[0])
                P.wait("gpsimd", "sh1", hb_dma[1])
            P.wait("gpsimd", "pe", ch_pe[par])
            v_hb = P.emit("gpsimd", lambda e, s, ch=ch, par=par: e.dma_start(
                out=hb_ch(par).rearrange("p (k c) -> p k c", k=4),
                in_=hb_vv[ch]), inc=svp, inc_n=16)
            P.wait("scalar", sup, v_wd)
            P.wait("scalar", "pe", ch_pe[par])
            conv_ss[par] = P.emit("scalar", lambda e, s, par=par: e.activation(
                wd_bf(par), wd_arr(par), AF.Copy), inc="ss")
            if ch <= 3:
                for tsub in (2 * ch, 2 * ch + 1):
                    sl = tsub % 4
                    gate = xprev[sl] if ch <= 1 else xid4[sl]
                    if gate:
                        P.wait("scalar", "pe", gate)
                    v_x[tsub] = P.emit("scalar", lambda e, s, tsub=tsub, ds=ds, sl=sl: e.dma_start(
                        out=x4_b(sl), in_=x_v[tsub][:, 512 * ds:512 * (ds + 1)]),
                        inc="sz0", inc_n=16)
            if ch in (1, 3):
                woc = (ch - 1) // 2
                parw = woc % 2
                if ds > 0:
                    P.wait("scalar", "pe", wo_pe[parw])
                v_wo[woc] = P.emit("scalar", lambda e, s, ds=ds, woc=woc, parw=parw: e.dma_start(
                    out=wo_ch(parw), in_=wo_v[woc, ds]),
                    inc=("sm0" if parw == 0 else "sm1"), inc_n=16)
            P.wait("tensor", "ss", conv_ss[par])
            P.wait("tensor", svp, v_hb)
            for tsub in range(TSUB):
                if ds > 0 and ch == 0:
                    P.wait("tensor", "ss", evac_ss[tsub])
                for k in range(4):
                    lh = hb_ch(par).rearrange("p (k t) -> p k t", k=4)[:, k, 128 * tsub:128 * (tsub + 1)]
                    rh = wd_bf(par).rearrange("p (k c) -> p k c", k=4)[:, k, :]
                    first = (ch == 0 and k == 0)
                    is_last = (tsub == TSUB - 1 and k == 3)
                    P.emit("tensor", lambda e, s, tsub=tsub, lh=lh, rh=rh, first=first: e.matmul(
                        ps_out[tsub], lh, rh, start=first, stop=False),
                        inc="pe" if is_last else None)
            if 1 <= ch <= 4:
                for tsub in (2 * (ch - 1), 2 * (ch - 1) + 1):
                    sl = tsub % 4
                    P.wait("tensor", "sz0", v_x[tsub])
                    P.emit("tensor", lambda e, s, tsub=tsub, sl=sl: e.matmul(
                        ps_out[tsub], id_r, x4_b(sl), start=False, stop=False), inc="pe")
                    xid4[sl] = P.cnt["pe"]
            ch_pe[par] = P.cnt["pe"]
        for woc in range(4):
            par = woc % 2
            smp = "sm0" if par == 0 else "sm1"
            if woc >= 2:
                P.wait("scalar", "pe", wo_pe[par])
                v_wo[woc] = P.emit("scalar", lambda e, s, ds=ds, woc=woc, par=par: e.dma_start(
                    out=wo_ch(par), in_=wo_v[woc, ds]), inc=smp, inc_n=16)
            if ds == 0 and woc == 0:
                P.wait("gpsimd", "so", yf_sq[0])
            for tsub in range(TSUB):
                parT = tsub % 2
                if ds > 0 or woc > 0 or tsub >= 2:
                    P.wait("gpsimd", "pe", yfT_pe[parT])
                syp = "sy0" if parT == 0 else "sy1"
                v_yf = P.emit("gpsimd", lambda e, s, tsub=tsub, parT=parT, woc=woc: e.dma_start(
                    out=yfT_b(parT).rearrange("p (k c) -> p k c", k=4),
                    in_=yf_v[:, 4 * woc:4 * (woc + 1), 128 * tsub:128 * (tsub + 1)]), inc=syp, inc_n=16)
                P.wait("tensor", smp, v_wo[woc])
                P.wait("tensor", syp, v_yf)
                for k in range(4):
                    lh = yfT_b(parT).rearrange("p (k c) -> p k c", k=4)[:, k, :]
                    rh = wo_ch(par).rearrange("p (k c) -> p k c", k=4)[:, k, :]
                    P.emit("tensor", lambda e, s, tsub=tsub, lh=lh, rh=rh: e.matmul(
                        ps_out[tsub], lh, rh, start=False, stop=False),
                        inc="pe" if k == 3 else None)
                yfT_pe[parT] = P.cnt["pe"]
            wo_pe[par] = P.cnt["pe"]
        for tsub in range(TSUB):
            P.wait("tensor", "ss", z_ss)
            if ds == 0 and tsub == 0:
                P.wait("tensor", "sbd", v_bdt)
            P.emit("tensor", lambda e, s, tsub=tsub, ds=ds: e.matmul(
                ps_out[tsub], z_sb[:, tsub * 128: tsub * 128 + 128],
                bdT_s[:, ds * 512: ds * 512 + 512],
                start=False, stop=True), inc="pe")
            pv = P.cnt["pe"]
            sl = tsub % 4
            P.wait("scalar", "pe", pv)
            if ot_st4[sl]:
                P.wait("scalar", "st0", ot_st4[sl])
            evac_ss[tsub] = P.emit("scalar", lambda e, s, tsub=tsub, sl=sl: e.activation(
                ot4_b(sl), ps_out[tsub], AF.Copy), inc="ss")
            ot_st4[sl] = P.emit("scalar", lambda e, s, tsub=tsub, ds=ds, sl=sl: e.dma_start(
                out=out_v[tsub][:, 512 * ds:512 * (ds + 1)], in_=ot4_b(sl)),
                inc="st0", inc_n=16)

    P.wait("sync", "st0", P.cnt["st0"])
    P.wait("gpsimd", "gq", P.cnt["gq"])

    with ExitStack() as st_ctx:
        sems = {}
        for name in P.cnt:
            sems[name] = st_ctx.enter_context(nc.semaphore(name))
        block = st_ctx.enter_context(nc.Block())

        def runner(eng_name):
            def run(engine):
                for fn in P.items[eng_name]:
                    fn(engine, sems)
            return run
        block.sync(runner("sync"))
        block.tensor(runner("tensor"))
        block.vector(runner("vector"))
        block.scalar(runner("scalar"))
        block.gpsimd(runner("gpsimd"))

    stack.close()
    return nc


def kernel(**inputs):
    per_core = _prep(inputs)
    nc = build_program()
    res = run_bass_kernel_spmd(nc, per_core, core_ids=list(range(NCORES)))
    out = np.empty((2, LSEQ, D), np.float32)
    for c in range(NCORES):
        b, tp = divmod(c, 4)
        out[b, tp * TOK:(tp + 1) * TOK, :] = res.results[c]["out_sh"]
    return out
